# version 07
# baseline (speedup 1.0000x reference)
import contextlib
import math
import numpy as np
import concourse.bass as bass
import concourse.mybir as mybir
from concourse.bass_utils import run_bass_kernel_spmd

F32 = mybir.dt.float32
BF16 = mybir.dt.bfloat16
AF = mybir.ActivationFunctionType
ALU = mybir.AluOpType
AX = mybir.AxisListType.X

B_TOT, S, D = 32, 2048, 1024
NCORE = 8
BC = B_TOT // NCORE
CC = D // 128
NT = S // 512
H, HD = 16, 64
SG = S + 8

ENGS = ("sync", "scalar", "vector", "tensor", "gpsimd")


class Prog:

    def __init__(self, nc):
        self.nc = nc
        self.es = contextlib.ExitStack()
        self.thunks = {e: [] for e in ENGS}
        self.semh = {}
        self.cnt = {}
        self.water = {e: {} for e in ENGS}
        for e in ENGS:
            self.mksem("p_" + e)
        self.mksem("d_sy")
        self.mksem("d_gp")
        self.mksem("f_sy")
        self.mksem("f_gp")
        self.mksem("bar")
        self.fences = {"sync": [], "gpsimd": []}

    def init_consts(self, buf):
        self.constbuf = buf
        self.cmap = {}

    def constcol(self, val, rows=128):
        if val not in self.cmap:
            i = len(self.cmap)
            tok = self.op("vector", lambda e, i=i, val=val: e.memset(
                self.constbuf[:, i:i + 1], val))
            self.cmap[val] = (i, tok)
        i, tok = self.cmap[val]
        return self.constbuf[0:rows, i:i + 1], tok

    def mksem(self, name):
        self.semh[name] = self.es.enter_context(self.nc.semaphore(name))
        self.cnt[name] = 0
        return name

    def _fence(self, q):
        dsem = "d_sy" if q == "sync" else "d_gp"
        fsem = "f_sy" if q == "sync" else "f_gp"
        total = self.cnt[dsem]
        self._wait_direct(q, dsem, total)
        self.cnt[fsem] += 1
        fh = self.semh[fsem]
        self.thunks[q].append(lambda e, fh=fh: e.nop().then_inc(fh, 1))
        self.fences[q].append((total, self.cnt[fsem]))
        return (fsem, self.cnt[fsem])

    def _wait_direct(self, eng, sem, val):
        if val <= 0:
            return
        if self.water[eng].get(sem, 0) >= val:
            return
        self.water[eng][sem] = val
        h = self.semh[sem]
        self.thunks[eng].append(lambda e, h=h, v=val: e.wait_ge(h, v))

    def _wait(self, eng, sem, val):
        if val <= 0:
            return
        if sem in ("d_sy", "d_gp"):
            q = "sync" if sem == "d_sy" else "gpsimd"
            if eng == q:
                self._wait_direct(eng, sem, self.cnt[sem])
                return
            fsem = "f_sy" if sem == "d_sy" else "f_gp"
            for (tot, fc) in self.fences[q]:
                if tot >= val:
                    self._wait_direct(eng, fsem, fc)
                    return
            ftok = self._fence(q)
            self._wait_direct(eng, ftok[0], ftok[1])
            return
        self._wait_direct(eng, sem, val)

    def op(self, eng, fn, deps=(), inc=True):
        for (s, v) in deps:
            self._wait(eng, s, v)
        sem = "p_" + eng
        if not inc:
            self.thunks[eng].append(lambda e, fn=fn: fn(e))
            return (sem, self.cnt[sem])
        self.cnt[sem] += 1
        tok = (sem, self.cnt[sem])
        h = self.semh[sem]
        self.thunks[eng].append(lambda e, fn=fn, h=h: fn(e).then_inc(h, 1))
        return tok

    def raw(self, eng, fn, deps=()):
        for (s, v) in deps:
            self._wait(eng, s, v)
        self.thunks[eng].append(fn)
        return None

    def dma(self, eng, out, in_, deps=(), **kw):
        for (s, v) in deps:
            self._wait(eng, s, v)
        sem = "d_sy" if eng == "sync" else "d_gp"
        assert eng in ("sync", "gpsimd")
        self.cnt[sem] += 16
        tok = (sem, self.cnt[sem])
        h = self.semh[sem]
        self.thunks[eng].append(
            lambda e, out=out, in_=in_, h=h, kw=kw: e.dma_start(out, in_, **kw).then_inc(h, 16))
        return tok

    def barrier(self):
        bh = self.semh["bar"]
        target = self.cnt["bar"] + len(ENGS)
        for e in ENGS:
            self._wait(e, "p_" + e, self.cnt["p_" + e])
            if e == "sync":
                self._wait(e, "d_sy", self.cnt["d_sy"])
            if e == "gpsimd":
                self._wait(e, "d_gp", self.cnt["d_gp"])
            self.thunks[e].append(lambda en, bh=bh: en.nop().then_inc(bh, 1))
        self.cnt["bar"] = target
        for e in ENGS:
            self._wait(e, "bar", target)

    def finalize(self):
        nc = self.nc
        with nc.Block() as block:
            @block.sync
            def _(e):
                for t in self.thunks["sync"]:
                    t(e)

            @block.scalar
            def _(e):
                for t in self.thunks["scalar"]:
                    t(e)

            @block.vector
            def _(e):
                for t in self.thunks["vector"]:
                    t(e)

            @block.tensor
            def _(e):
                for t in self.thunks["tensor"]:
                    t(e)

            @block.gpsimd
            def _(e):
                for t in self.thunks["gpsimd"]:
                    t(e)
        self.es.close()


_NSEQ = [0]


def _nm(name):
    _NSEQ[0] += 1
    return f"{name}_{_NSEQ[0]}"


def ts(i, n):
    return slice(i * n, (i + 1) * n)



def build_program(dbg_external=(), upto=99):
    nc = bass.Bass("TRN2", target_bir_lowering=False)
    g = {}

    def din(name, shape, dt=F32):
        g[name] = nc.dram_tensor(name, shape, dt, kind="ExternalInput")
        return g[name]

    def dscr(name, shape, dt=F32):
        kind = "ExternalOutput" if name in dbg_external else "Internal"
        g[name] = nc.dram_tensor(name, shape, dt, kind=kind)
        return g[name]

    din("x_fm", [BC, CC, 128, S])
    din("l0_wkT", [CC, 128, D]); din("l0_wvT", [CC, 128, D])
    din("l0_wrT", [CC, 128, D]); din("l0_woT", [CC, 128, D])
    din("l0_par", [128, 40])
    din("l0_n1w", [128, CC]); din("l0_n2w", [128, CC])
    din("qkvT", [CC, 128, 3072])
    din("l0_qkvb", [128, 24])
    din("l0_outwT", [CC, 128, D]); din("l0_outb", [128, CC])
    din("gateT", [CC, 128, 8])
    din("w1T", [8, CC, 128, 2048]); din("w2T", [8, CC, 128, 2048])
    din("w3T", [8, 16, 128, D])
    din("l0_b1", [8, 128, 16]); din("l0_b2", [8, 128, 16]); din("l0_b3", [8, 128, CC])
    din("rope_cos", [128, S]); din("rope_sin", [128, S])
    din("ropeRT", [128, 128])
    din("onehot8", [8, 8 * 128])
    din("l1_wq", [CC, 128, D]); din("l1_wk", [CC, 128, D])
    din("l1_wv", [CC, 128, D]); din("l1_wo", [CC, 128, D])
    din("l1_bo", [128, CC])
    din("alibias", [128, 1024])
    din("causalm", [128, 896])
    din("l1_W1", [CC, 128, 4096]); din("l1_b1f", [128, 32])
    din("l1_W2f", [32, 128, D]); din("l1_b2f", [128, CC])
    din("l1_g", [128, CC]); din("l1_be", [128, CC])
    din("l1_wr", [CC, 128, 8]); din("l1_br", [8, 1])

    dscr("xbf", [BC, CC, 128, SG], BF16)
    dscr("bufK", [BC, CC, 128, S])
    dscr("bufV", [BC, CC, 128, S])
    dscr("bufSR", [BC, CC, 128, S])
    dscr("bufA", [BC, CC, 128, S])
    dscr("bufX1", [BC, CC, 128, S])
    dscr("bufXN", [BC, CC, 128, S], BF16)
    dscr("bufX2", [BC, CC, 128, S])
    dscr("bufX3", [BC, CC, 128, S])
    dscr("bufX4", [BC, CC, 128, S])
    dscr("qkvTB", [CC, 128, 3072], BF16)
    dscr("outwTB", [CC, 128, D], BF16)
    dscr("gateTB", [CC, 128, 8], BF16)
    dscr("w1TB", [8, CC, 128, 2048], BF16)
    dscr("w2TB", [8, CC, 128, 2048], BF16)
    dscr("w3TB", [8, 16, 128, D], BF16)
    dscr("bufX6", [BC, CC, 128, S])
    dscr("bufY0", [BC, CC, 128, S])
    dscr("bufP0", [BC, 1, S])
    dscr("bufSMALL", [128, 48])
    out_fm = nc.dram_tensor("out_fm", [BC, CC, 128, S], F32, kind="ExternalOutput")
    g["out_fm"] = out_fm

    p = Prog(nc)
    sb = lambda name, shape, dt=F32: p.es.enter_context(nc.sbuf_tensor(name, shape, dt))

    p.init_consts(sb("constbuf", [128, 16]))
    par = sb("par", [128, 40])
    dpar = sb("dpar", [128, 64])
    t_par = p.dma("sync", par[:], g["l0_par"].ap())

    t0 = p.op("vector", lambda e: e.tensor_copy(dpar[:, 0:8], par[:, 16:24]), deps=[t_par])
    t1 = p.op("scalar", lambda e: e.activation(dpar[:, 8:16], par[:, 16:24], AF.Identity, scale=-1.0, bias=1.0), deps=[t_par])
    t2 = p.op("vector", lambda e: e.tensor_copy(dpar[:, 16:24], par[:, 24:32]), deps=[t_par])
    t3 = p.op("scalar", lambda e: e.activation(dpar[:, 24:32], par[:, 24:32], AF.Identity, scale=-1.0, bias=1.0), deps=[t_par])
    t4 = p.op("vector", lambda e: e.tensor_copy(dpar[:, 32:40], par[:, 32:40]), deps=[t_par])
    t5 = p.op("scalar", lambda e: e.activation(dpar[:, 40:48], par[:, 32:40], AF.Identity, scale=-1.0, bias=1.0), deps=[t_par])
    t6a = p.op("scalar", lambda e: e.activation(dpar[:, 48:56], par[:, 0:8], AF.Exp), deps=[t_par])
    t6 = p.op("scalar", lambda e: e.activation(dpar[:, 48:56], dpar[:, 48:56], AF.Exp, scale=-1.0), deps=[t6a])
    t7 = p.op("scalar", lambda e: e.activation(dpar[:, 56:64], par[:, 8:16], AF.Exp), deps=[t_par])
    dpar_tok = [t0, t1, t2, t3, t4, t5, t6, t7]

    def dcol(f, cc):
        return dpar[:, f * 8 + cc: f * 8 + cc + 1]

    zg = sb("zg", [128, 8], BF16)
    tz = p.op("vector", lambda e: e.memset(zg[:], 0.0))
    ph0_toks = []
    for b in range(BC):
        for cc in range(CC):
            ph0_toks.append(p.dma("gpsimd", g["xbf"].ap()[b, cc, :, 8:SG],
                                  g["x_fm"].ap()[b, cc, :, :]))
            ph0_toks.append(p.dma("gpsimd", g["xbf"].ap()[b, cc, :, 0:8], zg[:], deps=[tz]))
    p.barrier()
    for cc in range(CC):
        p.dma("gpsimd", g["qkvTB"].ap()[cc], g["qkvT"].ap()[cc])
        p.dma("gpsimd", g["outwTB"].ap()[cc], g["l0_outwT"].ap()[cc])
        p.dma("gpsimd", g["gateTB"].ap()[cc], g["gateT"].ap()[cc])
    for e in range(8):
        for cc in range(CC):
            p.dma("gpsimd", g["w1TB"].ap()[e, cc], g["w1T"].ap()[e, cc])
            p.dma("gpsimd", g["w2TB"].ap()[e, cc], g["w2T"].ap()[e, cc])
        for hc in range(16):
            p.dma("gpsimd", g["w3TB"].ap()[e, hc], g["w3T"].ap()[e, hc])

    with contextlib.ExitStack() as ph:
        psb = lambda name, shape, dt=F32: ph.enter_context(nc.sbuf_tensor(_nm(name), shape, dt))
        psum = lambda name, shape, dt=F32: ph.enter_context(nc.psum_tensor(_nm(name), shape, dt))
        wsc = {}
        wsc_toks = []
        wtmp = psb("wtmp", [128, D])
        tok_prev = dpar_tok
        for pi, (wname, fA, fB, nA, nB) in enumerate([
                ("l0_wkT", 0, 1, "wk1", "wk2"),
                ("l0_wvT", 2, 3, "wv1", "wv2"),
                ("l0_wrT", 4, 5, "wr1", "wr2")]):
            wsc[nA] = psb(nA, [128, CC * D], BF16)
            wsc[nB] = psb(nB, [128, CC * D], BF16)
            for cc in range(CC):
                tl = p.dma("sync", wtmp[:], g[wname].ap()[cc], deps=tok_prev)
                ta = p.op("vector", lambda e, nA=nA, cc=cc, fA=fA: e.tensor_scalar(
                    wsc[nA][:, ts(cc, D)], wtmp[:], dcol(fA, cc), None, ALU.mult), deps=[tl] + dpar_tok)
                tb = p.op("vector", lambda e, nB=nB, cc=cc, fB=fB: e.tensor_scalar(
                    wsc[nB][:, ts(cc, D)], wtmp[:], dcol(fB, cc), None, ALU.mult), deps=[tl] + dpar_tok)
                tok_prev = [ta, tb]
                wsc_toks.extend([ta, tb])

        xw = [psb(f"xw{i}", [128, CC * 513], BF16) for i in range(2)]
        xw_tok = [(None, 0), (None, 0)]
        pk = [psum(f"pk{i}", [128, 512]) for i in range(4)]
        pk_tok = [(None, 0)] * 4
        ev = [psb(f"ev{i}", [128, 512]) for i in range(4)]
        ev_tok = [(None, 0)] * 4
        slot = 0
        pslot = 0
        dst = {"k": g["bufK"], "v": g["bufV"], "r": g["bufSR"]}
        wpair = {"k": ("wk1", "wk2"), "v": ("wv1", "wv2"), "r": ("wr1", "wr2")}
        for b in range(BC):
            for tt in range(NT):
                xs = xw[slot]
                loads = []
                for cc in range(CC):
                    dep = [xw_tok[slot]] if xw_tok[slot][0] else []
                    loads.append(p.dma("sync", xs[:, cc * 513:(cc + 1) * 513],
                                       g["xbf"].ap()[b, cc, :, 8 + tt * 512 - 1: 8 + tt * 512 + 512],
                                       deps=dep))
                consumed = []
                for proj in ("k", "v", "r"):
                    w1n, w2n = wpair[proj]
                    for occ in range(CC):
                        pp = pk[pslot]
                        mm_tok = None
                        for icc in range(CC):
                            lhs1 = wsc[w1n][:, icc * D + occ * 128: icc * D + occ * 128 + 128]
                            lhs2 = wsc[w2n][:, icc * D + occ * 128: icc * D + occ * 128 + 128]
                            rhs1 = xs[:, icc * 513 + 1: icc * 513 + 513]
                            rhs2 = xs[:, icc * 513 + 0: icc * 513 + 512]
                            deps = loads + wsc_toks + ([pk_tok[pslot]] if pk_tok[pslot][0] else [])
                            mm_tok = p.op("tensor", lambda e, pp=pp, lhs1=lhs1, rhs1=rhs1, icc=icc: e.matmul(
                                pp[:], lhs1, rhs1, start=(icc == 0), stop=False), deps=deps, inc=False)
                            mm_tok = p.op("tensor", lambda e, pp=pp, lhs2=lhs2, rhs2=rhs2, icc=icc: e.matmul(
                                pp[:], lhs2, rhs2, start=False, stop=(icc == CC - 1)), deps=[], inc=(icc == CC - 1))
                        consumed.append(mm_tok)
                        ebuf = ev[pslot]
                        edep = [mm_tok] + ([ev_tok[pslot]] if ev_tok[pslot][0] else [])
                        if proj == "r":
                            et = p.op("scalar", lambda e, ebuf=ebuf, pp=pp: e.activation(
                                ebuf[:], pp[:], AF.Sigmoid), deps=edep)
                        else:
                            et = p.op("scalar", lambda e, ebuf=ebuf, pp=pp: e.copy(ebuf[:], pp[:]), deps=edep)
                        pk_tok[pslot] = et
                        ev_tok[pslot] = p.dma("sync", dst[proj].ap()[b, occ, :, tt * 512:(tt + 1) * 512],
                                              ebuf[:], deps=[et])
                        pslot = (pslot + 1) % 4
                xw_tok[slot] = consumed[-1]
                slot = 1 - slot
    p.barrier()

    with contextlib.ExitStack() as ph:
        psb = lambda name, shape, dt=F32: ph.enter_context(nc.sbuf_tensor(_nm(name), shape, dt))
        psum = lambda name, shape, dt=F32: ph.enter_context(nc.psum_tensor(_nm(name), shape, dt))
        ones = psb("ones2", [128, 2048])
        t_ones = p.op("vector", lambda e: e.memset(ones[:], 1.0))
        ewsl = psb("ewsl", [128, 2048])
        slK = psb("slK", [128, 2048]); slV = psb("slV", [128, 2048])
        slEK = psb("slEK", [128, 2048]); slEKV = psb("slEKV", [128, 2048])
        slP = psb("slP", [128, 2049]); slQ = psb("slQ", [128, 2049])
        slN = psb("slN", [128, 2048]); slD = psb("slD", [128, 2048])
        slR = psb("slR", [128, 2048]); slSR = psb("slSR", [128, 2048])
        rwkv = psb("rwkv", [128, CC * 2048], BF16)
        woTb = psb("woTb", [128, CC * D], BF16)
        wtm = psb("wtm", [128, D])
        tg1 = p.op("vector", lambda e: e.memset(slP[:, 0:1], 0.0))
        tg2 = p.op("vector", lambda e: e.memset(slQ[:, 0:1], 0.0))
        wo_tok = []
        tprev = (None, 0)
        for cc in range(CC):
            tl = p.dma("sync", wtm[:], g["l0_woT"].ap()[cc], deps=[tprev] if tprev[0] else [])
            tc = p.op("scalar", lambda e, cc=cc: e.copy(woTb[:, ts(cc, D)], wtm[:]), deps=[tl])
            tprev = tc
            wo_tok.append(tc)

        lt = {k: (None, 0) for k in ["K", "V", "EK", "EKV", "P", "Q", "N", "D", "R", "SR", "ew"]}
        rw_tok = {}
        pmm = [psum(f"po{i}", [128, 512]) for i in range(2)]
        pmm_tok = [(None, 0)] * 2
        evo = [psb(f"evo{i}", [128, 512]) for i in range(2)]
        evo_tok = [(None, 0)] * 2
        mslot = 0
        for b in range(BC):
            rwkv_consumed = rw_tok.get(b - 1, [])
            for cc in range(CC):
                dd = lambda k: [lt[k]] if lt[k][0] else []
                tK = p.dma("sync", slK[:], g["bufK"].ap()[b, cc], deps=dd("K"))
                tV = p.dma("sync", slV[:], g["bufV"].ap()[b, cc], deps=dd("V"))
                tR = p.dma("sync", slSR[:], g["bufSR"].ap()[b, cc], deps=dd("SR"))
                tew = p.op("scalar", lambda e, cc=cc: e.activation(
                    ewsl[:], ones[:, 0:2048], AF.Identity, scale=dcol(6, cc)),
                    deps=dd("ew") + dpar_tok + [t_ones])
                tEK = p.op("scalar", lambda e: e.activation(slEK[:], slK[:], AF.Exp),
                           deps=[tK] + dd("EK"))
                lt["K"] = tEK
                tEKV = p.op("vector", lambda e: e.tensor_mul(slEKV[:], slEK[:], slV[:]),
                            deps=[tEK, tV] + dd("EKV"))
                lt["V"] = tEKV
                tQ = p.op("vector", lambda e: e.tensor_tensor_scan(
                    slQ[:, 1:2049], ewsl[:], slEK[:], 0.0, ALU.mult, ALU.add),
                    deps=[tew, tEK, tg2] + dd("Q"))
                tP = p.op("vector", lambda e: e.tensor_tensor_scan(
                    slP[:, 1:2049], ewsl[:], slEKV[:], 0.0, ALU.mult, ALU.add),
                    deps=[tew, tEKV, tg1] + dd("P"))
                lt["ew"] = tP
                tD = p.op("vector", lambda e, cc=cc: e.scalar_tensor_tensor(
                    slD[:], slEK[:], dcol(7, cc), slQ[:, 0:2048], ALU.mult, ALU.add),
                    deps=[tQ] + dd("D"))
                lt["EK"] = tD
                lt["Q"] = tD
                tN = p.op("vector", lambda e, cc=cc: e.scalar_tensor_tensor(
                    slN[:], slEKV[:], dcol(7, cc), slP[:, 0:2048], ALU.mult, ALU.add),
                    deps=[tP] + dd("N"))
                lt["EKV"] = tN
                lt["P"] = tN
                tL = p.op("scalar", lambda e: e.activation(slR[:], slD[:], AF.Ln),
                          deps=[tD] + dd("R"))
                tRc = p.op("scalar", lambda e: e.activation(slR[:], slR[:], AF.Exp, scale=-1.0),
                           deps=[tL])
                lt["D"] = tRc
                tY = p.op("vector", lambda e: e.tensor_mul(slN[:], slN[:], slR[:]),
                          deps=[tRc, tN])
                lt["R"] = tY
                lt["N"] = tY
                trw = p.op("vector", lambda e, cc=cc: e.tensor_mul(
                    rwkv[:, ts(cc, 2048)], slN[:], slSR[:]),
                    deps=[tY, tR] + rwkv_consumed)
                lt["SR"] = trw
                lt["N"] = trw
                rw_tok.setdefault(b, []).append(trw)
            consumed = []
            for tt in range(NT):
                for occ in range(CC):
                    pp = pmm[mslot]
                    mm = None
                    for icc in range(CC):
                        deps = [rw_tok[b][icc]] + wo_tok + ([pmm_tok[mslot]] if pmm_tok[mslot][0] else [])
                        mm = p.op("tensor", lambda e, pp=pp, icc=icc, occ=occ, tt=tt: e.matmul(
                            pp[:], woTb[:, icc * D + occ * 128: icc * D + occ * 128 + 128],
                            rwkv[:, icc * 2048 + tt * 512: icc * 2048 + tt * 512 + 512],
                            start=(icc == 0), stop=(icc == CC - 1)), deps=deps, inc=(icc == CC - 1))
                    consumed.append(mm)
                    eb = evo[mslot]
                    edep = [mm] + ([evo_tok[mslot]] if evo_tok[mslot][0] else [])
                    et = p.op("scalar", lambda e, eb=eb, pp=pp: e.copy(eb[:], pp[:]), deps=edep)
                    pmm_tok[mslot] = et
                    evo_tok[mslot] = p.dma("sync", g["bufA"].ap()[b, occ, :, tt * 512:(tt + 1) * 512],
                                           eb[:], deps=[et])
                    mslot = 1 - mslot
            rw_tok[b] = consumed
    p.barrier()

    emit_resid_ln(p, nc, g, src1="x_fm", src2="bufA", dst="bufX1", dst2=None)
    p.barrier()
    cur = "bufX1"

    if upto >= 4:
        emit_rms(p, nc, g, src="bufX1", dst="bufXN", wname="l0_n1w", eps=1e-6)
        p.barrier()
        emit_l0attn(p, nc, g)
        p.barrier()
        cur = "bufX2"
    if upto >= 5:
        emit_rms(p, nc, g, src="bufX2", dst="bufXN", wname="l0_n2w", eps=1e-6)
        p.barrier()
        emit_moe(p, nc, g)
        p.barrier()
        emit_resid_ln(p, nc, g, src1="bufX1", src2="bufX3", dst="bufX4")
        p.barrier()
        cur = "bufX4"
    if upto >= 7:
        emit_l1attn(p, nc, g)
        p.barrier()
        emit_resid_ln(p, nc, g, src1="bufX4", src2="bufA", dst="bufX6")
        p.barrier()
        cur = "bufX6"
    if upto >= 9:
        emit_l1router(p, nc, g)
        p.barrier()
        emit_thresholds(p, nc, g)
        p.barrier()
        emit_l1ffn(p, nc, g, path=0)
        p.barrier()
        emit_l1ffn(p, nc, g, path=1)
        p.barrier()
        emit_resid_ln(p, nc, g, src1="bufX6", src2="bufA", dst="out_fm")
        p.barrier()
        cur = None

    if cur is None:
        p.finalize()
        return nc
    with contextlib.ExitStack() as ph:
        psb = lambda name, shape, dt=F32: ph.enter_context(nc.sbuf_tensor(_nm(name), shape, dt))
        cp = psb("cpb", [128, 2048])
        tprev = (None, 0)
        for b in range(BC):
            for cc in range(CC):
                tl = p.dma("sync", cp[:], g[cur].ap()[b, cc], deps=[tprev] if tprev[0] else [])
                tprev = p.dma("sync", g["out_fm"].ap()[b, cc], cp[:], deps=[tl])
    p.barrier()
    p.finalize()
    return nc


def emit_resid_ln(p, nc, g, src1, src2, dst, dst2=None, eps=1e-5):
    with contextlib.ExitStack() as ph:
        psb = lambda name, shape, dt=F32: ph.enter_context(nc.sbuf_tensor(_nm(name), shape, dt))
        psum = lambda name, shape, dt=F32: ph.enter_context(nc.psum_tensor(_nm(name), shape, dt))
        xs = [psb(f"ln_x{i}", [128, CC * 512]) for i in range(2)]
        ys = [psb(f"ln_y{i}", [128, CC * 512]) for i in range(2)]
        sqs = [psb(f"ln_sq{i}", [128, 512]) for i in range(2)]
        stM = psb("ln_stM", [1, 512])
        stV = psb("ln_stV", [1, 512])
        stT = psb("ln_stT", [1, 512])
        onec = psb("ln_one", [128, 1])
        one1 = psb("ln_one1", [1, 128])
        t_on = p.op("vector", lambda e: e.memset(onec[:], 1.0))
        t_on1 = p.op("vector", lambda e: e.memset(one1[:], 1.0))
        pS = psum("ln_pS", [1, 512])
        pQ = psum("ln_pQ", [1, 512])
        pB = psum("ln_pB", [128, 512])
        pM = psum("ln_pM", [128, 512])
        xs_tok = [(None, 0)] * 2
        ys_tok = [(None, 0)] * 2
        sq_tok = [(None, 0)] * 2
        stat_tok = (None, 0)
        pS_tok = (None, 0)
        pB_tok = (None, 0)
        pM_tok = (None, 0)
        bB_prev = (None, 0)
        bM_prev = (None, 0)
        slot = 0
        for b in range(BC):
            for tt in range(NT):
                x = xs[slot]
                y = ys[slot]
                loads = []
                adds = []
                for cc in range(CC):
                    dep = [xs_tok[slot]] if xs_tok[slot][0] else []
                    l1 = p.dma("sync", x[:, ts(cc, 512)],
                               g[src1].ap()[b, cc, :, tt * 512:(tt + 1) * 512], deps=dep)
                    if src2 is not None:
                        dep2 = [ys_tok[slot]] if ys_tok[slot][0] else []
                        l2 = p.dma("sync", y[:, ts(cc, 512)],
                                   g[src2].ap()[b, cc, :, tt * 512:(tt + 1) * 512], deps=dep2)
                        ta = p.op("vector", lambda e, cc=cc, x=x, y=y: e.tensor_add(
                            x[:, ts(cc, 512)], x[:, ts(cc, 512)], y[:, ts(cc, 512)]),
                            deps=[l1, l2])
                        if dst2 is not None:
                            adds.append(p.dma("sync", g[dst2].ap()[b, cc, :, tt * 512:(tt + 1) * 512],
                                        x[:, ts(cc, 512)], deps=[ta]))
                    else:
                        ta = l1
                    loads.append(ta)
                mmS = None
                for cc in range(CC):
                    dep = [loads[cc], t_on] + ([pS_tok] if pS_tok[0] else [])
                    mmS = p.op("tensor", lambda e, cc=cc, x=x: e.matmul(
                        pS[:], onec[:], x[:, ts(cc, 512)],
                        start=(cc == 0), stop=(cc == CC - 1)), deps=dep, inc=(cc == CC - 1))
                mmQ = None
                for cc in range(CC):
                    sq = sqs[cc % 2]
                    dep = [loads[cc]] + ([sq_tok[cc % 2]] if sq_tok[cc % 2][0] else [])
                    tq = p.op("scalar", lambda e, cc=cc, x=x, sq=sq: e.activation(
                        sq[:], x[:, ts(cc, 512)], AF.Square), deps=dep)
                    mmQ = p.op("tensor", lambda e, cc=cc, sq=sq: e.matmul(
                        pQ[:], onec[:], sq[:],
                        start=(cc == 0), stop=(cc == CC - 1)), deps=[tq, mmS])
                    sq_tok[cc % 2] = mmQ
                dep0 = [mmS, mmQ] + ([stat_tok] if stat_tok[0] else [])
                s0 = p.op("vector", lambda e: e.tensor_scalar(
                    stM[:], pS[:], 1.0 / D, None, ALU.mult), deps=dep0)
                s1 = p.op("vector", lambda e: e.tensor_scalar(
                    stV[:], pQ[:], 1.0 / D, None, ALU.mult),
                    deps=[mmQ, s0] + ([bB_prev] if bB_prev[0] else []))
                pS_tok = s1
                s2 = p.op("vector", lambda e: e.tensor_mul(stT[:], stM[:], stM[:]),
                          deps=[s0])
                s3 = p.op("vector", lambda e: e.tensor_sub(stV[:], stV[:], stT[:]),
                          deps=[s1, s2])
                epsc, epst = p.constcol(eps, rows=1)
                s4 = p.op("scalar", lambda e, epsc=epsc: e.activation(stV[:], stV[:], AF.Ln, bias=epsc),
                          deps=[s3, epst])
                s5 = p.op("scalar", lambda e: e.activation(stV[:], stV[:], AF.Exp, scale=-0.5),
                          deps=[s4])
                s6 = p.op("vector", lambda e: e.tensor_mul(stT[:], stM[:], stV[:]),
                          deps=[s5, s2] + ([bM_prev] if bM_prev[0] else []))
                stat_tok = s6
                depB = [s5, t_on1] + ([pB_tok] if pB_tok[0] else [])
                bB = p.op("tensor", lambda e: e.matmul(pB[:], one1[:], stV[:],
                                                       start=True, stop=True), deps=depB)
                depM = [s6, t_on1, bB] + ([pM_tok] if pM_tok[0] else [])
                bM = p.op("tensor", lambda e: e.matmul(pM[:], one1[:], stT[:],
                                                       start=True, stop=True), deps=depM)
                bB_prev, bM_prev = bB, bM
                outs = []
                for cc in range(CC):
                    dep = [bB, loads[cc]] + ([ys_tok[slot]] if ys_tok[slot][0] else [])
                    m1 = p.op("vector", lambda e, cc=cc, x=x, y=y: e.tensor_mul(
                        y[:, ts(cc, 512)], x[:, ts(cc, 512)], pB[:]), deps=dep)
                    m2 = p.op("vector", lambda e, cc=cc, y=y: e.tensor_sub(
                        y[:, ts(cc, 512)], y[:, ts(cc, 512)], pM[:]), deps=[m1, bM])
                    st = p.dma("sync", g[dst].ap()[b, cc, :, tt * 512:(tt + 1) * 512],
                               y[:, ts(cc, 512)], deps=[m2])
                    outs.append(st)
                    pM_tok = m2
                    pB_tok = m2
                xs_tok[slot] = outs[-1]
                ys_tok[slot] = outs[-1]
                slot = 1 - slot



def _prep_inputs(inputs):
    f = np.float32
    w = {}
    x = np.ascontiguousarray(inputs["L0_attn_x"], dtype=f)
    x_fm = np.ascontiguousarray(x.transpose(0, 2, 1)).reshape(B_TOT, CC, 128, S)

    def T2(a):
        return np.ascontiguousarray(np.ascontiguousarray(a.T, dtype=f).reshape(CC, 128, -1))

    def NAT(a):
        return np.ascontiguousarray(np.ascontiguousarray(a, dtype=f).reshape(CC, 128, -1))

    w["l0_wkT"] = T2(inputs["L0_attn_Wk"])
    w["l0_wvT"] = T2(inputs["L0_attn_Wv"])
    w["l0_wrT"] = T2(inputs["L0_attn_Wr"])
    w["l0_woT"] = T2(inputs["L0_attn_Wo"])
    par = np.zeros((128, 40), f)
    for i, nm in enumerate(["L0_attn_time_decay", "L0_attn_time_first",
                            "L0_attn_time_mix_k", "L0_attn_time_mix_v", "L0_attn_time_mix_r"]):
        v = np.asarray(inputs[nm], f).reshape(-1)
        par[:, i * 8:(i + 1) * 8] = v.reshape(CC, 128).T
    w["l0_par"] = par
    w["l0_n1w"] = np.ascontiguousarray(np.asarray(inputs["L0_ffn_norm1_w"], f).reshape(CC, 128).T)
    w["l0_n2w"] = np.ascontiguousarray(np.asarray(inputs["L0_ffn_norm2_w"], f).reshape(CC, 128).T)
    w["qkvT"] = np.ascontiguousarray(np.asarray(inputs["L0_ffn_qkv_w"], f).T.reshape(CC, 128, 3072))
    w["l0_qkvb"] = np.ascontiguousarray(np.asarray(inputs["L0_ffn_qkv_b"], f).reshape(24, 128).T)
    w["l0_outwT"] = T2(inputs["L0_ffn_out_w"])
    w["l0_outb"] = np.ascontiguousarray(np.asarray(inputs["L0_ffn_out_b"], f).reshape(CC, 128).T)
    w["gateT"] = np.ascontiguousarray(np.asarray(inputs["L0_ffn_gate_w"], f).T.reshape(CC, 128, 8))
    w["w1T"] = np.ascontiguousarray(np.asarray(inputs["L0_ffn_w1"], f).transpose(0, 2, 1).reshape(8, CC, 128, 2048))
    w["w2T"] = np.ascontiguousarray(np.asarray(inputs["L0_ffn_w2"], f).transpose(0, 2, 1).reshape(8, CC, 128, 2048))
    w["w3T"] = np.ascontiguousarray(np.asarray(inputs["L0_ffn_w3"], f).transpose(0, 2, 1).reshape(8, 16, 128, D))
    w["l0_b1"] = np.ascontiguousarray(np.asarray(inputs["L0_ffn_b1"], f).reshape(8, 16, 128).transpose(0, 2, 1))
    w["l0_b2"] = np.ascontiguousarray(np.asarray(inputs["L0_ffn_b2"], f).reshape(8, 16, 128).transpose(0, 2, 1))
    w["l0_b3"] = np.ascontiguousarray(np.asarray(inputs["L0_ffn_b3"], f).reshape(8, CC, 128).transpose(0, 2, 1))
    inv_freq = 1.0 / 10000.0 ** (np.arange(0, 32, dtype=np.float64) * 2.0 / 64.0)
    tgrid = np.arange(S, dtype=np.float64)
    ang = tgrid[None, :] * inv_freq[:, None]
    cos32 = np.cos(ang).astype(f); sin32 = np.sin(ang).astype(f)
    w["rope_cos"] = np.ascontiguousarray(np.tile(cos32, (4, 1)))
    w["rope_sin"] = np.ascontiguousarray(np.tile(sin32, (4, 1)))
    RT = np.zeros((128, 128), f)
    for blk in (0, 64):
        for i in range(32):
            RT[blk + i, blk + i + 32] = 1.0
            RT[blk + i + 32, blk + i] = -1.0
    w["ropeRT"] = RT
    oh = np.zeros((8, 8 * 128), f)
    for e_ in range(8):
        oh[e_, e_ * 128:(e_ + 1) * 128] = 1.0
    w["onehot8"] = oh
    w["l1_wq"] = NAT(inputs["L1_attn_Wq"]); w["l1_wk"] = NAT(inputs["L1_attn_Wk"])
    w["l1_wv"] = NAT(inputs["L1_attn_Wv"]); w["l1_wo"] = NAT(inputs["L1_attn_Wo"])
    w["l1_bo"] = np.ascontiguousarray(np.asarray(inputs["L1_attn_bo"], f).reshape(CC, 128).T)
    def get_slopes(n):
        def pow2(n_):
            start = 2.0 ** (-(2.0 ** (-(math.log2(n_) - 3))))
            return [start * start ** i for i in range(n_)]
        if math.log2(n).is_integer():
            return pow2(n)
        c = 2 ** math.floor(math.log2(n))
        return pow2(c) + get_slopes(2 * c)[0::2][:n - c]
    slopes = np.asarray(get_slopes(16), np.float64)
    ab = np.zeros((128, 1024), np.float64)
    prow = np.arange(128)
    for h in range(16):
        a8 = slopes[h] / 8.0
        for kc in range(16):
            for j in range(4):
                ab[:, h * 64 + kc * 4 + j] = a8 * (128 * kc + prow) - 20.0 - a8 * 512.0 * j
    w["alibias"] = ab.astype(f)
    cm = np.zeros((128, 896), f)
    jj = np.arange(896) - 384
    cm[:, :] = (jj[None, :] >= prow[:, None]).astype(f)
    w["causalm"] = cm
    w["l1_W1"] = np.ascontiguousarray(np.asarray(inputs["L1_ffn_W1"], f).reshape(CC, 128, 4096))
    w["l1_b1f"] = np.ascontiguousarray(np.asarray(inputs["L1_ffn_b1"], f).reshape(32, 128).T)
    w["l1_W2f"] = np.ascontiguousarray(np.asarray(inputs["L1_ffn_W2"], f).reshape(32, 128, D))
    w["l1_b2f"] = np.ascontiguousarray(np.asarray(inputs["L1_ffn_b2"], f).reshape(CC, 128).T)
    w["l1_g"] = np.ascontiguousarray(np.asarray(inputs["L1_ffn_ln_gamma"], f).reshape(CC, 128).T)
    w["l1_be"] = np.ascontiguousarray(np.asarray(inputs["L1_ffn_ln_beta"], f).reshape(CC, 128).T)
    w["l1_wr"] = np.ascontiguousarray(np.asarray(inputs["L1_ffn_Wr"], f).reshape(CC, 128, 8))
    w["l1_br"] = np.ascontiguousarray(np.asarray(inputs["L1_ffn_br"], f).reshape(8, 1))
    return x_fm, w


def kernel(**inputs):
    x_fm, w = _prep_inputs(inputs)
    nc = build_program()
    in_maps = []
    for c in range(NCORE):
        m = {"x_fm": np.ascontiguousarray(x_fm[c * BC:(c + 1) * BC])}
        m.update(w)
        in_maps.append(m)
    res = run_bass_kernel_spmd(nc, in_maps, core_ids=list(range(NCORE)))
    shards = [r["out_fm"] for r in res.results]
    out = np.concatenate(shards, axis=0)
    out = out.reshape(B_TOT, D, S).transpose(0, 2, 1)
    return np.ascontiguousarray(out, dtype=np.float32)


def emit_rms(p, nc, g, src, dst, wname, eps):
    with contextlib.ExitStack() as ph:
        psb = lambda name, shape, dt=F32: ph.enter_context(nc.sbuf_tensor(_nm(name), shape, dt))
        psum = lambda name, shape, dt=F32: ph.enter_context(nc.psum_tensor(_nm(name), shape, dt))
        xs = [psb(f"rm_x{i}", [128, CC * 512]) for i in range(2)]
        ys = [psb(f"rm_y{i}", [128, CC * 512], BF16) for i in range(2)]
        sqs = [psb(f"rm_sq{i}", [128, 512]) for i in range(2)]
        stV = psb("rm_stV", [1, 512])
        wsb = psb("rm_w", [128, CC])
        onec = psb("rm_one", [128, 1])
        one1 = psb("rm_one1", [1, 128])
        pQ = psum("rm_pQ", [1, 512])
        pB = psum("rm_pB", [128, 512])
        t_on = p.op("vector", lambda e: e.memset(onec[:], 1.0))
        t_on1 = p.op("vector", lambda e: e.memset(one1[:], 1.0))
        t_w = p.dma("sync", wsb[:], g[wname].ap())
        epsc, epst = p.constcol(eps, rows=1)
        xs_tok = [(None, 0)] * 2
        ys_tok = [(None, 0)] * 2
        sq_tok = [(None, 0)] * 2
        stV_tok = (None, 0)
        pQ_tok = (None, 0)
        pB_tok = (None, 0)
        slot = 0
        for b in range(BC):
            for tt in range(NT):
                x = xs[slot]; y = ys[slot]
                loads = []
                for cc in range(CC):
                    dep = [xs_tok[slot]] if xs_tok[slot][0] else []
                    loads.append(p.dma("sync", x[:, ts(cc, 512)],
                                       g[src].ap()[b, cc, :, tt * 512:(tt + 1) * 512], deps=dep))
                mmQ = None
                for cc in range(CC):
                    sq = sqs[cc % 2]
                    dep = [loads[cc]] + ([sq_tok[cc % 2]] if sq_tok[cc % 2][0] else [])
                    tq = p.op("scalar", lambda e, cc=cc, x=x, sq=sq: e.activation(
                        sq[:], x[:, ts(cc, 512)], AF.Square), deps=dep)
                    dep2 = [tq, t_on] + ([pQ_tok] if pQ_tok[0] and cc == 0 else [])
                    mmQ = p.op("tensor", lambda e, cc=cc, sq=sq: e.matmul(
                        pQ[:], onec[:], sq[:],
                        start=(cc == 0), stop=(cc == CC - 1)), deps=dep2)
                    sq_tok[cc % 2] = mmQ
                dep0 = [mmQ] + ([stV_tok] if stV_tok[0] else [])
                s1 = p.op("vector", lambda e: e.tensor_scalar(
                    stV[:], pQ[:], 1.0 / D, None, ALU.mult), deps=dep0)
                pQ_tok = s1
                s4 = p.op("scalar", lambda e: e.activation(stV[:], stV[:], AF.Ln, bias=epsc),
                          deps=[s1, epst])
                s5 = p.op("scalar", lambda e: e.activation(stV[:], stV[:], AF.Exp, scale=-0.5),
                          deps=[s4])
                depB = [s5, t_on1] + ([pB_tok] if pB_tok[0] else [])
                bB = p.op("tensor", lambda e: e.matmul(pB[:], one1[:], stV[:],
                                                       start=True, stop=True), deps=depB)
                stV_tok = bB
                outs = []
                m2 = None
                for cc in range(CC):
                    dep = [bB, loads[cc], t_w] + ([ys_tok[slot]] if ys_tok[slot][0] else [])
                    m2 = p.op("vector", lambda e, cc=cc, x=x, y=y: e.scalar_tensor_tensor(
                        y[:, ts(cc, 512)], x[:, ts(cc, 512)], wsb[:, cc:cc + 1], pB[:],
                        ALU.mult, ALU.mult), deps=dep)
                    outs.append(p.dma("sync", g[dst].ap()[b, cc, :, tt * 512:(tt + 1) * 512],
                                      y[:, ts(cc, 512)], deps=[m2]))
                pB_tok = m2
                xs_tok[slot] = outs[-1]
                ys_tok[slot] = outs[-1]
                slot = 1 - slot


def emit_l0attn(p, nc, g):
    neg20, t20 = p.constcol(-20.0)
    with contextlib.ExitStack() as ph:
        psb = lambda name, shape, dt=F32: ph.enter_context(nc.sbuf_tensor(_nm(name), shape, dt))
        psum = lambda name, shape, dt=F32: ph.enter_context(nc.psum_tensor(_nm(name), shape, dt))
        qkvTb = psb("a4_qkvT", [128, CC * 3072], BF16)
        outwTb = psb("a4_outwT", [128, CC * D], BF16)
        cosb = psb("a4_cos", [128, S], BF16)
        sinb = psb("a4_sin", [128, S], BF16)
        rtb = psb("a4_rt", [128, 128], BF16)
        qkvb = psb("a4_qb", [128, 24])
        qb8 = psb("a4_qb8", [128, CC])
        outb = psb("a4_ob", [128, CC])
        xnb = psb("a4_xn", [128, CC * S], BF16)
        vh = psb("a4_vh", [128, 16 * 16 * 65], BF16)
        ctxb = psb("a4_ctx", [128, CC * S], BF16)
        qh2 = psb("a4_qh", [128, S], BF16)
        kh2 = psb("a4_kh", [128, S], BF16)
        tmpA = psb("a4_tA", [128, 512])
        tmpB = psb("a4_tB", [128, 512])
        tmpctx = psb("a4_tc", [64, 512])
        rec = psb("a4_rec", [1, 512])
        one1 = psb("a4_one1", [1, 128])
        pts = [psb(f"a4_pt{i}", [128, 1024], BF16) for i in range(2)]
        evo = [psb(f"a4_ev{i}", [128, 512]) for i in range(2)]
        xres = [psb(f"a4_xr{i}", [128, 512]) for i in range(2)]
        ppq = [psum(f"a4_ppq{i}", [128, 1024]) for i in range(2)]
        pctxs = [psum(f"a4_pctx{i}", [128, 512]) for i in range(2)]
        pbc = psum("a4_pbc", [128, 512])
        wl = []
        for cc in range(CC):
            wl.append(p.dma("sync", qkvTb[:, cc * 3072:(cc + 1) * 3072], g["qkvTB"].ap()[cc]))
            wl.append(p.dma("sync", outwTb[:, ts(cc, D)], g["outwTB"].ap()[cc]))
        wl.append(p.dma("gpsimd", cosb[:], g["rope_cos"].ap()))
        wl.append(p.dma("gpsimd", sinb[:], g["rope_sin"].ap()))
        wl.append(p.dma("gpsimd", rtb[:], g["ropeRT"].ap()))
        wl.append(p.dma("sync", qkvb[:], g["l0_qkvb"].ap()))
        wl.append(p.dma("sync", outb[:], g["l0_outb"].ap()))
        t_q8 = p.op("scalar", lambda e: e.activation(qb8[:], qkvb[:, 0:8], AF.Identity, scale=0.125),
                    deps=wl)
        t_on1 = p.op("vector", lambda e: e.memset(one1[:], 1.0))
        t_vones = p.op("vector", lambda e: e.memset(vh[:, 64:16640:65], 1.0))

        pp_tok = [(None, 0)] * 4
        pt_tok = [(None, 0)] * 2
        evo_tok = [(None, 0)] * 2
        xres_tok = [(None, 0)] * 2
        xnb_tok = (None, 0)
        vh_tok = (None, 0)
        ctxb_tok = (None, 0)
        qh_tok = (None, 0)
        kh_tok = (None, 0)
        tA_tok = (None, 0); tB_tok = (None, 0)
        tctx_tok = (None, 0); rec_tok = (None, 0)
        pctx_tok = [(None, 0)] * 2; pbc_tok = (None, 0)
        def pbank(j):
            return ppq[j // 2][:, (j % 2) * 512:(j % 2) * 512 + 512]
        pslot = 0; qslot = 0; ptslot = 0; eslot = 0
        for b in range(BC):
            xl = []
            for cc in range(CC):
                dep = [xnb_tok] if xnb_tok[0] else []
                xl.append(p.dma("sync", xnb[:, cc * S:(cc + 1) * S],
                                g["bufXN"].ap()[b, cc], deps=dep))
            vtoks = []
            for kc in range(16):
                for vc in range(2):
                    pv = pbank(pslot)
                    mm = None
                    for icc in range(CC):
                        dep = xl + wl + ([pp_tok[pslot]] if pp_tok[pslot][0] else [])
                        mm = p.op("tensor", lambda e, pv=pv, icc=icc, kc=kc, vc=vc: e.matmul(
                            pv, xnb[:, icc * S + kc * 128: icc * S + kc * 128 + 128],
                            qkvTb[:, icc * 3072 + 2048 + vc * 512: icc * 3072 + 2048 + vc * 512 + 512],
                            start=(icc == 0), stop=(icc == CC - 1)), deps=dep, inc=(icc == CC - 1))
                    last = None
                    for j in range(8):
                        h = vc * 8 + j
                        dep = [mm, t_vones] + ([vh_tok] if vh_tok[0] else [])
                        last = p.op("vector", lambda e, pv=pv, j=j, kc=kc, h=h: e.tensor_copy(
                            vh[:, (kc * 16 + h) * 65: (kc * 16 + h) * 65 + 64],
                            pv[:, j * 64: j * 64 + 64]), deps=dep)
                        vtoks.append(last)
                    pp_tok[pslot] = last
                    pslot = (pslot + 1) % 4
            av_last = []
            qk_last = []
            qk_cons = (None, 0)
            for occ in range(CC):
                for (which, colbase) in (("q", 0), ("k", 1024)):
                    buf = qh2 if which == "q" else kh2
                    btok = qh_tok if which == "q" else kh_tok
                    for tt in range(NT):
                        pq = pbank(pslot)
                        mm = None
                        for icc in range(CC):
                            dep = xl + wl + ([pp_tok[pslot]] if pp_tok[pslot][0] else []) + ([btok] if btok[0] else [])
                            mm = p.op("tensor", lambda e, pq=pq, icc=icc, occ=occ, tt=tt, colbase=colbase: e.matmul(
                                pq, qkvTb[:, icc * 3072 + colbase + occ * 128: icc * 3072 + colbase + occ * 128 + 128],
                                xnb[:, icc * S + tt * 512: icc * S + tt * 512 + 512],
                                start=(icc == 0), stop=(icc == CC - 1)), deps=dep, inc=(icc == CC - 1))
                        qk_last.append(mm)
                        wardep = [qk_cons] if qk_cons[0] else []
                        if which == "q":
                            et = p.op("scalar", lambda e, pq=pq, buf=buf, tt=tt, occ=occ: e.activation(
                                buf[:, tt * 512:(tt + 1) * 512], pq, AF.Identity,
                                scale=0.125, bias=qb8[:, occ:occ + 1]), deps=[mm, t_q8] + wardep)
                        else:
                            et = p.op("scalar", lambda e, pq=pq, buf=buf, tt=tt, occ=occ: e.activation(
                                buf[:, tt * 512:(tt + 1) * 512], pq, AF.Identity,
                                bias=qkvb[:, 8 + occ:9 + occ]), deps=[mm] + wardep)
                        pp_tok[pslot] = et
                        pslot = (pslot + 1) % 4
                        pr = pbank(pslot)
                        dep = [et] + ([pp_tok[pslot]] if pp_tok[pslot][0] else [])
                        mmr = p.op("tensor", lambda e, pr=pr, buf=buf, tt=tt: e.matmul(
                            pr, rtb[:], buf[:, tt * 512:(tt + 1) * 512],
                            start=True, stop=True), deps=dep + wl)
                        a1t = p.op("vector", lambda e, buf=buf, tt=tt: e.tensor_mul(
                            tmpA[:], buf[:, tt * 512:(tt + 1) * 512], cosb[:, tt * 512:(tt + 1) * 512]),
                            deps=[et] + ([tA_tok] if tA_tok[0] else []))
                        b1t = p.op("vector", lambda e, pr=pr, tt=tt: e.tensor_mul(
                            tmpB[:], pr, sinb[:, tt * 512:(tt + 1) * 512]),
                            deps=[mmr] + ([tB_tok] if tB_tok[0] else []))
                        c1t = p.op("vector", lambda e, buf=buf, tt=tt: e.tensor_add(
                            buf[:, tt * 512:(tt + 1) * 512], tmpA[:], tmpB[:]),
                            deps=[a1t, b1t])
                        tA_tok = c1t; tB_tok = c1t
                        pp_tok[pslot] = c1t
                        pslot = (pslot + 1) % 4
                        if which == "q":
                            qh_tok = c1t
                        else:
                            kh_tok = c1t
                for qt in range(NT):
                    avmm = [None, None]
                    sq_pend = {}

                    def emit_S(kc, qt=qt, occ=occ):
                        nonlocal qslot, qk_cons
                        pq = ppq[qslot]
                        smm = None
                        for hh in range(2):
                            hr = 64 * hh
                            wdep = [t for t in (pp_tok[2 * qslot], pp_tok[2 * qslot + 1]) if t[0]] if hh == 0 else []
                            dep = [qh_tok, kh_tok] + wdep
                            smm = p.op("tensor", lambda e, pq=pq, hr=hr, kc=kc, qt=qt, hh=hh: e.matmul(
                                pq[0:128, hh * 512:(hh + 1) * 512],
                                kh2[hr:hr + 64, kc * 128:(kc + 1) * 128],
                                qh2[hr:hr + 64, qt * 512:(qt + 1) * 512],
                                start=True, stop=True), deps=dep, inc=(hh == 1))
                        qk_cons = smm
                        sq_pend[kc] = (pq, qslot, smm)
                        qslot = 1 - qslot

                    emit_S(0)
                    for kc in range(16):
                        if kc + 1 < 16:
                            emit_S(kc + 1)
                        pq, psl, smm = sq_pend.pop(kc)
                        pt = pts[ptslot]
                        dep = [smm, t20] + ([pt_tok[ptslot]] if pt_tok[ptslot][0] else [])
                        ex = p.op("scalar", lambda e, pq=pq, pt=pt: e.activation(
                            pt[:], pq[:], AF.Exp, bias=neg20), deps=dep)
                        pp_tok[2 * psl] = ex
                        pp_tok[2 * psl + 1] = ex
                        for hh in range(2):
                            h = 2 * occ + hh
                            dep = [ex] + vtoks[-1:] + ([pctx_tok[hh]] if pctx_tok[hh][0] and kc == 0 else [])
                            avmm[hh] = p.op("tensor", lambda e, pt=pt, kc=kc, h=h, hh=hh: e.matmul(
                                pctxs[hh][0:65, :], vh[:, (kc * 16 + h) * 65: (kc * 16 + h) * 65 + 65],
                                pt[:, hh * 512:(hh + 1) * 512], start=(kc == 0), stop=(kc == 15)),
                                deps=dep, inc=(kc == 15))
                        pt_tok[ptslot] = avmm[1]
                        ptslot = 1 - ptslot
                    for hh in range(2):
                        h = 2 * occ + hh
                        hr = 64 * hh
                        pctx = pctxs[hh]
                        av_last.append(avmm[hh])
                        tL = p.op("scalar", lambda e, pctx=pctx: e.activation(rec[:], pctx[64:65, :], AF.Ln),
                                  deps=[avmm[hh]] + ([rec_tok] if rec_tok[0] else []))
                        tE = p.op("scalar", lambda e: e.activation(rec[:], rec[:], AF.Exp, scale=-1.0),
                                  deps=[tL])
                        dep = [tE, t_on1] + ([pbc_tok] if pbc_tok[0] else [])
                        bc = p.op("tensor", lambda e: e.matmul(pbc[0:64, :], one1[:, 0:64], rec[:],
                                                               start=True, stop=True), deps=dep)
                        tc1 = p.op("scalar", lambda e, pctx=pctx: e.copy(tmpctx[:], pctx[0:64, :]),
                                   deps=[avmm[hh], tE] + ([tctx_tok] if tctx_tok[0] else []))
                        pctx_tok[hh] = tc1
                        mu = p.op("vector", lambda e: e.tensor_mul(tmpctx[:], tmpctx[:], pbc[0:64, :]),
                                  deps=[tc1, bc])
                        pbc_tok = mu
                        rec_tok = bc
                        dep = [mu] + ([ctxb_tok] if ctxb_tok[0] else [])
                        cb = p.op("scalar", lambda e, h=h, hr=hr, qt=qt, occ=occ: e.activation(
                            ctxb[hr:hr + 64, occ * S + qt * 512: occ * S + qt * 512 + 512],
                            tmpctx[:], AF.Identity,
                            bias=qkvb[hr:hr + 64, 16 + occ:17 + occ]), deps=dep)
                        tctx_tok = cb
                        av_last.append(cb)
            xnb_tok = qk_last[-1]
            vh_tok = av_last[-1]
            op_last = []
            for occ in range(CC):
                for tt in range(NT):
                    po = pbank(pslot)
                    mm = None
                    for icc in range(CC):
                        dep = av_last[-1:] + wl + ([pp_tok[pslot]] if pp_tok[pslot][0] else [])
                        mm = p.op("tensor", lambda e, po=po, icc=icc, occ=occ, tt=tt: e.matmul(
                            po, outwTb[:, icc * D + occ * 128: icc * D + occ * 128 + 128],
                            ctxb[:, icc * S + tt * 512: icc * S + tt * 512 + 512],
                            start=(icc == 0), stop=(icc == CC - 1)), deps=dep, inc=(icc == CC - 1))
                    op_last.append(mm)
                    eb = evo[eslot]; xr = xres[eslot]
                    lx = p.dma("sync", xr[:], g["bufX1"].ap()[b, occ, :, tt * 512:(tt + 1) * 512],
                               deps=[xres_tok[eslot]] if xres_tok[eslot][0] else [])
                    e1 = p.op("scalar", lambda e, po=po, eb=eb, occ=occ: e.activation(
                        eb[:], po, AF.Identity, bias=outb[:, occ:occ + 1]),
                        deps=[mm] + ([evo_tok[eslot]] if evo_tok[eslot][0] else []))
                    pp_tok[pslot] = e1
                    pslot = (pslot + 1) % 4
                    e2 = p.op("vector", lambda e, eb=eb, xr=xr: e.tensor_add(eb[:], eb[:], xr[:]),
                              deps=[e1, lx])
                    st = p.dma("sync", g["bufX2"].ap()[b, occ, :, tt * 512:(tt + 1) * 512],
                               eb[:], deps=[e2])
                    evo_tok[eslot] = st
                    xres_tok[eslot] = e2
                    eslot = 1 - eslot
            ctxb_tok = op_last[-1]


def emit_moe(p, nc, g):
    with contextlib.ExitStack() as ph:
        psb = lambda name, shape, dt=F32: ph.enter_context(nc.sbuf_tensor(_nm(name), shape, dt))
        psum = lambda name, shape, dt=F32: ph.enter_context(nc.psum_tensor(_nm(name), shape, dt))
        W1s = psb("m_w1", [128, CC * 2048], BF16)
        W2s = psb("m_w2", [128, CC * 2048], BF16)
        W3s = psb("m_w3", [128, 16 * D], BF16)
        acc = psb("m_acc", [128, 2 * CC * 512])
        gbufs = [psb(f"m_g{i}", [128, 16 * 512], BF16) for i in range(2)]
        s1s = [psb(f"m_s1{i}", [128, 512]) for i in range(2)]
        t1b = [psb(f"m_t1{i}", [128, 512]) for i in range(2)]
        combSB = psb("m_csb", [128, 512])
        xt = [psb(f"m_xt{i}", [128, CC * 512], BF16) for i in range(2)]
        xg = psb("m_xg", [128, CC * 128], BF16)
        gtb = psb("m_gt", [128, CC * 8], BF16)
        lg = psb("m_lg", [128, 8])
        m1 = psb("m_m1", [128, 1]); m2 = psb("m_m2", [128, 1])
        e1 = psb("m_e1", [128, 8]); eq2 = psb("m_eq2", [128, 8])
        mk = psb("m_mk", [128, 8]); c1 = psb("m_c1", [128, 8])
        g1 = psb("m_g1", [128, 1]); g2 = psb("m_g2", [128, 1])
        combT = psb("m_cT", [128, 8])
        combF = psb("m_cF", [8, S])
        idf = psb("m_idf", [128, 128])
        oneh = psb("m_oneh", [8, 8 * 128])
        b1c = psb("m_b1", [128, 16]); b2c = psb("m_b2", [128, 16]); b3c = psb("m_b3", [128, CC])
        xr = [psb(f"m_xr{i}", [128, 512]) for i in range(2)]
        pAs = [psum(f"m_pA{i}", [128, 512]) for i in range(2)]
        pBs = [psum(f"m_pB{i}", [128, 512]) for i in range(2)]
        pC = [psum(f"m_pC{i}", [128, 512]) for i in range(2)]
        pD = psum("m_pD", [128, 512])
        pG = psum("m_pG", [128, 512])
        ti0 = p.op("gpsimd", lambda e: e.memset(idf[:], 1.0))
        ti1 = p.op("gpsimd", lambda e: e.affine_select(idf[:], idf[:], [[1, 128]], ALU.is_equal,
                                                       0.0, base=0, channel_multiplier=-1), deps=[ti0])
        t_oh = p.dma("sync", oneh[:], g["onehot8"].ap())
        wg = [p.dma("sync", gtb[:, cc * 8:(cc + 1) * 8], g["gateTB"].ap()[cc]) for cc in range(CC)]

        pp_tok = {k: (None, 0) for k in ["A0", "A1", "B0", "B1", "C0", "C1", "D", "G"]}
        s1_tok = [(None, 0)] * 2
        t1_tok = [(None, 0)] * 2
        comb_toks = {}
        gb_tok = [(None, 0)] * 2
        gslot = 0
        acc_tok = {}
        xt_tok = [(None, 0)] * 2
        xg_tok = (None, 0)
        lg_tok = (None, 0)
        combF_tok = (None, 0)
        combSB_tok = (None, 0)
        w_tok = (None, 0)
        h_cons = (None, 0)
        y_cons = (None, 0)
        xr_tok = [(None, 0)] * 2
        cslot = 0; xslot = 0; rslot = 0
        for b in range(BC):
            gtoks = []
            for tc in range(16):
                xls = []
                for icc in range(CC):
                    dep = [xg_tok] if xg_tok[0] else []
                    xls.append(p.dma("sync", xg[:, icc * 128:(icc + 1) * 128],
                                     g["bufXN"].ap()[b, icc, :, tc * 128:(tc + 1) * 128], deps=dep))
                mm = None
                for icc in range(CC):
                    dep = xls + wg + ([pp_tok["G"]] if pp_tok["G"][0] else [])
                    mm = p.op("tensor", lambda e, icc=icc: e.matmul(
                        pG[0:128, 0:8], xg[:, icc * 128:(icc + 1) * 128], gtb[:, icc * 8:(icc + 1) * 8],
                        start=(icc == 0), stop=(icc == CC - 1)), deps=dep, inc=(icc == CC - 1))
                xg_tok = mm
                cp = p.op("vector", lambda e: e.tensor_copy(lg[:], pG[0:128, 0:8]),
                          deps=[mm] + ([lg_tok] if lg_tok[0] else []))
                pp_tok["G"] = cp
                v1 = p.op("vector", lambda e: e.reduce_max(m1[:], lg[:], axis=AX), deps=[cp])
                v2 = p.op("vector", lambda e: e.tensor_scalar(e1[:], lg[:], m1[:, 0:1], None, ALU.is_equal), deps=[v1])
                v3 = p.op("vector", lambda e: e.scalar_tensor_tensor(mk[:], e1[:], -1e30, lg[:], ALU.mult, ALU.add), deps=[v2])
                v4 = p.op("vector", lambda e: e.reduce_max(m2[:], mk[:], axis=AX), deps=[v3])
                v5 = p.op("vector", lambda e: e.tensor_sub(m1[:], m1[:], m2[:]), deps=[v4])
                v6 = p.op("scalar", lambda e: e.activation(g1[:], m1[:], AF.Sigmoid), deps=[v5])
                v7 = p.op("scalar", lambda e: e.activation(g2[:], g1[:], AF.Identity, scale=-1.0, bias=1.0), deps=[v6])
                v8 = p.op("vector", lambda e: e.tensor_scalar(eq2[:], lg[:], m2[:, 0:1], None, ALU.is_equal), deps=[v4])
                v9 = p.op("vector", lambda e: e.tensor_scalar(c1[:], e1[:], g1[:, 0:1], None, ALU.mult), deps=[v6, v8])
                vA = p.op("vector", lambda e: e.tensor_scalar(eq2[:], eq2[:], g2[:, 0:1], None, ALU.mult), deps=[v7, v9])
                vB = p.op("vector", lambda e: e.tensor_add(combT[:], c1[:], eq2[:]),
                          deps=[vA] + ([comb_toks.get("T", (None, 0))] if comb_toks.get("T", (None, 0))[0] else []))
                lg_tok = vB
                tr = p.op("tensor", lambda e: e.transpose(pG[0:8, 0:128], combT[:], idf[:]),
                          deps=[vB, ti1, cp])
                cf = p.op("scalar", lambda e, tc=tc: e.copy(combF[0:8, tc * 128:(tc + 1) * 128], pG[0:8, 0:128]),
                          deps=[tr] + ([combF_tok] if combF_tok[0] else []))
                pp_tok["G"] = cf
                comb_toks["T"] = tr
                gtoks.append(cf)
            for tg in range(2):
              for ei in range(8):
                  wl = []
                  dep12 = [h_cons] if h_cons[0] else []
                  dep3 = [y_cons] if y_cons[0] else []
                  for cc in range(CC):
                      wl.append(p.dma("sync", W1s[:, cc * 2048:(cc + 1) * 2048], g["w1TB"].ap()[ei, cc], deps=dep12))
                      wl.append(p.dma("sync", W2s[:, cc * 2048:(cc + 1) * 2048], g["w2TB"].ap()[ei, cc], deps=dep12))
                  for hc in range(16):
                      wl.append(p.dma("sync", W3s[:, hc * D:(hc + 1) * D], g["w3TB"].ap()[ei, hc], deps=dep3))
                  depb = [w_tok] if w_tok[0] else []
                  wl.append(p.dma("sync", b1c[:], g["l0_b1"].ap()[ei], deps=depb))
                  wl.append(p.dma("sync", b2c[:], g["l0_b2"].ap()[ei], deps=depb))
                  wl.append(p.dma("sync", b3c[:], g["l0_b3"].ap()[ei], deps=depb))
                  wlast = []
                  for tt in (2 * tg, 2 * tg + 1):
                      gbuf = gbufs[gslot]
                      x = xt[xslot]
                      xls = []
                      for icc in range(CC):
                          dep = [xt_tok[xslot]] if xt_tok[xslot][0] else []
                          xls.append(p.dma("sync", x[:, ts(icc, 512)],
                                           g["bufXN"].ap()[b, icc, :, tt * 512:(tt + 1) * 512], deps=dep))
                      dep = gtoks + [t_oh] + ([pp_tok["D"]] if pp_tok["D"][0] else [])
                      bd = p.op("tensor", lambda e, ei=ei, tt=tt: e.matmul(
                          pD[:], oneh[:, ei * 128:(ei + 1) * 128], combF[0:8, tt * 512:(tt + 1) * 512],
                          start=True, stop=True), deps=dep)
                      cs = p.op("scalar", lambda e: e.copy(combSB[:], pD[:]),
                                deps=[bd] + ([combSB_tok] if combSB_tok[0] else []))
                      pp_tok["D"] = cs
                      hlast = None
                      for hc in range(16):
                          hci = hc % 2
                          pA = pAs[hci]; pB = pBs[hci]
                          mmA = None
                          for icc in range(CC):
                              dep = xls + wl + ([pp_tok["A%d" % hci]] if pp_tok["A%d" % hci][0] and icc == 0 else [])
                              mmA = p.op("tensor", lambda e, icc=icc, hc=hc, x=x, pA=pA: e.matmul(
                                  pA[:], W1s[:, icc * 2048 + hc * 128: icc * 2048 + hc * 128 + 128],
                                  x[:, ts(icc, 512)], start=(icc == 0), stop=(icc == CC - 1)), deps=dep, inc=(icc == CC - 1))
                          mmB = None
                          for icc in range(CC):
                              dep = ([pp_tok["B%d" % hci]] if pp_tok["B%d" % hci][0] and icc == 0 else [])
                              mmB = p.op("tensor", lambda e, icc=icc, hc=hc, x=x, pB=pB: e.matmul(
                                  pB[:], W2s[:, icc * 2048 + hc * 128: icc * 2048 + hc * 128 + 128],
                                  x[:, ts(icc, 512)], start=(icc == 0), stop=(icc == CC - 1)), deps=dep, inc=(icc == CC - 1))
                          s1t = s1s[hci]
                          sv = p.op("scalar", lambda e, hc=hc, pA=pA, s1t=s1t: e.activation(
                              s1t[:], pA[:], AF.Silu, bias=b1c[:, hc:hc + 1]),
                              deps=[mmA] + ([s1_tok[hci]] if s1_tok[hci][0] else []))
                          pp_tok["A%d" % hci] = sv
                          gv = p.op("vector", lambda e, hc=hc, pB=pB, s1t=s1t, gbuf=gbuf: e.scalar_tensor_tensor(
                              gbuf[:, hc * 512:(hc + 1) * 512], pB[:], b2c[:, hc:hc + 1], s1t[:],
                              ALU.add, ALU.mult),
                              deps=[mmB, sv] + ([gb_tok[gslot]] if gb_tok[gslot][0] and hc == 0 else []))
                          s1_tok[hci] = gv
                          pp_tok["B%d" % hci] = gv
                          hlast = gv
                          h_cons = mmB
                      alast = None
                      for occ in range(CC):
                          ci = occ % 2
                          pc = pC[ci]
                          mmC = None
                          for hc in range(16):
                              dep = [hlast, wl[-1]] + ([pp_tok["C%d" % ci]] if pp_tok["C%d" % ci][0] and hc == 0 else [])
                              mmC = p.op("tensor", lambda e, hc=hc, occ=occ, pc=pc, gbuf=gbuf: e.matmul(
                                  pc[:], W3s[:, hc * D + occ * 128: hc * D + occ * 128 + 128],
                                  gbuf[:, hc * 512:(hc + 1) * 512],
                                  start=(hc == 0), stop=(hc == 15)), deps=dep, inc=(hc == 15))
                          y_cons = mmC
                          tb = t1b[occ % 2]
                          aslice = acc[:, ((tt - 2 * tg) * CC + occ) * 512: ((tt - 2 * tg) * CC + occ) * 512 + 512]
                          if ei == 0:
                              tv = p.op("vector", lambda e, pc=pc, occ=occ, aslice=aslice: e.scalar_tensor_tensor(
                                  aslice, pc[:], b3c[:, occ:occ + 1], combSB[:], ALU.add, ALU.mult),
                                  deps=[mmC, cs] + ([acc_tok.get((tt - 2 * tg, occ), (None, 0))] if acc_tok.get((tt - 2 * tg, occ), (None, 0))[0] else []))
                              pp_tok["C%d" % ci] = tv
                              acc_tok[(tt - 2 * tg, occ)] = tv
                              alast = tv
                          else:
                              tv = p.op("vector", lambda e, pc=pc, occ=occ, tb=tb: e.scalar_tensor_tensor(
                                  tb[:], pc[:], b3c[:, occ:occ + 1], combSB[:], ALU.add, ALU.mult),
                                  deps=[mmC, cs] + ([t1_tok[occ % 2]] if t1_tok[occ % 2][0] else []))
                              pp_tok["C%d" % ci] = tv
                              av = p.op("gpsimd", lambda e, tb=tb, aslice=aslice: e.tensor_add(
                                  aslice, aslice, tb[:]),
                                  deps=[tv, acc_tok.get((tt - 2 * tg, occ), (None, 0))])
                              t1_tok[occ % 2] = av
                              acc_tok[(tt - 2 * tg, occ)] = av
                              alast = av
                      wlast.append(alast)
                      gb_tok[gslot] = alast
                      combSB_tok = alast
                      xt_tok[xslot] = alast
                      xslot = 1 - xslot
                      gslot = 1 - gslot
                  w_tok = wlast[-1]
              for tt in (2 * tg, 2 * tg + 1):
                for occ in range(CC):
                    xrb = xr[rslot]
                    lx = p.dma("sync", xrb[:], g["bufX2"].ap()[b, occ, :, tt * 512:(tt + 1) * 512],
                               deps=[xr_tok[rslot]] if xr_tok[rslot][0] else [])
                    aslice = acc[:, ((tt - 2 * tg) * CC + occ) * 512: ((tt - 2 * tg) * CC + occ) * 512 + 512]
                    ad = p.op("vector", lambda e, xrb=xrb, aslice=aslice: e.tensor_add(
                        xrb[:], xrb[:], aslice), deps=[lx, acc_tok[(tt - 2 * tg, occ)]])
                    st = p.dma("sync", g["bufX3"].ap()[b, occ, :, tt * 512:(tt + 1) * 512],
                               xrb[:], deps=[ad])
                    xr_tok[rslot] = st
                    rslot = 1 - rslot
                    acc_tok[(tt - 2 * tg, occ)] = ad


def emit_l1attn(p, nc, g):
    with contextlib.ExitStack() as ph:
        psb = lambda name, shape, dt=F32: ph.enter_context(nc.sbuf_tensor(_nm(name), shape, dt))
        psum = lambda name, shape, dt=F32: ph.enter_context(nc.psum_tensor(_nm(name), shape, dt))
        wqb = psb("l1_wqb", [128, CC * D], BF16)
        wkb = psb("l1_wkb", [128, CC * D], BF16)
        wvb = psb("l1_wvb", [128, CC * D], BF16)
        wob = psb("l1_wob", [128, CC * D], BF16)
        bo = psb("l1_bob", [128, CC])
        ab = psb("l1_ab", [128, 1024])
        cmf = psb("l1_cmf", [128, 896])
        cmb = psb("l1_cmb", [128, 896], BF16)
        xb = psb("l1_xb", [128, CC * S], BF16)
        vh = psb("l1_vh", [128, 16 * 16 * 65], BF16)
        ctxb = psb("l1_ctx", [128, CC * S], BF16)
        qh2 = psb("l1_qh", [128, S], BF16)
        kh2 = psb("l1_kh", [128, S], BF16)
        tmpctx = psb("l1_tc", [64, 512])
        rec = psb("l1_rec", [1, 512])
        one1 = psb("l1_one1", [1, 128])
        pts = [psb(f"l1_pt{i}", [128, 512], BF16) for i in range(4)]
        evo = [psb(f"l1_ev{i}", [128, 512]) for i in range(2)]
        pp = [psum(f"l1_pp{i}", [128, 512]) for i in range(4)]
        pctxs = [psum(f"l1_pctx{i}", [128, 512]) for i in range(2)]
        pbc = psum("l1_pbc", [128, 512])
        wl = []
        for cc in range(CC):
            wl.append(p.dma("gpsimd", wqb[:, ts(cc, D)], g["l1_wq"].ap()[cc]))
            wl.append(p.dma("gpsimd", wkb[:, ts(cc, D)], g["l1_wk"].ap()[cc]))
            wl.append(p.dma("gpsimd", wvb[:, ts(cc, D)], g["l1_wv"].ap()[cc]))
            wl.append(p.dma("gpsimd", wob[:, ts(cc, D)], g["l1_wo"].ap()[cc]))
        wl.append(p.dma("sync", bo[:], g["l1_bo"].ap()))
        wl.append(p.dma("sync", ab[:], g["alibias"].ap()))
        wl.append(p.dma("sync", cmf[:], g["causalm"].ap()))
        t_cm = p.op("scalar", lambda e: e.copy(cmb[:], cmf[:]), deps=wl)
        t_on1 = p.op("vector", lambda e: e.memset(one1[:], 1.0))
        t_vones = p.op("vector", lambda e: e.memset(vh[:, 64:16640:65], 1.0))

        pp_tok = [(None, 0)] * 4
        pt_tok = [(None, 0)] * 4
        evo_tok = [(None, 0)] * 2
        xb_tok = (None, 0)
        vh_tok = (None, 0)
        ctxb_tok = (None, 0)
        tctx_tok = (None, 0); rec_tok = (None, 0)
        pctx_tok = [(None, 0)] * 2; pbc_tok = (None, 0)
        pslot = 0; ptslot = 0; eslot = 0
        for b in range(BC):
            xl = []
            for cc in range(CC):
                dep = [xb_tok] if xb_tok[0] else []
                xl.append(p.dma("gpsimd", xb[:, cc * S:(cc + 1) * S],
                                g["bufX4"].ap()[b, cc], deps=dep))
            vtoks = []
            for kc in range(16):
                for vc in range(2):
                    pv = pp[pslot]
                    mm = None
                    for icc in range(CC):
                        dep = xl + wl + ([pp_tok[pslot]] if pp_tok[pslot][0] else [])
                        mm = p.op("tensor", lambda e, pv=pv, icc=icc, kc=kc, vc=vc: e.matmul(
                            pv[:], xb[:, icc * S + kc * 128: icc * S + kc * 128 + 128],
                            wvb[:, icc * D + vc * 512: icc * D + vc * 512 + 512],
                            start=(icc == 0), stop=(icc == CC - 1)), deps=dep, inc=(icc == CC - 1))
                    last = None
                    for j in range(8):
                        h = vc * 8 + j
                        dep = [mm, t_vones] + ([vh_tok] if vh_tok[0] else [])
                        last = p.op("vector", lambda e, pv=pv, j=j, kc=kc, h=h: e.tensor_copy(
                            vh[:, (kc * 16 + h) * 65: (kc * 16 + h) * 65 + 64],
                            pv[:, j * 64: j * 64 + 64]), deps=dep)
                        vtoks.append(last)
                    pp_tok[pslot] = last
                    pslot = (pslot + 1) % 4
            av_last = []
            qk_last = []
            qk_cons = (None, 0)
            for occ in range(CC):
                for (which, wt) in (("q", wqb), ("k", wkb)):
                    buf = qh2 if which == "q" else kh2
                    for tt in range(NT):
                        pq = pp[pslot]
                        mm = None
                        for icc in range(CC):
                            dep = xl + wl + ([pp_tok[pslot]] if pp_tok[pslot][0] else [])
                            mm = p.op("tensor", lambda e, pq=pq, icc=icc, occ=occ, tt=tt, wt=wt: e.matmul(
                                pq[:], wt[:, icc * D + occ * 128: icc * D + occ * 128 + 128],
                                xb[:, icc * S + tt * 512: icc * S + tt * 512 + 512],
                                start=(icc == 0), stop=(icc == CC - 1)), deps=dep, inc=(icc == CC - 1))
                        qk_last.append(mm)
                        wardep = [qk_cons] if qk_cons[0] else []
                        if which == "q":
                            et = p.op("scalar", lambda e, pq=pq, buf=buf, tt=tt: e.activation(
                                buf[:, tt * 512:(tt + 1) * 512], pq[:], AF.Identity,
                                scale=0.125), deps=[mm] + wardep)
                        else:
                            et = p.op("scalar", lambda e, pq=pq, buf=buf, tt=tt: e.copy(
                                buf[:, tt * 512:(tt + 1) * 512], pq[:]), deps=[mm] + wardep)
                        pp_tok[pslot] = et
                        pslot = (pslot + 1) % 4
                        if which == "q":
                            qh_tok = et
                        else:
                            kh_tok = et
                for qt in range(NT):
                    nkc = 4 * qt + 4
                    avmm = [None, None]
                    sq_pend = {}

                    def emit_S(kc, qt=qt, occ=occ):
                        nonlocal pslot, qk_cons
                        for hh in range(2):
                            hr = 64 * hh
                            pS = pp[pslot]
                            dep = [qh_tok, kh_tok] + ([pp_tok[pslot]] if pp_tok[pslot][0] else [])
                            smm = p.op("tensor", lambda e, pS=pS, hr=hr, kc=kc, qt=qt: e.matmul(
                                pS[0:128, :], kh2[hr:hr + 64, kc * 128:(kc + 1) * 128],
                                qh2[hr:hr + 64, qt * 512:(qt + 1) * 512],
                                start=True, stop=True), deps=dep)
                            qk_cons = smm
                            sq_pend[(kc, hh)] = (pS, pslot, smm)
                            pslot = (pslot + 1) % 4

                    emit_S(0)
                    for kc in range(nkc):
                        if kc + 1 < nkc:
                            emit_S(kc + 1)
                        for hh in range(2):
                            h = 2 * occ + hh
                            pS, psl, smm = sq_pend.pop((kc, hh))
                            pt = pts[2 * hh + ptslot]
                            dep = [smm] + ([pt_tok[2 * hh + ptslot]] if pt_tok[2 * hh + ptslot][0] else [])
                            ex = p.op("scalar", lambda e, pS=pS, pt=pt, h=h, kc=kc, qt=qt: e.activation(
                                pt[:], pS[:], AF.Exp,
                                bias=ab[:, h * 64 + kc * 4 + qt: h * 64 + kc * 4 + qt + 1]), deps=dep)
                            pp_tok[psl] = ex
                            lastpt = ex
                            if kc >= 4 * qt:
                                delta = qt * 512 - kc * 128 + 384
                                lastpt = p.op("vector", lambda e, pt=pt, delta=delta: e.tensor_mul(
                                    pt[:], pt[:], cmb[:, delta: delta + 512]), deps=[ex, t_cm])
                            dep = [lastpt] + vtoks[-1:] + ([pctx_tok[hh]] if pctx_tok[hh][0] and kc == 0 else [])
                            avmm[hh] = p.op("tensor", lambda e, pt=pt, kc=kc, h=h, nkc=nkc, hh=hh: e.matmul(
                                pctxs[hh][0:65, :], vh[:, (kc * 16 + h) * 65: (kc * 16 + h) * 65 + 65],
                                pt[:], start=(kc == 0), stop=(kc == nkc - 1)), deps=dep,
                                inc=(kc == nkc - 1))
                            pt_tok[2 * hh + ptslot] = avmm[hh]
                        ptslot = 1 - ptslot
                    for hh in range(2):
                        h = 2 * occ + hh
                        hr = 64 * hh
                        pctx = pctxs[hh]
                        av_last.append(avmm[hh])
                        tL = p.op("scalar", lambda e, pctx=pctx: e.activation(rec[:], pctx[64:65, :], AF.Ln),
                                  deps=[avmm[hh]] + ([rec_tok] if rec_tok[0] else []))
                        tE = p.op("scalar", lambda e: e.activation(rec[:], rec[:], AF.Exp, scale=-1.0),
                                  deps=[tL])
                        dep = [tE, t_on1] + ([pbc_tok] if pbc_tok[0] else [])
                        bc = p.op("tensor", lambda e: e.matmul(pbc[0:64, :], one1[:, 0:64], rec[:],
                                                               start=True, stop=True), deps=dep)
                        tc1 = p.op("scalar", lambda e, pctx=pctx: e.copy(tmpctx[:], pctx[0:64, :]),
                                   deps=[avmm[hh], tE] + ([tctx_tok] if tctx_tok[0] else []))
                        pctx_tok[hh] = tc1
                        dep = [tc1, bc] + ([ctxb_tok] if ctxb_tok[0] else [])
                        mu = p.op("vector", lambda e, h=h, hr=hr, qt=qt, occ=occ: e.tensor_mul(
                            ctxb[hr:hr + 64, occ * S + qt * 512: occ * S + qt * 512 + 512],
                            tmpctx[:], pbc[0:64, :]), deps=dep)
                        pbc_tok = mu
                        rec_tok = bc
                        tctx_tok = mu
                        av_last.append(mu)
            xb_tok = qk_last[-1]
            vh_tok = av_last[-1]
            op_last = []
            for occ in range(CC):
                for tt in range(NT):
                    po = pp[pslot]
                    mm = None
                    for icc in range(CC):
                        dep = av_last[-1:] + wl + ([pp_tok[pslot]] if pp_tok[pslot][0] else [])
                        mm = p.op("tensor", lambda e, po=po, icc=icc, occ=occ, tt=tt: e.matmul(
                            po[:], wob[:, icc * D + occ * 128: icc * D + occ * 128 + 128],
                            ctxb[:, icc * S + tt * 512: icc * S + tt * 512 + 512],
                            start=(icc == 0), stop=(icc == CC - 1)), deps=dep, inc=(icc == CC - 1))
                    op_last.append(mm)
                    eb = evo[eslot]
                    e1 = p.op("scalar", lambda e, po=po, eb=eb, occ=occ: e.activation(
                        eb[:], po[:], AF.Identity, bias=bo[:, occ:occ + 1]),
                        deps=[mm] + ([evo_tok[eslot]] if evo_tok[eslot][0] else []))
                    pp_tok[pslot] = e1
                    pslot = (pslot + 1) % 4
                    st = p.dma("sync", g["bufA"].ap()[b, occ, :, tt * 512:(tt + 1) * 512],
                               eb[:], deps=[e1])
                    evo_tok[eslot] = st
                    eslot = 1 - eslot
            ctxb_tok = op_last[-1]


def emit_l1router(p, nc, g):
    with contextlib.ExitStack() as ph:
        psb = lambda name, shape, dt=F32: ph.enter_context(nc.sbuf_tensor(_nm(name), shape, dt))
        psum = lambda name, shape, dt=F32: ph.enter_context(nc.psum_tensor(_nm(name), shape, dt))
        xs = [psb(f"r_x{i}", [128, CC * 512]) for i in range(2)]
        hb = psb("r_h", [128, CC * 512], BF16)
        sq = psb("r_sq", [128, 512])
        sq2 = psb("r_sq2", [128, 512])
        stM = psb("r_stM", [1, 512]); stV = psb("r_stV", [1, 512]); stT = psb("r_stT", [1, 512])
        gcol = psb("r_g", [128, CC]); becol = psb("r_be", [128, CC])
        wrb = psb("r_wr", [128, CC * 8], BF16)
        brs = psb("r_br", [8, 1])
        lgF = psb("r_lg", [8, 512]); eF = psb("r_eF", [8, 512])
        r1 = psb("r_r1", [1, 512]); p0r = psb("r_p0", [1, 512])
        onec = psb("r_one", [128, 1]); one1 = psb("r_one1", [1, 128])
        pS = psum("r_pS", [1, 512]); pQ = psum("r_pQ", [1, 512])
        pB = psum("r_pB", [128, 512]); pM = psum("r_pM", [128, 512])
        pL = psum("r_pL", [128, 512])
        t_on = p.op("vector", lambda e: e.memset(onec[:], 1.0))
        t_on1 = p.op("vector", lambda e: e.memset(one1[:], 1.0))
        wl = [p.dma("sync", gcol[:], g["l1_g"].ap()),
              p.dma("sync", becol[:], g["l1_be"].ap()),
              p.dma("sync", brs[:], g["l1_br"].ap())]
        for cc in range(CC):
            wl.append(p.dma("gpsimd", wrb[:, cc * 8:(cc + 1) * 8], g["l1_wr"].ap()[cc]))
        epsc, epst = p.constcol(1e-5, rows=1)
        xs_tok = [(None, 0)] * 2
        hb_tok = (None, 0); sq_tok = [(None, 0)] * 2; st_tok = (None, 0)
        pS_tok = (None, 0); pB_tok = (None, 0); pM_tok = (None, 0)
        pL_tok = (None, 0); lg_tok = (None, 0); e_tok = (None, 0)
        r1_tok = (None, 0); p0_tok = (None, 0)
        bB_prev = (None, 0); bM_prev = (None, 0)
        slot = 0
        for b in range(BC):
            for tt in range(NT):
                x = xs[slot]
                loads = []
                for cc in range(CC):
                    dep = [xs_tok[slot]] if xs_tok[slot][0] else []
                    loads.append(p.dma("sync", x[:, ts(cc, 512)],
                                       g["bufX6"].ap()[b, cc, :, tt * 512:(tt + 1) * 512], deps=dep))
                mmS = None
                for cc in range(CC):
                    dep = [loads[cc], t_on] + ([pS_tok] if pS_tok[0] else [])
                    mmS = p.op("tensor", lambda e, cc=cc, x=x: e.matmul(
                        pS[:], onec[:], x[:, ts(cc, 512)],
                        start=(cc == 0), stop=(cc == CC - 1)), deps=dep, inc=(cc == CC - 1))
                mmQ = None
                for cc in range(CC):
                    sqx = sq2 if cc % 2 else sq
                    dep = [loads[cc]] + ([sq_tok[cc % 2]] if sq_tok[cc % 2][0] else [])
                    tq = p.op("scalar", lambda e, cc=cc, x=x, sqx=sqx: e.activation(
                        sqx[:], x[:, ts(cc, 512)], AF.Square), deps=dep)
                    mmQ = p.op("tensor", lambda e, cc=cc, sqx=sqx: e.matmul(
                        pQ[:], onec[:], sqx[:], start=(cc == 0), stop=(cc == CC - 1)),
                        deps=[tq, mmS])
                    sq_tok[cc % 2] = mmQ
                dep0 = [mmS, mmQ] + ([st_tok] if st_tok[0] else [])
                s0 = p.op("vector", lambda e: e.tensor_scalar(stM[:], pS[:], 1.0 / D, None, ALU.mult), deps=dep0)
                s1 = p.op("vector", lambda e: e.tensor_scalar(stV[:], pQ[:], 1.0 / D, None, ALU.mult),
                          deps=[mmQ, s0] + ([bB_prev] if bB_prev[0] else []))
                pS_tok = s1
                s2 = p.op("vector", lambda e: e.tensor_mul(stT[:], stM[:], stM[:]), deps=[s0])
                s3 = p.op("vector", lambda e: e.tensor_sub(stV[:], stV[:], stT[:]), deps=[s1, s2])
                s4 = p.op("scalar", lambda e: e.activation(stV[:], stV[:], AF.Ln, bias=epsc), deps=[s3, epst])
                s5 = p.op("scalar", lambda e: e.activation(stV[:], stV[:], AF.Exp, scale=-0.5), deps=[s4])
                s6 = p.op("vector", lambda e: e.tensor_mul(stT[:], stM[:], stV[:]),
                          deps=[s5, s2] + ([bM_prev] if bM_prev[0] else []))
                st_tok = s6
                bB = p.op("tensor", lambda e: e.matmul(pB[:], one1[:], stV[:], start=True, stop=True),
                          deps=[s5, t_on1] + ([pB_tok] if pB_tok[0] else []))
                bM = p.op("tensor", lambda e: e.matmul(pM[:], one1[:], stT[:], start=True, stop=True),
                          deps=[s6, t_on1, bB] + ([pM_tok] if pM_tok[0] else []))
                bB_prev, bM_prev = bB, bM
                hs = []
                m2 = None
                for cc in range(CC):
                    sqx = sq2 if cc % 2 else sq
                    dep = [bB, loads[cc]] + ([hb_tok] if hb_tok[0] else [])
                    m1 = p.op("vector", lambda e, cc=cc, x=x, sqx=sqx: e.tensor_mul(
                        sqx[:], x[:, ts(cc, 512)], pB[:]), deps=dep + [sq_tok[cc % 2]])
                    m2 = p.op("vector", lambda e, cc=cc, sqx=sqx: e.tensor_sub(sqx[:], sqx[:], pM[:]),
                              deps=[m1, bM])
                    m3 = p.op("scalar", lambda e, cc=cc, sqx=sqx: e.activation(
                        hb[:, ts(cc, 512)], sqx[:], AF.Identity,
                        scale=gcol[:, cc:cc + 1], bias=becol[:, cc:cc + 1]), deps=[m2] + wl)
                    sq_tok[cc % 2] = m3
                    hs.append(m3)
                pM_tok = m2
                pB_tok = m2
                mmL = None
                for cc in range(CC):
                    dep = [hs[cc]] + wl + ([pL_tok] if pL_tok[0] and cc == 0 else [])
                    mmL = p.op("tensor", lambda e, cc=cc: e.matmul(
                        pL[0:8, :], wrb[:, cc * 8:(cc + 1) * 8], hb[:, ts(cc, 512)],
                        start=(cc == 0), stop=(cc == CC - 1)), deps=dep, inc=(cc == CC - 1))
                hb_tok = mmL
                lg = p.op("scalar", lambda e: e.activation(lgF[:], pL[0:8, :], AF.Identity,
                                                           bias=brs[:, 0:1]),
                          deps=[mmL] + wl + ([lg_tok] if lg_tok[0] else []))
                pL_tok = lg
                ee = p.op("scalar", lambda e: e.activation(eF[:], lgF[:], AF.Exp),
                          deps=[lg] + ([e_tok] if e_tok[0] else []))
                lg_tok = ee
                mmE = p.op("tensor", lambda e: e.matmul(pS[:], onec[0:8, :], eF[:],
                                                        start=True, stop=True),
                           deps=[ee, t_on, s1] + ([pS_tok] if pS_tok[0] else []))
                pS_tok = mmE
                rL = p.op("scalar", lambda e: e.activation(r1[:], pS[:], AF.Ln),
                          deps=[mmE] + ([r1_tok] if r1_tok[0] else []))
                rE = p.op("scalar", lambda e: e.activation(r1[:], r1[:], AF.Exp, scale=-1.0), deps=[rL])
                pS_tok = rL
                pv = p.op("vector", lambda e: e.tensor_mul(p0r[:], eF[0:1, :], r1[:]),
                          deps=[rE] + ([p0_tok] if p0_tok[0] else []))
                e_tok = pv
                r1_tok = pv
                p0_tok = p.dma("sync", g["bufP0"].ap()[b, :, tt * 512:(tt + 1) * 512], p0r[:], deps=[pv])
                xs_tok[slot] = p0_tok
                slot = 1 - slot


def _bisect_search(p, nc, psb, psum_, absap, ncols, n, k, thr_out, regs, cells, deps):
    tcell, tcol, c11, cnt, trash, onec, one1, pT, pc = (
        cells["tcell"], cells["tcol"], cells["c11"], cells["cnt"], cells["trash"],
        cells["onec"], cells["one1"], cells["pT"], cells["pc"])
    lo, hi, mid, cr = regs
    KBITS = int(np.float32(k).view(np.int32))
    p.raw("vector", lambda e: e.reg_mov(lo, 0), deps=deps)
    p.raw("vector", lambda e: e.reg_mov(hi, 0x7F800000))
    tcol_tok = cells.get("tcol_tok", (None, 0))
    c11_tok = cells.get("c11_tok", (None, 0))
    for it in range(31):
        def _mids(e, lo=lo, hi=hi, mid=mid):
            e.reg_add(mid, lo, hi)
            e.reg_alu(mid, mid, 1, ALU.arith_shift_right)
            return e.reg_save(cells["tcell"][0:1, 0:1], mid)
        tw = p.op("vector", _mids, deps=[cells.get("pT_tok", (None, 0))] if cells.get("pT_tok", (None, 0))[0] else [])
        bc = p.op("tensor", lambda e: e.matmul(pT[:, 0:1], one1[:], tcell[0:1, 0:1].bitcast(F32),
                                               start=True, stop=True),
                  deps=[tw] + ([tcol_tok] if tcol_tok[0] else []))
        cp = p.op("scalar", lambda e: e.copy(tcol[:], pT[:, 0:1]), deps=[bc])
        cells["pT_tok"] = cp
        half = ncols // 2
        if half >= 64:
            ct = p.op("vector", lambda e, half=half: e.tensor_scalar(
                trash[:, 0:half], absap[:, 0:half], tcol[:, 0:1], 0.0, ALU.is_le, ALU.add,
                accum_out=cnt[:]), deps=[cp] + ([c11_tok] if c11_tok[0] else []))
            ct2 = p.op("vector", lambda e, half=half, ncols=ncols: e.tensor_scalar(
                trash[:, half:ncols], absap[:, half:ncols], tcol[:, 0:1], 0.0, ALU.is_le, ALU.add,
                accum_out=cells["cnt2"][:]), deps=[cp] + ([c11_tok] if c11_tok[0] else []))
            tcol_tok = max(ct, ct2, key=lambda t: 0)
            rd = p.op("tensor", lambda e: e.matmul(pc[0:1, 0:1], onec[:], cnt[:],
                                                   start=True, stop=False), deps=[ct, ct2])
            rd = p.op("tensor", lambda e: e.matmul(pc[0:1, 0:1], onec[:], cells["cnt2"][:],
                                                   start=False, stop=True), deps=[])
            tcol_tok = rd
        else:
            ct = p.op("vector", lambda e: e.tensor_scalar(
                trash[:, 0:ncols], absap, tcol[:, 0:1], 0.0, ALU.is_le, ALU.add,
                accum_out=cnt[:]), deps=[cp] + ([c11_tok] if c11_tok[0] else []))
            tcol_tok = ct
            rd = p.op("tensor", lambda e: e.matmul(pc[0:1, 0:1], onec[:], cnt[:],
                                                   start=True, stop=True), deps=[ct])
        c2 = p.op("scalar", lambda e: e.copy(c11[0:1, 0:1], pc[0:1, 0:1]), deps=[rd])
        c11_tok = c2

        def _branch(e, lo=lo, hi=hi, mid=mid, cr=cr, KBITS=KBITS):
            e.reg_load(cr, cells["c11"][0:1, 0:1].bitcast(mybir.dt.int32))
            with e.If_lt(cr, KBITS):
                e.reg_add(lo, mid, 1)
            with e.Else():
                e.reg_mov(hi, mid)
            return e.nop()
        p.op("vector", _branch, deps=[c2])
    tw = p.op("vector", lambda e: e.reg_save(cells["tcell"][0:1, 0:1], hi))
    bc = p.op("tensor", lambda e: e.matmul(pT[:, 0:1], one1[:], tcell[0:1, 0:1].bitcast(F32),
                                           start=True, stop=True),
              deps=[tw] + ([tcol_tok] if tcol_tok[0] else []))
    fin = p.op("scalar", lambda e: e.copy(thr_out, pT[:, 0:1]), deps=[bc])
    cells["pT_tok"] = fin
    cells["tcol_tok"] = tcol_tok
    cells["c11_tok"] = c11_tok
    return fin


def emit_thresholds(p, nc, g):
    with contextlib.ExitStack() as ph:
        psb = lambda name, shape, dt=F32: ph.enter_context(nc.sbuf_tensor(_nm(name), shape, dt))
        psum_ = lambda name, shape, dt=F32: ph.enter_context(nc.psum_tensor(_nm(name), shape, dt))
        absw = psb("th_absw", [128, 32768])
        trash = psb("th_trash", [128, 32768], BF16)
        small = psb("th_small", [128, 48])
        b1a = psb("th_b1a", [128, 32])
        b2a = psb("th_b2a", [128, 8])
        f1 = psb("th_f1", [128, 32])
        f2 = psb("th_f2", [128, 32])
        tcell = psb("th_tcell", [1, 1], mybir.dt.int32)
        tcol = psb("th_tcol", [128, 1])
        c11 = psb("th_c11", [1, 1])
        cnt = psb("th_cnt", [128, 1])
        onec = psb("th_one", [128, 1])
        one1 = psb("th_one1", [1, 128])
        pT = psum_("th_pT", [128, 512])
        pc = psum_("th_pc", [128, 512])
        t_on = p.op("vector", lambda e: e.memset(onec[:], 1.0))
        t_on1 = p.op("vector", lambda e: e.memset(one1[:], 1.0))
        lo = nc.vector.alloc_register("th_lo")
        hi = nc.vector.alloc_register("th_hi")
        mid = nc.vector.alloc_register("th_mid")
        cr = nc.vector.alloc_register("th_cr")
        regs = (lo, hi, mid, cr)
        cnt2 = psb("th_cnt2", [128, 1])
        cells = dict(tcell=tcell, tcol=tcol, c11=c11, cnt=cnt, cnt2=cnt2, trash=trash,
                     onec=onec, one1=one1, pT=pT, pc=pc)
        nW, kW = 4194304, 209715
        lds = [p.dma("sync", absw[:, cc * 4096:(cc + 1) * 4096], g["l1_W1"].ap()[cc])
               for cc in range(CC)]
        ab = p.op("scalar", lambda e: e.activation(absw[:], absw[:], AF.Abs),
                  deps=lds + [t_on, t_on1])
        t1 = _bisect_search(p, nc, psb, psum_, absw[:], 32768, nW, kW,
                            small[:, 0:1], regs, cells, [ab])
        t2 = _bisect_search(p, nc, psb, psum_, absw[:], 32768, nW, nW - kW + 1,
                            small[:, 1:2], regs, cells, [t1])
        lds = []
        prev = t2
        for j in range(32):
            lds.append(p.dma("sync", absw[:, j * 1024:(j + 1) * 1024], g["l1_W2f"].ap()[j],
                             deps=[prev]))
        ab = p.op("scalar", lambda e: e.activation(absw[:], absw[:], AF.Abs), deps=lds)
        t3 = _bisect_search(p, nc, psb, psum_, absw[:], 32768, nW, kW,
                            small[:, 4:5], regs, cells, [ab])
        t4 = _bisect_search(p, nc, psb, psum_, absw[:], 32768, nW, nW - kW + 1,
                            small[:, 5:6], regs, cells, [t3])
        lb1 = p.dma("sync", b1a[:], g["l1_b1f"].ap())
        lb2 = p.dma("sync", b2a[:], g["l1_b2f"].ap())
        ab1 = p.op("scalar", lambda e: e.activation(b1a[:], b1a[:], AF.Abs), deps=[lb1])
        ab2 = p.op("scalar", lambda e: e.activation(b2a[:], b2a[:], AF.Abs), deps=[lb2])
        t5 = _bisect_search(p, nc, psb, psum_, b1a[:], 32, 4096, 204,
                            small[:, 2:3], regs, cells, [ab1, t4])
        t6 = _bisect_search(p, nc, psb, psum_, b1a[:], 32, 4096, 4096 - 204 + 1,
                            small[:, 3:4], regs, cells, [t5])
        t7 = _bisect_search(p, nc, psb, psum_, b2a[:], 8, 1024, 51,
                            small[:, 6:7], regs, cells, [t6, ab2])
        t8 = _bisect_search(p, nc, psb, psum_, b2a[:], 8, 1024, 1024 - 51 + 1,
                            small[:, 7:8], regs, cells, [t7])
        lb1v = p.dma("sync", f2[:, 0:32], g["l1_b1f"].ap(), deps=[t8])
        u1 = p.op("vector", lambda e: e.tensor_scalar(f1[:, 0:32], b1a[:], small[:, 2:3], None, ALU.is_le), deps=[t8])
        u2 = p.op("scalar", lambda e: e.activation(f1[:, 0:32], f1[:, 0:32], AF.Identity, scale=0.8, bias=1.0), deps=[u1])
        u3 = p.op("vector", lambda e: e.tensor_scalar(b1a[:], b1a[:], small[:, 3:4], None, ALU.is_ge), deps=[u2])
        u4 = p.op("vector", lambda e: e.scalar_tensor_tensor(f1[:, 0:32], b1a[:], -0.8, f1[:, 0:32], ALU.mult, ALU.add), deps=[u3])
        u5 = p.op("vector", lambda e: e.tensor_mul(small[:, 8:40], f2[:, 0:32], f1[:, 0:32]), deps=[u4, lb1v])
        lb2v = p.dma("sync", f2[:, 0:8], g["l1_b2f"].ap(), deps=[u5])
        w1 = p.op("vector", lambda e: e.tensor_scalar(f1[:, 0:8], b2a[:], small[:, 6:7], None, ALU.is_le), deps=[t8, u5])
        w2 = p.op("scalar", lambda e: e.activation(f1[:, 0:8], f1[:, 0:8], AF.Identity, scale=0.8, bias=1.0), deps=[w1])
        w3 = p.op("vector", lambda e: e.tensor_scalar(b2a[:], b2a[:], small[:, 7:8], None, ALU.is_ge), deps=[w2])
        w4 = p.op("vector", lambda e: e.scalar_tensor_tensor(f1[:, 0:8], b2a[:], -0.8, f1[:, 0:8], ALU.mult, ALU.add), deps=[w3])
        w5 = p.op("vector", lambda e: e.tensor_mul(small[:, 40:48], f2[:, 0:8], f1[:, 0:8]), deps=[w4, lb2v])
        p.dma("sync", g["bufSMALL"].ap(), small[:], deps=[u5, w5])
        nc.vector.free_register(lo); nc.vector.free_register(hi)
        nc.vector.free_register(mid); nc.vector.free_register(cr)


def emit_l1ffn(p, nc, g, path):
    with contextlib.ExitStack() as ph:
        psb = lambda name, shape, dt=F32: ph.enter_context(nc.sbuf_tensor(_nm(name), shape, dt))
        psum_ = lambda name, shape, dt=F32: ph.enter_context(nc.psum_tensor(_nm(name), shape, dt))
        W1b = psb("f_w1", [128, CC * 4096], BF16)
        W2b = psb("f_w2", [128, 32 * D], BF16)
        gbufL = psb("f_g", [128, 32 * 512], BF16)
        small = psb("f_small", [128, 48])
        xt = [psb(f"f_xt{i}", [128, CC * 512], BF16) for i in range(2)]
        ev = [psb(f"f_ev{i}", [128, 512]) for i in range(2)]
        one1 = psb("f_one1", [1, 128])
        b1cl = psb("f_b1", [128, 32])
        b2cl = psb("f_b2", [128, CC])
        us = [psb(f"f_u{i}", [128, 512]) for i in range(3)]
        q1s = [psb(f"f_q1{i}", [128, 512]) for i in range(2)]
        q3s = [psb(f"f_q3{i}", [128, 512]) for i in range(2)]
        sgs = [psb(f"f_sg{i}", [128, 512]) for i in range(2)]
        if path == 1:
            y0t = [psb(f"f_y0{i}", [128, 512], BF16) for i in range(2)]
            p0r = psb("f_p0", [1, 512])
            combP = psb("f_cp", [128, 512], BF16)
            wch = us[0]; fa = q1s[0]; ft = q3s[0]
        pF = [psum_(f"f_pF{i}", [128, 512]) for i in range(2)]
        pY = [psum_(f"f_pY{i}", [128, 512]) for i in range(2)]
        pP = psum_("f_pP", [128, 512])
        t_on1 = p.op("vector", lambda e: e.memset(one1[:], 1.0))
        wl = []
        tsm = p.dma("sync", small[:], g["bufSMALL"].ap())
        if path == 0:
            for cc in range(CC):
                wl.append(p.dma("gpsimd", W1b[:, cc * 4096:(cc + 1) * 4096], g["l1_W1"].ap()[cc]))
            for j in range(32):
                wl.append(p.dma("gpsimd", W2b[:, j * D:(j + 1) * D], g["l1_W2f"].ap()[j]))
            wl.append(p.dma("sync", b1cl[:, 0:32], g["l1_b1f"].ap()))
            wl.append(p.dma("sync", b2cl[:], g["l1_b2f"].ap()))
            bload = wl[-1]
        else:
            wprev = (None, 0)
            for (wname, wdst, nblk, blkw) in (("l1_W1", W1b, CC, 4096), ("l1_W2f", W2b, 32, D)):
                for ib in range(nblk):
                    nch = blkw // 512
                    for jc in range(nch):
                        dep = [wprev] if wprev[0] else []
                        lw = p.dma("sync", wch[:], g[wname].ap()[ib, :, jc * 512:(jc + 1) * 512],
                                   deps=dep)
                        tb = 0 if wname == "l1_W1" else 4
                        a1 = p.op("scalar", lambda e: e.activation(fa[:], wch[:], AF.Abs),
                                  deps=[lw, tsm])
                        a2 = p.op("vector", lambda e, tb=tb: e.tensor_scalar(
                            ft[:], fa[:], small[:, tb:tb + 1], None, ALU.is_le), deps=[a1])
                        a3 = p.op("scalar", lambda e: e.activation(ft[:], ft[:], AF.Identity,
                                                                   scale=0.8, bias=1.0), deps=[a2])
                        a4 = p.op("vector", lambda e, tb=tb: e.tensor_scalar(
                            fa[:], fa[:], small[:, tb + 1:tb + 2], None, ALU.is_ge), deps=[a3])
                        a5 = p.op("vector", lambda e: e.scalar_tensor_tensor(
                            ft[:], fa[:], -0.8, ft[:], ALU.mult, ALU.add), deps=[a4])
                        a6 = p.op("vector", lambda e, wdst=wdst, ib=ib, jc=jc, blkw=blkw: e.tensor_mul(
                            wdst[:, ib * blkw + jc * 512: ib * blkw + jc * 512 + 512],
                            wch[:], ft[:]), deps=[a5])
                        wprev = a6
                        wl.append(a6)
            bl1 = p.op("vector", lambda e: e.tensor_copy(b1cl[:, 0:32], small[:, 8:40]), deps=[tsm])
            bl2 = p.op("vector", lambda e: e.tensor_copy(b2cl[:], small[:, 40:48]), deps=[tsm])
            wl += [bl1, bl2]
            bload = bl2
        xt_tok = [(None, 0)] * 2
        u_tok = [(None, 0)] * 3; q1_tok = [(None, 0)] * 2
        q3_tok = [(None, 0)] * 2; sgm_tok = [(None, 0)] * 2
        gb_tok = (None, 0)
        pF_tok = [(None, 0)] * 2
        pY_tok = [(None, 0)] * 2
        pP_tok = (None, 0)
        ev_tok = [(None, 0)] * 2
        y0_tok = [(None, 0)] * 2
        p0_tok = (None, 0); cp_tok = (None, 0)
        xslot = 0; eslot = 0
        for b in range(BC):
            for tt in range(NT):
                x = xt[xslot]
                xls = []
                for cc in range(CC):
                    dep = [xt_tok[xslot]] if xt_tok[xslot][0] else []
                    xls.append(p.dma("gpsimd", x[:, ts(cc, 512)],
                                     g["bufX6"].ap()[b, cc, :, tt * 512:(tt + 1) * 512], deps=dep))
                if path == 1:
                    lp = p.dma("sync", p0r[:], g["bufP0"].ap()[b, :, tt * 512:(tt + 1) * 512],
                               deps=[p0_tok] if p0_tok[0] else [])
                    bcp = p.op("tensor", lambda e: e.matmul(pP[:], one1[:], p0r[:],
                                                            start=True, stop=True),
                               deps=[lp, t_on1] + ([pP_tok] if pP_tok[0] else []))
                    p0_tok = bcp
                    cpt = p.op("scalar", lambda e: e.copy(combP[:], pP[:]),
                               deps=[bcp] + ([cp_tok] if cp_tok[0] else []))
                    pP_tok = cpt
                glast = None
                for fc in range(32):
                    ci = fc % 2
                    gi = fc % 3
                    pf = pF[ci]
                    u = us[gi]; q1 = q1s[ci]; q3 = q3s[ci]; sgm = sgs[ci]
                    mmF = None
                    for icc in range(CC):
                        dep = xls + wl + ([pF_tok[ci]] if pF_tok[ci][0] and icc == 0 else [])
                        mmF = p.op("tensor", lambda e, pf=pf, icc=icc, fc=fc, x=x: e.matmul(
                            pf[:], W1b[:, icc * 4096 + fc * 128: icc * 4096 + fc * 128 + 128],
                            x[:, ts(icc, 512)], start=(icc == 0), stop=(icc == CC - 1)), deps=dep, inc=(icc == CC - 1))
                    uv = p.op("scalar", lambda e, pf=pf, fc=fc, u=u: e.activation(
                        u[:], pf[:], AF.Identity, bias=b1cl[:, fc:fc + 1]),
                        deps=[mmF, bload] + ([u_tok[gi]] if u_tok[gi][0] else []))
                    pF_tok[ci] = uv
                    q1v = p.op("gpsimd", lambda e, u=u, q1=q1: e.tensor_mul(q1[:], u[:], u[:]),
                               deps=[uv] + ([q1_tok[ci]] if q1_tok[ci][0] else []))
                    q2v = p.op("scalar", lambda e, q1=q1: e.activation(q1[:], q1[:], AF.Identity,
                                                                scale=0.044715, bias=1.0), deps=[q1v])
                    q3v = p.op("vector", lambda e, u=u, q1=q1, q3=q3: e.tensor_mul(q3[:], u[:], q1[:]),
                               deps=[q2v] + ([q3_tok[ci]] if q3_tok[ci][0] else []))
                    q1_tok[ci] = q3v
                    sgv = p.op("scalar", lambda e, q3=q3, sgm=sgm: e.activation(sgm[:], q3[:], AF.Sigmoid,
                                                                scale=1.5957691216057308),
                               deps=[q3v] + ([sgm_tok[ci]] if sgm_tok[ci][0] else []))
                    q3_tok[ci] = sgv
                    gv = p.op("vector", lambda e, fc=fc, u=u, sgm=sgm: e.tensor_mul(
                        gbufL[:, fc * 512:(fc + 1) * 512], u[:], sgm[:]),
                        deps=[sgv] + ([gb_tok] if gb_tok[0] and fc < 2 else []))
                    u_tok[gi] = gv
                    sgm_tok[ci] = gv
                    glast = gv
                ylast = None
                for occ in range(CC):
                    ci = occ % 2
                    py = pY[ci]
                    mmY = None
                    for fc in range(32):
                        dep = [glast] + wl + ([pY_tok[ci]] if pY_tok[ci][0] and fc == 0 else [])
                        mmY = p.op("tensor", lambda e, py=py, fc=fc, occ=occ: e.matmul(
                            py[:], W2b[:, fc * D + occ * 128: fc * D + occ * 128 + 128],
                            gbufL[:, fc * 512:(fc + 1) * 512],
                            start=(fc == 0), stop=(fc == 31)), deps=dep, inc=(fc == 31))
                    eb = ev[eslot]
                    yv = p.op("scalar", lambda e, py=py, eb=eb, occ=occ: e.activation(
                        eb[:], py[:], AF.Identity, bias=b2cl[:, occ:occ + 1]),
                        deps=[mmY, bload] + ([ev_tok[eslot]] if ev_tok[eslot][0] else []))
                    pY_tok[ci] = yv
                    if path == 0:
                        st = p.dma("sync", g["bufY0"].ap()[b, occ, :, tt * 512:(tt + 1) * 512],
                                   eb[:], deps=[yv])
                        ev_tok[eslot] = st
                        ylast = st
                    else:
                        yb = y0t[eslot]
                        ly = p.dma("gpsimd", yb[:], g["bufY0"].ap()[b, occ, :, tt * 512:(tt + 1) * 512],
                                   deps=[y0_tok[eslot]] if y0_tok[eslot][0] else [])
                        d1 = p.op("vector", lambda e, yb=yb, eb=eb: e.tensor_sub(yb[:], yb[:], eb[:]),
                                  deps=[ly, yv])
                        d2 = p.op("vector", lambda e, yb=yb: e.tensor_mul(yb[:], yb[:], combP[:]),
                                  deps=[d1, cpt])
                        cp_tok = d2
                        d3 = p.op("gpsimd", lambda e, yb=yb, eb=eb: e.tensor_add(eb[:], eb[:], yb[:]),
                                  deps=[d2])
                        st = p.dma("sync", g["bufA"].ap()[b, occ, :, tt * 512:(tt + 1) * 512],
                                   eb[:], deps=[d3])
                        ev_tok[eslot] = st
                        y0_tok[eslot] = d3
                        ylast = st
                    eslot = 1 - eslot
                gb_tok = ylast
                xt_tok[xslot] = ylast
                xslot = 1 - xslot


# version 08
# speedup vs baseline: 1.0551x; 1.0551x over previous
import contextlib
import math
import numpy as np
import concourse.bass as bass
import concourse.mybir as mybir
from concourse.bass_utils import run_bass_kernel_spmd

F32 = mybir.dt.float32
BF16 = mybir.dt.bfloat16
AF = mybir.ActivationFunctionType
ALU = mybir.AluOpType
AX = mybir.AxisListType.X

B_TOT, S, D = 32, 2048, 1024
NCORE = 8
BC = B_TOT // NCORE
CC = D // 128
NT = S // 512
H, HD = 16, 64
SG = S + 8

ENGS = ("sync", "scalar", "vector", "tensor", "gpsimd")


class Prog:

    def __init__(self, nc):
        self.nc = nc
        self.es = contextlib.ExitStack()
        self.thunks = {e: [] for e in ENGS}
        self.semh = {}
        self.cnt = {}
        self.water = {e: {} for e in ENGS}
        for e in ENGS:
            self.mksem("p_" + e)
        self.mksem("d_sy")
        self.mksem("d_gp")
        self.mksem("f_sy")
        self.mksem("f_gp")
        self.mksem("bar")
        self.fences = {"sync": [], "gpsimd": []}

    def init_consts(self, buf):
        self.constbuf = buf
        self.cmap = {}

    def constcol(self, val, rows=128):
        if val not in self.cmap:
            i = len(self.cmap)
            tok = self.op("vector", lambda e, i=i, val=val: e.memset(
                self.constbuf[:, i:i + 1], val))
            self.cmap[val] = (i, tok)
        i, tok = self.cmap[val]
        return self.constbuf[0:rows, i:i + 1], tok

    def mksem(self, name):
        self.semh[name] = self.es.enter_context(self.nc.semaphore(name))
        self.cnt[name] = 0
        return name

    def _fence(self, q):
        dsem = "d_sy" if q == "sync" else "d_gp"
        fsem = "f_sy" if q == "sync" else "f_gp"
        total = self.cnt[dsem]
        self._wait_direct(q, dsem, total)
        self.cnt[fsem] += 1
        fh = self.semh[fsem]
        self.thunks[q].append(lambda e, fh=fh: e.nop().then_inc(fh, 1))
        self.fences[q].append((total, self.cnt[fsem]))
        return (fsem, self.cnt[fsem])

    def _wait_direct(self, eng, sem, val):
        if val <= 0:
            return
        if self.water[eng].get(sem, 0) >= val:
            return
        self.water[eng][sem] = val
        h = self.semh[sem]
        self.thunks[eng].append(lambda e, h=h, v=val: e.wait_ge(h, v))

    def _wait(self, eng, sem, val):
        if val <= 0:
            return
        if sem in ("d_sy", "d_gp"):
            q = "sync" if sem == "d_sy" else "gpsimd"
            if eng == q:
                self._wait_direct(eng, sem, self.cnt[sem])
                return
            fsem = "f_sy" if sem == "d_sy" else "f_gp"
            for (tot, fc) in self.fences[q]:
                if tot >= val:
                    self._wait_direct(eng, fsem, fc)
                    return
            ftok = self._fence(q)
            self._wait_direct(eng, ftok[0], ftok[1])
            return
        self._wait_direct(eng, sem, val)

    def op(self, eng, fn, deps=(), inc=True):
        for (s, v) in deps:
            self._wait(eng, s, v)
        sem = "p_" + eng
        if not inc:
            self.thunks[eng].append(lambda e, fn=fn: fn(e))
            return (sem, self.cnt[sem])
        self.cnt[sem] += 1
        tok = (sem, self.cnt[sem])
        h = self.semh[sem]
        self.thunks[eng].append(lambda e, fn=fn, h=h: fn(e).then_inc(h, 1))
        return tok

    def raw(self, eng, fn, deps=()):
        for (s, v) in deps:
            self._wait(eng, s, v)
        self.thunks[eng].append(fn)
        return None

    def dma(self, eng, out, in_, deps=(), **kw):
        for (s, v) in deps:
            self._wait(eng, s, v)
        sem = "d_sy" if eng == "sync" else "d_gp"
        assert eng in ("sync", "gpsimd")
        self.cnt[sem] += 16
        tok = (sem, self.cnt[sem])
        h = self.semh[sem]
        self.thunks[eng].append(
            lambda e, out=out, in_=in_, h=h, kw=kw: e.dma_start(out, in_, **kw).then_inc(h, 16))
        return tok

    def barrier(self):
        bh = self.semh["bar"]
        target = self.cnt["bar"] + len(ENGS)
        for e in ENGS:
            self._wait(e, "p_" + e, self.cnt["p_" + e])
            if e == "sync":
                self._wait(e, "d_sy", self.cnt["d_sy"])
            if e == "gpsimd":
                self._wait(e, "d_gp", self.cnt["d_gp"])
            self.thunks[e].append(lambda en, bh=bh: en.nop().then_inc(bh, 1))
        self.cnt["bar"] = target
        for e in ENGS:
            self._wait(e, "bar", target)

    def finalize(self):
        nc = self.nc
        with nc.Block() as block:
            @block.sync
            def _(e):
                for t in self.thunks["sync"]:
                    t(e)

            @block.scalar
            def _(e):
                for t in self.thunks["scalar"]:
                    t(e)

            @block.vector
            def _(e):
                for t in self.thunks["vector"]:
                    t(e)

            @block.tensor
            def _(e):
                for t in self.thunks["tensor"]:
                    t(e)

            @block.gpsimd
            def _(e):
                for t in self.thunks["gpsimd"]:
                    t(e)
        self.es.close()


_NSEQ = [0]


def _nm(name):
    _NSEQ[0] += 1
    return f"{name}_{_NSEQ[0]}"


def ts(i, n):
    return slice(i * n, (i + 1) * n)



def build_program(dbg_external=(), upto=99):
    nc = bass.Bass("TRN2", target_bir_lowering=False)
    g = {}

    def din(name, shape, dt=F32):
        g[name] = nc.dram_tensor(name, shape, dt, kind="ExternalInput")
        return g[name]

    def dscr(name, shape, dt=F32):
        kind = "ExternalOutput" if name in dbg_external else "Internal"
        g[name] = nc.dram_tensor(name, shape, dt, kind=kind)
        return g[name]

    din("x_fm", [BC, CC, 128, S])
    din("l0_wkT", [CC, 128, D]); din("l0_wvT", [CC, 128, D])
    din("l0_wrT", [CC, 128, D]); din("l0_woT", [CC, 128, D])
    din("l0_par", [128, 40])
    din("l0_n1w", [128, CC]); din("l0_n2w", [128, CC])
    din("qkvT", [CC, 128, 3072])
    din("l0_qkvb", [128, 24])
    din("l0_outwT", [CC, 128, D]); din("l0_outb", [128, CC])
    din("gateT", [CC, 128, 8])
    din("w1T", [8, CC, 128, 2048]); din("w2T", [8, CC, 128, 2048])
    din("w3T", [8, 16, 128, D])
    din("l0_b1", [8, 128, 16]); din("l0_b2", [8, 128, 16]); din("l0_b3", [8, 128, CC])
    din("rope_cos", [128, S]); din("rope_sin", [128, S])
    din("ropeRT", [128, 128])
    din("onehot8", [8, 8 * 128])
    din("l1_wq", [CC, 128, D]); din("l1_wk", [CC, 128, D])
    din("l1_wv", [CC, 128, D]); din("l1_wo", [CC, 128, D])
    din("l1_bo", [128, CC])
    din("alibias", [128, 1024])
    din("causalm", [128, 896])
    din("l1_W1", [CC, 128, 4096]); din("l1_b1f", [128, 32])
    din("l1_W2f", [32, 128, D]); din("l1_b2f", [128, CC])
    din("l1_g", [128, CC]); din("l1_be", [128, CC])
    din("l1_wr", [CC, 128, 8]); din("l1_br", [8, 1])

    dscr("xbf", [BC, CC, 128, SG], BF16)
    dscr("bufK", [BC, CC, 128, S])
    dscr("bufV", [BC, CC, 128, S])
    dscr("bufSR", [BC, CC, 128, S])
    dscr("bufA", [BC, CC, 128, S])
    dscr("bufX1", [BC, CC, 128, S])
    dscr("bufXN", [BC, CC, 128, S], BF16)
    dscr("bufX2", [BC, CC, 128, S])
    dscr("bufX3", [BC, CC, 128, S])
    dscr("bufX4", [BC, CC, 128, S])
    dscr("qkvTB", [CC, 128, 3072], BF16)
    dscr("outwTB", [CC, 128, D], BF16)
    dscr("gateTB", [CC, 128, 8], BF16)
    dscr("w1TB", [8, CC, 128, 2048], BF16)
    dscr("w2TB", [8, CC, 128, 2048], BF16)
    dscr("w3TB", [8, 16, 128, D], BF16)
    dscr("bufX6", [BC, CC, 128, S])
    dscr("bufY0", [BC, CC, 128, S])
    dscr("bufP0", [BC, 1, S])
    dscr("bufSMALL", [128, 48])
    out_fm = nc.dram_tensor("out_fm", [BC, CC, 128, S], F32, kind="ExternalOutput")
    g["out_fm"] = out_fm

    p = Prog(nc)
    sb = lambda name, shape, dt=F32: p.es.enter_context(nc.sbuf_tensor(name, shape, dt))

    p.init_consts(sb("constbuf", [128, 16]))
    par = sb("par", [128, 40])
    dpar = sb("dpar", [128, 64])
    t_par = p.dma("sync", par[:], g["l0_par"].ap())

    t0 = p.op("vector", lambda e: e.tensor_copy(dpar[:, 0:8], par[:, 16:24]), deps=[t_par])
    t1 = p.op("scalar", lambda e: e.activation(dpar[:, 8:16], par[:, 16:24], AF.Identity, scale=-1.0, bias=1.0), deps=[t_par])
    t2 = p.op("vector", lambda e: e.tensor_copy(dpar[:, 16:24], par[:, 24:32]), deps=[t_par])
    t3 = p.op("scalar", lambda e: e.activation(dpar[:, 24:32], par[:, 24:32], AF.Identity, scale=-1.0, bias=1.0), deps=[t_par])
    t4 = p.op("vector", lambda e: e.tensor_copy(dpar[:, 32:40], par[:, 32:40]), deps=[t_par])
    t5 = p.op("scalar", lambda e: e.activation(dpar[:, 40:48], par[:, 32:40], AF.Identity, scale=-1.0, bias=1.0), deps=[t_par])
    t6a = p.op("scalar", lambda e: e.activation(dpar[:, 48:56], par[:, 0:8], AF.Exp), deps=[t_par])
    t6 = p.op("scalar", lambda e: e.activation(dpar[:, 48:56], dpar[:, 48:56], AF.Exp, scale=-1.0), deps=[t6a])
    t7 = p.op("scalar", lambda e: e.activation(dpar[:, 56:64], par[:, 8:16], AF.Exp), deps=[t_par])
    dpar_tok = [t0, t1, t2, t3, t4, t5, t6, t7]

    def dcol(f, cc):
        return dpar[:, f * 8 + cc: f * 8 + cc + 1]

    zg = sb("zg", [128, 8], BF16)
    tz = p.op("vector", lambda e: e.memset(zg[:], 0.0))
    ph0_toks = []
    for b in range(BC):
        for cc in range(CC):
            ph0_toks.append(p.dma("gpsimd", g["xbf"].ap()[b, cc, :, 8:SG],
                                  g["x_fm"].ap()[b, cc, :, :]))
            ph0_toks.append(p.dma("gpsimd", g["xbf"].ap()[b, cc, :, 0:8], zg[:], deps=[tz]))
    p.barrier()

    with contextlib.ExitStack() as ph:
        psb = lambda name, shape, dt=F32: ph.enter_context(nc.sbuf_tensor(_nm(name), shape, dt))
        psum = lambda name, shape, dt=F32: ph.enter_context(nc.psum_tensor(_nm(name), shape, dt))
        wsc = {}
        wsc_toks = []
        wtmp = psb("wtmp", [128, D])
        tok_prev = dpar_tok
        for pi, (wname, fA, fB, nA, nB) in enumerate([
                ("l0_wkT", 0, 1, "wk1", "wk2"),
                ("l0_wvT", 2, 3, "wv1", "wv2"),
                ("l0_wrT", 4, 5, "wr1", "wr2")]):
            wsc[nA] = psb(nA, [128, CC * D], BF16)
            wsc[nB] = psb(nB, [128, CC * D], BF16)
            for cc in range(CC):
                tl = p.dma("sync", wtmp[:], g[wname].ap()[cc], deps=tok_prev)
                ta = p.op("vector", lambda e, nA=nA, cc=cc, fA=fA: e.tensor_scalar(
                    wsc[nA][:, ts(cc, D)], wtmp[:], dcol(fA, cc), None, ALU.mult), deps=[tl] + dpar_tok)
                tb = p.op("vector", lambda e, nB=nB, cc=cc, fB=fB: e.tensor_scalar(
                    wsc[nB][:, ts(cc, D)], wtmp[:], dcol(fB, cc), None, ALU.mult), deps=[tl] + dpar_tok)
                tok_prev = [ta, tb]
                wsc_toks.extend([ta, tb])

        xw = [psb(f"xw{i}", [128, CC * 513], BF16) for i in range(2)]
        xw_tok = [(None, 0), (None, 0)]
        pk = [psum(f"pk{i}", [128, 512]) for i in range(4)]
        pk_tok = [(None, 0)] * 4
        ev = [psb(f"ev{i}", [128, 512]) for i in range(4)]
        ev_tok = [(None, 0)] * 4
        slot = 0
        pslot = 0
        dst = {"k": g["bufK"], "v": g["bufV"], "r": g["bufSR"]}
        wpair = {"k": ("wk1", "wk2"), "v": ("wv1", "wv2"), "r": ("wr1", "wr2")}
        for b in range(BC):
            for tt in range(NT):
                xs = xw[slot]
                loads = []
                for cc in range(CC):
                    dep = [xw_tok[slot]] if xw_tok[slot][0] else []
                    loads.append(p.dma("sync", xs[:, cc * 513:(cc + 1) * 513],
                                       g["xbf"].ap()[b, cc, :, 8 + tt * 512 - 1: 8 + tt * 512 + 512],
                                       deps=dep))
                consumed = []
                for proj in ("k", "v", "r"):
                    w1n, w2n = wpair[proj]
                    for occ in range(CC):
                        pp = pk[pslot]
                        mm_tok = None
                        for icc in range(CC):
                            lhs1 = wsc[w1n][:, icc * D + occ * 128: icc * D + occ * 128 + 128]
                            lhs2 = wsc[w2n][:, icc * D + occ * 128: icc * D + occ * 128 + 128]
                            rhs1 = xs[:, icc * 513 + 1: icc * 513 + 513]
                            rhs2 = xs[:, icc * 513 + 0: icc * 513 + 512]
                            deps = loads + wsc_toks + ([pk_tok[pslot]] if pk_tok[pslot][0] else [])
                            mm_tok = p.op("tensor", lambda e, pp=pp, lhs1=lhs1, rhs1=rhs1, icc=icc: e.matmul(
                                pp[:], lhs1, rhs1, start=(icc == 0), stop=False), deps=deps, inc=False)
                            mm_tok = p.op("tensor", lambda e, pp=pp, lhs2=lhs2, rhs2=rhs2, icc=icc: e.matmul(
                                pp[:], lhs2, rhs2, start=False, stop=(icc == CC - 1)), deps=[], inc=(icc == CC - 1))
                        consumed.append(mm_tok)
                        ebuf = ev[pslot]
                        edep = [mm_tok] + ([ev_tok[pslot]] if ev_tok[pslot][0] else [])
                        if proj == "r":
                            et = p.op("scalar", lambda e, ebuf=ebuf, pp=pp: e.activation(
                                ebuf[:], pp[:], AF.Sigmoid), deps=edep)
                        else:
                            et = p.op("scalar", lambda e, ebuf=ebuf, pp=pp: e.copy(ebuf[:], pp[:]), deps=edep)
                        pk_tok[pslot] = et
                        ev_tok[pslot] = p.dma("sync", dst[proj].ap()[b, occ, :, tt * 512:(tt + 1) * 512],
                                              ebuf[:], deps=[et])
                        pslot = (pslot + 1) % 4
                xw_tok[slot] = consumed[-1]
                slot = 1 - slot
    p.barrier()

    for cc in range(CC):
        p.dma("gpsimd", g["qkvTB"].ap()[cc], g["qkvT"].ap()[cc])
        p.dma("gpsimd", g["outwTB"].ap()[cc], g["l0_outwT"].ap()[cc])
        p.dma("gpsimd", g["gateTB"].ap()[cc], g["gateT"].ap()[cc])
    for e in range(8):
        for cc in range(CC):
            p.dma("gpsimd", g["w1TB"].ap()[e, cc], g["w1T"].ap()[e, cc])
            p.dma("gpsimd", g["w2TB"].ap()[e, cc], g["w2T"].ap()[e, cc])
        for hc in range(16):
            p.dma("gpsimd", g["w3TB"].ap()[e, hc], g["w3T"].ap()[e, hc])

    with contextlib.ExitStack() as ph:
        psb = lambda name, shape, dt=F32: ph.enter_context(nc.sbuf_tensor(_nm(name), shape, dt))
        psum = lambda name, shape, dt=F32: ph.enter_context(nc.psum_tensor(_nm(name), shape, dt))
        ones = psb("ones2", [128, 2048])
        t_ones = p.op("vector", lambda e: e.memset(ones[:], 1.0))
        ewsl = psb("ewsl", [128, 2048])
        slK = psb("slK", [128, 2048]); slV = psb("slV", [128, 2048])
        slEK = psb("slEK", [128, 2048]); slEKV = psb("slEKV", [128, 2048])
        slP = psb("slP", [128, 2049]); slQ = psb("slQ", [128, 2049])
        slN = psb("slN", [128, 2048]); slD = psb("slD", [128, 2048])
        slR = psb("slR", [128, 2048]); slSR = psb("slSR", [128, 2048])
        rwkv = psb("rwkv", [128, CC * 2048], BF16)
        woTb = psb("woTb", [128, CC * D], BF16)
        wtm = psb("wtm", [128, D])
        tg1 = p.op("vector", lambda e: e.memset(slP[:, 0:1], 0.0))
        tg2 = p.op("vector", lambda e: e.memset(slQ[:, 0:1], 0.0))
        wo_tok = []
        tprev = (None, 0)
        for cc in range(CC):
            tl = p.dma("sync", wtm[:], g["l0_woT"].ap()[cc], deps=[tprev] if tprev[0] else [])
            tc = p.op("scalar", lambda e, cc=cc: e.copy(woTb[:, ts(cc, D)], wtm[:]), deps=[tl])
            tprev = tc
            wo_tok.append(tc)

        lt = {k: (None, 0) for k in ["K", "V", "EK", "EKV", "P", "Q", "N", "D", "R", "SR", "ew"]}
        rw_tok = {}
        pmm = [psum(f"po{i}", [128, 512]) for i in range(2)]
        pmm_tok = [(None, 0)] * 2
        evo = [psb(f"evo{i}", [128, 512]) for i in range(2)]
        evo_tok = [(None, 0)] * 2
        mslot = 0
        for b in range(BC):
            rwkv_consumed = rw_tok.get(b - 1, [])
            for cc in range(CC):
                dd = lambda k: [lt[k]] if lt[k][0] else []
                tK = p.dma("sync", slK[:], g["bufK"].ap()[b, cc], deps=dd("K"))
                tV = p.dma("sync", slV[:], g["bufV"].ap()[b, cc], deps=dd("V"))
                tR = p.dma("sync", slSR[:], g["bufSR"].ap()[b, cc], deps=dd("SR"))
                tew = p.op("scalar", lambda e, cc=cc: e.activation(
                    ewsl[:], ones[:, 0:2048], AF.Identity, scale=dcol(6, cc)),
                    deps=dd("ew") + dpar_tok + [t_ones])
                tEK = p.op("scalar", lambda e: e.activation(slEK[:], slK[:], AF.Exp),
                           deps=[tK] + dd("EK"))
                lt["K"] = tEK
                tEKV = p.op("vector", lambda e: e.tensor_mul(slEKV[:], slEK[:], slV[:]),
                            deps=[tEK, tV] + dd("EKV"))
                lt["V"] = tEKV
                tQ = p.op("vector", lambda e: e.tensor_tensor_scan(
                    slQ[:, 1:2049], ewsl[:], slEK[:], 0.0, ALU.mult, ALU.add),
                    deps=[tew, tEK, tg2] + dd("Q"))
                tP = p.op("vector", lambda e: e.tensor_tensor_scan(
                    slP[:, 1:2049], ewsl[:], slEKV[:], 0.0, ALU.mult, ALU.add),
                    deps=[tew, tEKV, tg1] + dd("P"))
                lt["ew"] = tP
                tD = p.op("vector", lambda e, cc=cc: e.scalar_tensor_tensor(
                    slD[:], slEK[:], dcol(7, cc), slQ[:, 0:2048], ALU.mult, ALU.add),
                    deps=[tQ] + dd("D"))
                lt["EK"] = tD
                lt["Q"] = tD
                tN = p.op("vector", lambda e, cc=cc: e.scalar_tensor_tensor(
                    slN[:], slEKV[:], dcol(7, cc), slP[:, 0:2048], ALU.mult, ALU.add),
                    deps=[tP] + dd("N"))
                lt["EKV"] = tN
                lt["P"] = tN
                tL = p.op("scalar", lambda e: e.activation(slR[:], slD[:], AF.Ln),
                          deps=[tD] + dd("R"))
                tRc = p.op("scalar", lambda e: e.activation(slR[:], slR[:], AF.Exp, scale=-1.0),
                           deps=[tL])
                lt["D"] = tRc
                tY = p.op("vector", lambda e: e.tensor_mul(slN[:], slN[:], slR[:]),
                          deps=[tRc, tN])
                lt["R"] = tY
                lt["N"] = tY
                trw = p.op("vector", lambda e, cc=cc: e.tensor_mul(
                    rwkv[:, ts(cc, 2048)], slN[:], slSR[:]),
                    deps=[tY, tR] + rwkv_consumed)
                lt["SR"] = trw
                lt["N"] = trw
                rw_tok.setdefault(b, []).append(trw)
            consumed = []
            for tt in range(NT):
                for occ in range(CC):
                    pp = pmm[mslot]
                    mm = None
                    for icc in range(CC):
                        deps = [rw_tok[b][icc]] + wo_tok + ([pmm_tok[mslot]] if pmm_tok[mslot][0] else [])
                        mm = p.op("tensor", lambda e, pp=pp, icc=icc, occ=occ, tt=tt: e.matmul(
                            pp[:], woTb[:, icc * D + occ * 128: icc * D + occ * 128 + 128],
                            rwkv[:, icc * 2048 + tt * 512: icc * 2048 + tt * 512 + 512],
                            start=(icc == 0), stop=(icc == CC - 1)), deps=deps, inc=(icc == CC - 1))
                    consumed.append(mm)
                    eb = evo[mslot]
                    edep = [mm] + ([evo_tok[mslot]] if evo_tok[mslot][0] else [])
                    et = p.op("scalar", lambda e, eb=eb, pp=pp: e.copy(eb[:], pp[:]), deps=edep)
                    pmm_tok[mslot] = et
                    evo_tok[mslot] = p.dma("sync", g["bufA"].ap()[b, occ, :, tt * 512:(tt + 1) * 512],
                                           eb[:], deps=[et])
                    mslot = 1 - mslot
            rw_tok[b] = consumed
    p.barrier()

    emit_resid_ln(p, nc, g, src1="x_fm", src2="bufA", dst="bufX1", dst2=None)
    p.barrier()
    cur = "bufX1"

    if upto >= 4:
        emit_rms(p, nc, g, src="bufX1", dst="bufXN", wname="l0_n1w", eps=1e-6)
        p.barrier()
        emit_l0attn(p, nc, g)
        p.barrier()
        cur = "bufX2"
    if upto >= 5:
        emit_rms(p, nc, g, src="bufX2", dst="bufXN", wname="l0_n2w", eps=1e-6)
        p.barrier()
        emit_moe(p, nc, g)
        p.barrier()
        emit_resid_ln(p, nc, g, src1="bufX1", src2="bufX3", dst="bufX4")
        p.barrier()
        cur = "bufX4"
    if upto >= 7:
        emit_l1attn(p, nc, g)
        p.barrier()
        emit_resid_ln(p, nc, g, src1="bufX4", src2="bufA", dst="bufX6")
        p.barrier()
        cur = "bufX6"
    if upto >= 9:
        emit_l1router(p, nc, g)
        p.barrier()
        emit_thresholds(p, nc, g)
        p.barrier()
        emit_l1ffn(p, nc, g, path=0)
        p.barrier()
        emit_l1ffn(p, nc, g, path=1)
        p.barrier()
        emit_resid_ln(p, nc, g, src1="bufX6", src2="bufA", dst="out_fm")
        p.barrier()
        cur = None

    if cur is None:
        p.finalize()
        return nc
    with contextlib.ExitStack() as ph:
        psb = lambda name, shape, dt=F32: ph.enter_context(nc.sbuf_tensor(_nm(name), shape, dt))
        cp = psb("cpb", [128, 2048])
        tprev = (None, 0)
        for b in range(BC):
            for cc in range(CC):
                tl = p.dma("sync", cp[:], g[cur].ap()[b, cc], deps=[tprev] if tprev[0] else [])
                tprev = p.dma("sync", g["out_fm"].ap()[b, cc], cp[:], deps=[tl])
    p.barrier()
    p.finalize()
    return nc


def emit_resid_ln(p, nc, g, src1, src2, dst, dst2=None, eps=1e-5):
    with contextlib.ExitStack() as ph:
        psb = lambda name, shape, dt=F32: ph.enter_context(nc.sbuf_tensor(_nm(name), shape, dt))
        psum = lambda name, shape, dt=F32: ph.enter_context(nc.psum_tensor(_nm(name), shape, dt))
        xs = [psb(f"ln_x{i}", [128, CC * 512]) for i in range(2)]
        ys = [psb(f"ln_y{i}", [128, CC * 512]) for i in range(2)]
        sqs = [psb(f"ln_sq{i}", [128, 512]) for i in range(2)]
        stM = psb("ln_stM", [1, 512])
        stV = psb("ln_stV", [1, 512])
        stT = psb("ln_stT", [1, 512])
        onec = psb("ln_one", [128, 1])
        one1 = psb("ln_one1", [1, 128])
        t_on = p.op("vector", lambda e: e.memset(onec[:], 1.0))
        t_on1 = p.op("vector", lambda e: e.memset(one1[:], 1.0))
        pS = psum("ln_pS", [1, 512])
        pQ = psum("ln_pQ", [1, 512])
        pB = psum("ln_pB", [128, 512])
        pM = psum("ln_pM", [128, 512])
        xs_tok = [(None, 0)] * 2
        ys_tok = [(None, 0)] * 2
        sq_tok = [(None, 0)] * 2
        stat_tok = (None, 0)
        pS_tok = (None, 0)
        pB_tok = (None, 0)
        pM_tok = (None, 0)
        bB_prev = (None, 0)
        bM_prev = (None, 0)
        slot = 0
        for b in range(BC):
            for tt in range(NT):
                x = xs[slot]
                y = ys[slot]
                loads = []
                adds = []
                for cc in range(CC):
                    dep = [xs_tok[slot]] if xs_tok[slot][0] else []
                    l1 = p.dma("sync", x[:, ts(cc, 512)],
                               g[src1].ap()[b, cc, :, tt * 512:(tt + 1) * 512], deps=dep)
                    if src2 is not None:
                        dep2 = [ys_tok[slot]] if ys_tok[slot][0] else []
                        l2 = p.dma("sync", y[:, ts(cc, 512)],
                                   g[src2].ap()[b, cc, :, tt * 512:(tt + 1) * 512], deps=dep2)
                        ta = p.op("vector", lambda e, cc=cc, x=x, y=y: e.tensor_add(
                            x[:, ts(cc, 512)], x[:, ts(cc, 512)], y[:, ts(cc, 512)]),
                            deps=[l1, l2])
                        if dst2 is not None:
                            adds.append(p.dma("sync", g[dst2].ap()[b, cc, :, tt * 512:(tt + 1) * 512],
                                        x[:, ts(cc, 512)], deps=[ta]))
                    else:
                        ta = l1
                    loads.append(ta)
                mmS = None
                for cc in range(CC):
                    dep = [loads[cc], t_on] + ([pS_tok] if pS_tok[0] else [])
                    mmS = p.op("tensor", lambda e, cc=cc, x=x: e.matmul(
                        pS[:], onec[:], x[:, ts(cc, 512)],
                        start=(cc == 0), stop=(cc == CC - 1)), deps=dep, inc=(cc == CC - 1))
                mmQ = None
                for cc in range(CC):
                    sq = sqs[cc % 2]
                    dep = [loads[cc]] + ([sq_tok[cc % 2]] if sq_tok[cc % 2][0] else [])
                    tq = p.op("scalar", lambda e, cc=cc, x=x, sq=sq: e.activation(
                        sq[:], x[:, ts(cc, 512)], AF.Square), deps=dep)
                    mmQ = p.op("tensor", lambda e, cc=cc, sq=sq: e.matmul(
                        pQ[:], onec[:], sq[:],
                        start=(cc == 0), stop=(cc == CC - 1)), deps=[tq, mmS])
                    sq_tok[cc % 2] = mmQ
                dep0 = [mmS, mmQ] + ([stat_tok] if stat_tok[0] else [])
                s0 = p.op("vector", lambda e: e.tensor_scalar(
                    stM[:], pS[:], 1.0 / D, None, ALU.mult), deps=dep0)
                s1 = p.op("vector", lambda e: e.tensor_scalar(
                    stV[:], pQ[:], 1.0 / D, None, ALU.mult),
                    deps=[mmQ, s0] + ([bB_prev] if bB_prev[0] else []))
                pS_tok = s1
                s2 = p.op("vector", lambda e: e.tensor_mul(stT[:], stM[:], stM[:]),
                          deps=[s0])
                s3 = p.op("vector", lambda e: e.tensor_sub(stV[:], stV[:], stT[:]),
                          deps=[s1, s2])
                epsc, epst = p.constcol(eps, rows=1)
                s4 = p.op("scalar", lambda e, epsc=epsc: e.activation(stV[:], stV[:], AF.Ln, bias=epsc),
                          deps=[s3, epst])
                s5 = p.op("scalar", lambda e: e.activation(stV[:], stV[:], AF.Exp, scale=-0.5),
                          deps=[s4])
                s6 = p.op("vector", lambda e: e.tensor_mul(stT[:], stM[:], stV[:]),
                          deps=[s5, s2] + ([bM_prev] if bM_prev[0] else []))
                stat_tok = s6
                depB = [s5, t_on1] + ([pB_tok] if pB_tok[0] else [])
                bB = p.op("tensor", lambda e: e.matmul(pB[:], one1[:], stV[:],
                                                       start=True, stop=True), deps=depB)
                depM = [s6, t_on1, bB] + ([pM_tok] if pM_tok[0] else [])
                bM = p.op("tensor", lambda e: e.matmul(pM[:], one1[:], stT[:],
                                                       start=True, stop=True), deps=depM)
                bB_prev, bM_prev = bB, bM
                outs = []
                for cc in range(CC):
                    dep = [bB, loads[cc]] + ([ys_tok[slot]] if ys_tok[slot][0] else [])
                    m1 = p.op("vector", lambda e, cc=cc, x=x, y=y: e.tensor_mul(
                        y[:, ts(cc, 512)], x[:, ts(cc, 512)], pB[:]), deps=dep)
                    m2 = p.op("vector", lambda e, cc=cc, y=y: e.tensor_sub(
                        y[:, ts(cc, 512)], y[:, ts(cc, 512)], pM[:]), deps=[m1, bM])
                    st = p.dma("sync", g[dst].ap()[b, cc, :, tt * 512:(tt + 1) * 512],
                               y[:, ts(cc, 512)], deps=[m2])
                    outs.append(st)
                    pM_tok = m2
                    pB_tok = m2
                xs_tok[slot] = outs[-1]
                ys_tok[slot] = outs[-1]
                slot = 1 - slot



def _prep_inputs(inputs):
    f = np.float32
    w = {}
    x = np.ascontiguousarray(inputs["L0_attn_x"], dtype=f)
    x_fm = np.ascontiguousarray(x.transpose(0, 2, 1)).reshape(B_TOT, CC, 128, S)

    def T2(a):
        return np.ascontiguousarray(np.ascontiguousarray(a.T, dtype=f).reshape(CC, 128, -1))

    def NAT(a):
        return np.ascontiguousarray(np.ascontiguousarray(a, dtype=f).reshape(CC, 128, -1))

    w["l0_wkT"] = T2(inputs["L0_attn_Wk"])
    w["l0_wvT"] = T2(inputs["L0_attn_Wv"])
    w["l0_wrT"] = T2(inputs["L0_attn_Wr"])
    w["l0_woT"] = T2(inputs["L0_attn_Wo"])
    par = np.zeros((128, 40), f)
    for i, nm in enumerate(["L0_attn_time_decay", "L0_attn_time_first",
                            "L0_attn_time_mix_k", "L0_attn_time_mix_v", "L0_attn_time_mix_r"]):
        v = np.asarray(inputs[nm], f).reshape(-1)
        par[:, i * 8:(i + 1) * 8] = v.reshape(CC, 128).T
    w["l0_par"] = par
    w["l0_n1w"] = np.ascontiguousarray(np.asarray(inputs["L0_ffn_norm1_w"], f).reshape(CC, 128).T)
    w["l0_n2w"] = np.ascontiguousarray(np.asarray(inputs["L0_ffn_norm2_w"], f).reshape(CC, 128).T)
    w["qkvT"] = np.ascontiguousarray(np.asarray(inputs["L0_ffn_qkv_w"], f).T.reshape(CC, 128, 3072))
    w["l0_qkvb"] = np.ascontiguousarray(np.asarray(inputs["L0_ffn_qkv_b"], f).reshape(24, 128).T)
    w["l0_outwT"] = T2(inputs["L0_ffn_out_w"])
    w["l0_outb"] = np.ascontiguousarray(np.asarray(inputs["L0_ffn_out_b"], f).reshape(CC, 128).T)
    w["gateT"] = np.ascontiguousarray(np.asarray(inputs["L0_ffn_gate_w"], f).T.reshape(CC, 128, 8))
    w["w1T"] = np.ascontiguousarray(np.asarray(inputs["L0_ffn_w1"], f).transpose(0, 2, 1).reshape(8, CC, 128, 2048))
    w["w2T"] = np.ascontiguousarray(np.asarray(inputs["L0_ffn_w2"], f).transpose(0, 2, 1).reshape(8, CC, 128, 2048))
    w["w3T"] = np.ascontiguousarray(np.asarray(inputs["L0_ffn_w3"], f).transpose(0, 2, 1).reshape(8, 16, 128, D))
    w["l0_b1"] = np.ascontiguousarray(np.asarray(inputs["L0_ffn_b1"], f).reshape(8, 16, 128).transpose(0, 2, 1))
    w["l0_b2"] = np.ascontiguousarray(np.asarray(inputs["L0_ffn_b2"], f).reshape(8, 16, 128).transpose(0, 2, 1))
    w["l0_b3"] = np.ascontiguousarray(np.asarray(inputs["L0_ffn_b3"], f).reshape(8, CC, 128).transpose(0, 2, 1))
    inv_freq = 1.0 / 10000.0 ** (np.arange(0, 32, dtype=np.float64) * 2.0 / 64.0)
    tgrid = np.arange(S, dtype=np.float64)
    ang = tgrid[None, :] * inv_freq[:, None]
    cos32 = np.cos(ang).astype(f); sin32 = np.sin(ang).astype(f)
    w["rope_cos"] = np.ascontiguousarray(np.tile(cos32, (4, 1)))
    w["rope_sin"] = np.ascontiguousarray(np.tile(sin32, (4, 1)))
    RT = np.zeros((128, 128), f)
    for blk in (0, 64):
        for i in range(32):
            RT[blk + i, blk + i + 32] = 1.0
            RT[blk + i + 32, blk + i] = -1.0
    w["ropeRT"] = RT
    oh = np.zeros((8, 8 * 128), f)
    for e_ in range(8):
        oh[e_, e_ * 128:(e_ + 1) * 128] = 1.0
    w["onehot8"] = oh
    w["l1_wq"] = NAT(inputs["L1_attn_Wq"]); w["l1_wk"] = NAT(inputs["L1_attn_Wk"])
    w["l1_wv"] = NAT(inputs["L1_attn_Wv"]); w["l1_wo"] = NAT(inputs["L1_attn_Wo"])
    w["l1_bo"] = np.ascontiguousarray(np.asarray(inputs["L1_attn_bo"], f).reshape(CC, 128).T)
    def get_slopes(n):
        def pow2(n_):
            start = 2.0 ** (-(2.0 ** (-(math.log2(n_) - 3))))
            return [start * start ** i for i in range(n_)]
        if math.log2(n).is_integer():
            return pow2(n)
        c = 2 ** math.floor(math.log2(n))
        return pow2(c) + get_slopes(2 * c)[0::2][:n - c]
    slopes = np.asarray(get_slopes(16), np.float64)
    ab = np.zeros((128, 1024), np.float64)
    prow = np.arange(128)
    for h in range(16):
        a8 = slopes[h] / 8.0
        for kc in range(16):
            for j in range(4):
                ab[:, h * 64 + kc * 4 + j] = a8 * (128 * kc + prow) - 20.0 - a8 * 512.0 * j
    w["alibias"] = ab.astype(f)
    cm = np.zeros((128, 896), f)
    jj = np.arange(896) - 384
    cm[:, :] = (jj[None, :] >= prow[:, None]).astype(f)
    w["causalm"] = cm
    w["l1_W1"] = np.ascontiguousarray(np.asarray(inputs["L1_ffn_W1"], f).reshape(CC, 128, 4096))
    w["l1_b1f"] = np.ascontiguousarray(np.asarray(inputs["L1_ffn_b1"], f).reshape(32, 128).T)
    w["l1_W2f"] = np.ascontiguousarray(np.asarray(inputs["L1_ffn_W2"], f).reshape(32, 128, D))
    w["l1_b2f"] = np.ascontiguousarray(np.asarray(inputs["L1_ffn_b2"], f).reshape(CC, 128).T)
    w["l1_g"] = np.ascontiguousarray(np.asarray(inputs["L1_ffn_ln_gamma"], f).reshape(CC, 128).T)
    w["l1_be"] = np.ascontiguousarray(np.asarray(inputs["L1_ffn_ln_beta"], f).reshape(CC, 128).T)
    w["l1_wr"] = np.ascontiguousarray(np.asarray(inputs["L1_ffn_Wr"], f).reshape(CC, 128, 8))
    w["l1_br"] = np.ascontiguousarray(np.asarray(inputs["L1_ffn_br"], f).reshape(8, 1))
    return x_fm, w


def kernel(**inputs):
    x_fm, w = _prep_inputs(inputs)
    nc = build_program()
    in_maps = []
    for c in range(NCORE):
        m = {"x_fm": np.ascontiguousarray(x_fm[c * BC:(c + 1) * BC])}
        m.update(w)
        in_maps.append(m)
    res = run_bass_kernel_spmd(nc, in_maps, core_ids=list(range(NCORE)))
    shards = [r["out_fm"] for r in res.results]
    out = np.concatenate(shards, axis=0)
    out = out.reshape(B_TOT, D, S).transpose(0, 2, 1)
    return np.ascontiguousarray(out, dtype=np.float32)


def emit_rms(p, nc, g, src, dst, wname, eps):
    with contextlib.ExitStack() as ph:
        psb = lambda name, shape, dt=F32: ph.enter_context(nc.sbuf_tensor(_nm(name), shape, dt))
        psum = lambda name, shape, dt=F32: ph.enter_context(nc.psum_tensor(_nm(name), shape, dt))
        xs = [psb(f"rm_x{i}", [128, CC * 512]) for i in range(2)]
        ys = [psb(f"rm_y{i}", [128, CC * 512], BF16) for i in range(2)]
        sqs = [psb(f"rm_sq{i}", [128, 512]) for i in range(2)]
        stV = psb("rm_stV", [1, 512])
        wsb = psb("rm_w", [128, CC])
        onec = psb("rm_one", [128, 1])
        one1 = psb("rm_one1", [1, 128])
        pQ = psum("rm_pQ", [1, 512])
        pB = psum("rm_pB", [128, 512])
        t_on = p.op("vector", lambda e: e.memset(onec[:], 1.0))
        t_on1 = p.op("vector", lambda e: e.memset(one1[:], 1.0))
        t_w = p.dma("sync", wsb[:], g[wname].ap())
        epsc, epst = p.constcol(eps, rows=1)
        xs_tok = [(None, 0)] * 2
        ys_tok = [(None, 0)] * 2
        sq_tok = [(None, 0)] * 2
        stV_tok = (None, 0)
        pQ_tok = (None, 0)
        pB_tok = (None, 0)
        slot = 0
        for b in range(BC):
            for tt in range(NT):
                x = xs[slot]; y = ys[slot]
                loads = []
                for cc in range(CC):
                    dep = [xs_tok[slot]] if xs_tok[slot][0] else []
                    loads.append(p.dma("sync", x[:, ts(cc, 512)],
                                       g[src].ap()[b, cc, :, tt * 512:(tt + 1) * 512], deps=dep))
                mmQ = None
                for cc in range(CC):
                    sq = sqs[cc % 2]
                    dep = [loads[cc]] + ([sq_tok[cc % 2]] if sq_tok[cc % 2][0] else [])
                    tq = p.op("scalar", lambda e, cc=cc, x=x, sq=sq: e.activation(
                        sq[:], x[:, ts(cc, 512)], AF.Square), deps=dep)
                    dep2 = [tq, t_on] + ([pQ_tok] if pQ_tok[0] and cc == 0 else [])
                    mmQ = p.op("tensor", lambda e, cc=cc, sq=sq: e.matmul(
                        pQ[:], onec[:], sq[:],
                        start=(cc == 0), stop=(cc == CC - 1)), deps=dep2)
                    sq_tok[cc % 2] = mmQ
                dep0 = [mmQ] + ([stV_tok] if stV_tok[0] else [])
                s1 = p.op("vector", lambda e: e.tensor_scalar(
                    stV[:], pQ[:], 1.0 / D, None, ALU.mult), deps=dep0)
                pQ_tok = s1
                s4 = p.op("scalar", lambda e: e.activation(stV[:], stV[:], AF.Ln, bias=epsc),
                          deps=[s1, epst])
                s5 = p.op("scalar", lambda e: e.activation(stV[:], stV[:], AF.Exp, scale=-0.5),
                          deps=[s4])
                depB = [s5, t_on1] + ([pB_tok] if pB_tok[0] else [])
                bB = p.op("tensor", lambda e: e.matmul(pB[:], one1[:], stV[:],
                                                       start=True, stop=True), deps=depB)
                stV_tok = bB
                outs = []
                m2 = None
                for cc in range(CC):
                    dep = [bB, loads[cc], t_w] + ([ys_tok[slot]] if ys_tok[slot][0] else [])
                    m2 = p.op("vector", lambda e, cc=cc, x=x, y=y: e.scalar_tensor_tensor(
                        y[:, ts(cc, 512)], x[:, ts(cc, 512)], wsb[:, cc:cc + 1], pB[:],
                        ALU.mult, ALU.mult), deps=dep)
                    outs.append(p.dma("sync", g[dst].ap()[b, cc, :, tt * 512:(tt + 1) * 512],
                                      y[:, ts(cc, 512)], deps=[m2]))
                pB_tok = m2
                xs_tok[slot] = outs[-1]
                ys_tok[slot] = outs[-1]
                slot = 1 - slot


def emit_l0attn(p, nc, g):
    neg20, t20 = p.constcol(-20.0)
    with contextlib.ExitStack() as ph:
        psb = lambda name, shape, dt=F32: ph.enter_context(nc.sbuf_tensor(_nm(name), shape, dt))
        psum = lambda name, shape, dt=F32: ph.enter_context(nc.psum_tensor(_nm(name), shape, dt))
        qkvTb = psb("a4_qkvT", [128, CC * 3072], BF16)
        outwTb = psb("a4_outwT", [128, CC * D], BF16)
        cosb = psb("a4_cos", [128, S], BF16)
        sinb = psb("a4_sin", [128, S], BF16)
        rtb = psb("a4_rt", [128, 128], BF16)
        qkvb = psb("a4_qb", [128, 24])
        qb8 = psb("a4_qb8", [128, CC])
        outb = psb("a4_ob", [128, CC])
        xnb = psb("a4_xn", [128, CC * S], BF16)
        vh = psb("a4_vh", [128, 16 * 16 * 65], BF16)
        ctxb = psb("a4_ctx", [128, CC * S], BF16)
        qh2 = psb("a4_qh", [128, S], BF16)
        kh2 = psb("a4_kh", [128, S], BF16)
        tmpA = psb("a4_tA", [128, 512])
        tmpB = psb("a4_tB", [128, 512])
        tmpctx = psb("a4_tc", [64, 512])
        rec = psb("a4_rec", [1, 512])
        one1 = psb("a4_one1", [1, 128])
        pts = [psb(f"a4_pt{i}", [128, 1024], BF16) for i in range(2)]
        evo = [psb(f"a4_ev{i}", [128, 512]) for i in range(2)]
        xres = [psb(f"a4_xr{i}", [128, 512]) for i in range(2)]
        ppq = [psum(f"a4_ppq{i}", [128, 1024]) for i in range(2)]
        pctxs = [psum(f"a4_pctx{i}", [128, 512]) for i in range(2)]
        pbc = psum("a4_pbc", [128, 512])
        wl = []
        for cc in range(CC):
            wl.append(p.dma("sync", qkvTb[:, cc * 3072:(cc + 1) * 3072], g["qkvTB"].ap()[cc]))
            wl.append(p.dma("sync", outwTb[:, ts(cc, D)], g["outwTB"].ap()[cc]))
        wl.append(p.dma("gpsimd", cosb[:], g["rope_cos"].ap()))
        wl.append(p.dma("gpsimd", sinb[:], g["rope_sin"].ap()))
        wl.append(p.dma("gpsimd", rtb[:], g["ropeRT"].ap()))
        wl.append(p.dma("sync", qkvb[:], g["l0_qkvb"].ap()))
        wl.append(p.dma("sync", outb[:], g["l0_outb"].ap()))
        t_q8 = p.op("scalar", lambda e: e.activation(qb8[:], qkvb[:, 0:8], AF.Identity, scale=0.125),
                    deps=wl)
        t_on1 = p.op("vector", lambda e: e.memset(one1[:], 1.0))
        t_vones = p.op("vector", lambda e: e.memset(vh[:, 64:16640:65], 1.0))

        pp_tok = [(None, 0)] * 4
        pt_tok = [(None, 0)] * 2
        evo_tok = [(None, 0)] * 2
        xres_tok = [(None, 0)] * 2
        xnb_tok = (None, 0)
        vh_tok = (None, 0)
        ctxb_tok = (None, 0)
        qh_tok = (None, 0)
        kh_tok = (None, 0)
        tA_tok = (None, 0); tB_tok = (None, 0)
        tctx_tok = (None, 0); rec_tok = (None, 0)
        pctx_tok = [(None, 0)] * 2; pbc_tok = (None, 0)
        def pbank(j):
            return ppq[j // 2][:, (j % 2) * 512:(j % 2) * 512 + 512]
        pslot = 0; qslot = 0; ptslot = 0; eslot = 0
        for b in range(BC):
            xl = []
            for cc in range(CC):
                dep = [xnb_tok] if xnb_tok[0] else []
                xl.append(p.dma("sync", xnb[:, cc * S:(cc + 1) * S],
                                g["bufXN"].ap()[b, cc], deps=dep))
            vtoks = []
            for kc in range(16):
                for vc in range(2):
                    pv = pbank(pslot)
                    mm = None
                    for icc in range(CC):
                        dep = xl + wl + ([pp_tok[pslot]] if pp_tok[pslot][0] else [])
                        mm = p.op("tensor", lambda e, pv=pv, icc=icc, kc=kc, vc=vc: e.matmul(
                            pv, xnb[:, icc * S + kc * 128: icc * S + kc * 128 + 128],
                            qkvTb[:, icc * 3072 + 2048 + vc * 512: icc * 3072 + 2048 + vc * 512 + 512],
                            start=(icc == 0), stop=(icc == CC - 1)), deps=dep, inc=(icc == CC - 1))
                    last = None
                    for j in range(8):
                        h = vc * 8 + j
                        dep = [mm, t_vones] + ([vh_tok] if vh_tok[0] else [])
                        last = p.op("vector", lambda e, pv=pv, j=j, kc=kc, h=h: e.tensor_copy(
                            vh[:, (kc * 16 + h) * 65: (kc * 16 + h) * 65 + 64],
                            pv[:, j * 64: j * 64 + 64]), deps=dep)
                        vtoks.append(last)
                    pp_tok[pslot] = last
                    pslot = (pslot + 1) % 4
            av_last = []
            qk_last = []
            qk_cons = (None, 0)
            for occ in range(CC):
                for (which, colbase) in (("q", 0), ("k", 1024)):
                    buf = qh2 if which == "q" else kh2
                    btok = qh_tok if which == "q" else kh_tok
                    for tt in range(NT):
                        pq = pbank(pslot)
                        mm = None
                        for icc in range(CC):
                            dep = xl + wl + ([pp_tok[pslot]] if pp_tok[pslot][0] else []) + ([btok] if btok[0] else [])
                            mm = p.op("tensor", lambda e, pq=pq, icc=icc, occ=occ, tt=tt, colbase=colbase: e.matmul(
                                pq, qkvTb[:, icc * 3072 + colbase + occ * 128: icc * 3072 + colbase + occ * 128 + 128],
                                xnb[:, icc * S + tt * 512: icc * S + tt * 512 + 512],
                                start=(icc == 0), stop=(icc == CC - 1)), deps=dep, inc=(icc == CC - 1))
                        qk_last.append(mm)
                        wardep = [qk_cons] if qk_cons[0] else []
                        if which == "q":
                            et = p.op("scalar", lambda e, pq=pq, buf=buf, tt=tt, occ=occ: e.activation(
                                buf[:, tt * 512:(tt + 1) * 512], pq, AF.Identity,
                                scale=0.125, bias=qb8[:, occ:occ + 1]), deps=[mm, t_q8] + wardep)
                        else:
                            et = p.op("scalar", lambda e, pq=pq, buf=buf, tt=tt, occ=occ: e.activation(
                                buf[:, tt * 512:(tt + 1) * 512], pq, AF.Identity,
                                bias=qkvb[:, 8 + occ:9 + occ]), deps=[mm] + wardep)
                        pp_tok[pslot] = et
                        pslot = (pslot + 1) % 4
                        pr = pbank(pslot)
                        dep = [et] + ([pp_tok[pslot]] if pp_tok[pslot][0] else [])
                        mmr = p.op("tensor", lambda e, pr=pr, buf=buf, tt=tt: e.matmul(
                            pr, rtb[:], buf[:, tt * 512:(tt + 1) * 512],
                            start=True, stop=True), deps=dep + wl)
                        a1t = p.op("vector", lambda e, buf=buf, tt=tt: e.tensor_mul(
                            tmpA[:], buf[:, tt * 512:(tt + 1) * 512], cosb[:, tt * 512:(tt + 1) * 512]),
                            deps=[et] + ([tA_tok] if tA_tok[0] else []))
                        b1t = p.op("vector", lambda e, pr=pr, tt=tt: e.tensor_mul(
                            tmpB[:], pr, sinb[:, tt * 512:(tt + 1) * 512]),
                            deps=[mmr] + ([tB_tok] if tB_tok[0] else []))
                        c1t = p.op("vector", lambda e, buf=buf, tt=tt: e.tensor_add(
                            buf[:, tt * 512:(tt + 1) * 512], tmpA[:], tmpB[:]),
                            deps=[a1t, b1t])
                        tA_tok = c1t; tB_tok = c1t
                        pp_tok[pslot] = c1t
                        pslot = (pslot + 1) % 4
                        if which == "q":
                            qh_tok = c1t
                        else:
                            kh_tok = c1t
                for qt in range(NT):
                    avmm = [None, None]
                    sq_pend = {}

                    def emit_S(kc, qt=qt, occ=occ):
                        nonlocal qslot, qk_cons
                        pq = ppq[qslot]
                        smm = None
                        for hh in range(2):
                            hr = 64 * hh
                            wdep = [t for t in (pp_tok[2 * qslot], pp_tok[2 * qslot + 1]) if t[0]] if hh == 0 else []
                            dep = [qh_tok, kh_tok] + wdep
                            smm = p.op("tensor", lambda e, pq=pq, hr=hr, kc=kc, qt=qt, hh=hh: e.matmul(
                                pq[0:128, hh * 512:(hh + 1) * 512],
                                kh2[hr:hr + 64, kc * 128:(kc + 1) * 128],
                                qh2[hr:hr + 64, qt * 512:(qt + 1) * 512],
                                start=True, stop=True), deps=dep, inc=(hh == 1))
                        qk_cons = smm
                        sq_pend[kc] = (pq, qslot, smm)
                        qslot = 1 - qslot

                    emit_S(0)
                    for kc in range(16):
                        if kc + 1 < 16:
                            emit_S(kc + 1)
                        pq, psl, smm = sq_pend.pop(kc)
                        pt = pts[ptslot]
                        dep = [smm, t20] + ([pt_tok[ptslot]] if pt_tok[ptslot][0] else [])
                        ex = p.op("scalar", lambda e, pq=pq, pt=pt: e.activation(
                            pt[:], pq[:], AF.Exp, bias=neg20), deps=dep)
                        pp_tok[2 * psl] = ex
                        pp_tok[2 * psl + 1] = ex
                        for hh in range(2):
                            h = 2 * occ + hh
                            dep = [ex] + vtoks[-1:] + ([pctx_tok[hh]] if pctx_tok[hh][0] and kc == 0 else [])
                            avmm[hh] = p.op("tensor", lambda e, pt=pt, kc=kc, h=h, hh=hh: e.matmul(
                                pctxs[hh][0:65, :], vh[:, (kc * 16 + h) * 65: (kc * 16 + h) * 65 + 65],
                                pt[:, hh * 512:(hh + 1) * 512], start=(kc == 0), stop=(kc == 15)),
                                deps=dep, inc=(kc == 15))
                        pt_tok[ptslot] = avmm[1]
                        ptslot = 1 - ptslot
                    for hh in range(2):
                        h = 2 * occ + hh
                        hr = 64 * hh
                        pctx = pctxs[hh]
                        av_last.append(avmm[hh])
                        tL = p.op("scalar", lambda e, pctx=pctx: e.activation(rec[:], pctx[64:65, :], AF.Ln),
                                  deps=[avmm[hh]] + ([rec_tok] if rec_tok[0] else []))
                        tE = p.op("scalar", lambda e: e.activation(rec[:], rec[:], AF.Exp, scale=-1.0),
                                  deps=[tL])
                        dep = [tE, t_on1] + ([pbc_tok] if pbc_tok[0] else [])
                        bc = p.op("tensor", lambda e: e.matmul(pbc[0:64, :], one1[:, 0:64], rec[:],
                                                               start=True, stop=True), deps=dep)
                        tc1 = p.op("scalar", lambda e, pctx=pctx: e.copy(tmpctx[:], pctx[0:64, :]),
                                   deps=[avmm[hh], tE] + ([tctx_tok] if tctx_tok[0] else []))
                        pctx_tok[hh] = tc1
                        mu = p.op("vector", lambda e: e.tensor_mul(tmpctx[:], tmpctx[:], pbc[0:64, :]),
                                  deps=[tc1, bc])
                        pbc_tok = mu
                        rec_tok = bc
                        dep = [mu] + ([ctxb_tok] if ctxb_tok[0] else [])
                        cb = p.op("scalar", lambda e, h=h, hr=hr, qt=qt, occ=occ: e.activation(
                            ctxb[hr:hr + 64, occ * S + qt * 512: occ * S + qt * 512 + 512],
                            tmpctx[:], AF.Identity,
                            bias=qkvb[hr:hr + 64, 16 + occ:17 + occ]), deps=dep)
                        tctx_tok = cb
                        av_last.append(cb)
            xnb_tok = qk_last[-1]
            vh_tok = av_last[-1]
            op_last = []
            for occ in range(CC):
                for tt in range(NT):
                    po = pbank(pslot)
                    mm = None
                    for icc in range(CC):
                        dep = av_last[-1:] + wl + ([pp_tok[pslot]] if pp_tok[pslot][0] else [])
                        mm = p.op("tensor", lambda e, po=po, icc=icc, occ=occ, tt=tt: e.matmul(
                            po, outwTb[:, icc * D + occ * 128: icc * D + occ * 128 + 128],
                            ctxb[:, icc * S + tt * 512: icc * S + tt * 512 + 512],
                            start=(icc == 0), stop=(icc == CC - 1)), deps=dep, inc=(icc == CC - 1))
                    op_last.append(mm)
                    eb = evo[eslot]; xr = xres[eslot]
                    lx = p.dma("sync", xr[:], g["bufX1"].ap()[b, occ, :, tt * 512:(tt + 1) * 512],
                               deps=[xres_tok[eslot]] if xres_tok[eslot][0] else [])
                    e1 = p.op("scalar", lambda e, po=po, eb=eb, occ=occ: e.activation(
                        eb[:], po, AF.Identity, bias=outb[:, occ:occ + 1]),
                        deps=[mm] + ([evo_tok[eslot]] if evo_tok[eslot][0] else []))
                    pp_tok[pslot] = e1
                    pslot = (pslot + 1) % 4
                    e2 = p.op("vector", lambda e, eb=eb, xr=xr: e.tensor_add(eb[:], eb[:], xr[:]),
                              deps=[e1, lx])
                    st = p.dma("sync", g["bufX2"].ap()[b, occ, :, tt * 512:(tt + 1) * 512],
                               eb[:], deps=[e2])
                    evo_tok[eslot] = st
                    xres_tok[eslot] = e2
                    eslot = 1 - eslot
            ctxb_tok = op_last[-1]


def emit_moe(p, nc, g):
    with contextlib.ExitStack() as ph:
        psb = lambda name, shape, dt=F32: ph.enter_context(nc.sbuf_tensor(_nm(name), shape, dt))
        psum = lambda name, shape, dt=F32: ph.enter_context(nc.psum_tensor(_nm(name), shape, dt))
        W1s = psb("m_w1", [128, CC * 2048], BF16)
        W2s = psb("m_w2", [128, CC * 2048], BF16)
        W3s = psb("m_w3", [128, 16 * D], BF16)
        acc = psb("m_acc", [128, 2 * CC * 512])
        gbufs = [psb(f"m_g{i}", [128, 16 * 512], BF16) for i in range(2)]
        s1s = [psb(f"m_s1{i}", [128, 512]) for i in range(2)]
        t1b = [psb(f"m_t1{i}", [128, 512]) for i in range(2)]
        combSB = psb("m_csb", [128, 512])
        xt = [psb(f"m_xt{i}", [128, CC * 512], BF16) for i in range(2)]
        xg = psb("m_xg", [128, CC * 128], BF16)
        gtb = psb("m_gt", [128, CC * 8], BF16)
        lg = psb("m_lg", [128, 8])
        m1 = psb("m_m1", [128, 1]); m2 = psb("m_m2", [128, 1])
        e1 = psb("m_e1", [128, 8]); eq2 = psb("m_eq2", [128, 8])
        mk = psb("m_mk", [128, 8]); c1 = psb("m_c1", [128, 8])
        g1 = psb("m_g1", [128, 1]); g2 = psb("m_g2", [128, 1])
        combT = psb("m_cT", [128, 8])
        combF = psb("m_cF", [8, S])
        idf = psb("m_idf", [128, 128])
        oneh = psb("m_oneh", [8, 8 * 128])
        b1c = psb("m_b1", [128, 16]); b2c = psb("m_b2", [128, 16]); b3c = psb("m_b3", [128, CC])
        xr = [psb(f"m_xr{i}", [128, 512]) for i in range(2)]
        pAs = [psum(f"m_pA{i}", [128, 512]) for i in range(2)]
        pBs = [psum(f"m_pB{i}", [128, 512]) for i in range(2)]
        pC = [psum(f"m_pC{i}", [128, 512]) for i in range(2)]
        pD = psum("m_pD", [128, 512])
        pG = psum("m_pG", [128, 512])
        ti0 = p.op("gpsimd", lambda e: e.memset(idf[:], 1.0))
        ti1 = p.op("gpsimd", lambda e: e.affine_select(idf[:], idf[:], [[1, 128]], ALU.is_equal,
                                                       0.0, base=0, channel_multiplier=-1), deps=[ti0])
        t_oh = p.dma("sync", oneh[:], g["onehot8"].ap())
        wg = [p.dma("sync", gtb[:, cc * 8:(cc + 1) * 8], g["gateTB"].ap()[cc]) for cc in range(CC)]

        pp_tok = {k: (None, 0) for k in ["A0", "A1", "B0", "B1", "C0", "C1", "D", "G"]}
        s1_tok = [(None, 0)] * 2
        t1_tok = [(None, 0)] * 2
        comb_toks = {}
        gb_tok = [(None, 0)] * 2
        gslot = 0
        acc_tok = {}
        xt_tok = [(None, 0)] * 2
        xg_tok = (None, 0)
        lg_tok = (None, 0)
        combF_tok = (None, 0)
        combSB_tok = (None, 0)
        w_tok = (None, 0)
        h_cons = (None, 0)
        y_cons = (None, 0)
        xr_tok = [(None, 0)] * 2
        cslot = 0; xslot = 0; rslot = 0
        for b in range(BC):
            gtoks = []
            for tc in range(16):
                xls = []
                for icc in range(CC):
                    dep = [xg_tok] if xg_tok[0] else []
                    xls.append(p.dma("sync", xg[:, icc * 128:(icc + 1) * 128],
                                     g["bufXN"].ap()[b, icc, :, tc * 128:(tc + 1) * 128], deps=dep))
                mm = None
                for icc in range(CC):
                    dep = xls + wg + ([pp_tok["G"]] if pp_tok["G"][0] else [])
                    mm = p.op("tensor", lambda e, icc=icc: e.matmul(
                        pG[0:128, 0:8], xg[:, icc * 128:(icc + 1) * 128], gtb[:, icc * 8:(icc + 1) * 8],
                        start=(icc == 0), stop=(icc == CC - 1)), deps=dep, inc=(icc == CC - 1))
                xg_tok = mm
                cp = p.op("vector", lambda e: e.tensor_copy(lg[:], pG[0:128, 0:8]),
                          deps=[mm] + ([lg_tok] if lg_tok[0] else []))
                pp_tok["G"] = cp
                v1 = p.op("vector", lambda e: e.reduce_max(m1[:], lg[:], axis=AX), deps=[cp])
                v2 = p.op("vector", lambda e: e.tensor_scalar(e1[:], lg[:], m1[:, 0:1], None, ALU.is_equal), deps=[v1])
                v3 = p.op("vector", lambda e: e.scalar_tensor_tensor(mk[:], e1[:], -1e30, lg[:], ALU.mult, ALU.add), deps=[v2])
                v4 = p.op("vector", lambda e: e.reduce_max(m2[:], mk[:], axis=AX), deps=[v3])
                v5 = p.op("vector", lambda e: e.tensor_sub(m1[:], m1[:], m2[:]), deps=[v4])
                v6 = p.op("scalar", lambda e: e.activation(g1[:], m1[:], AF.Sigmoid), deps=[v5])
                v7 = p.op("scalar", lambda e: e.activation(g2[:], g1[:], AF.Identity, scale=-1.0, bias=1.0), deps=[v6])
                v8 = p.op("vector", lambda e: e.tensor_scalar(eq2[:], lg[:], m2[:, 0:1], None, ALU.is_equal), deps=[v4])
                v9 = p.op("vector", lambda e: e.tensor_scalar(c1[:], e1[:], g1[:, 0:1], None, ALU.mult), deps=[v6, v8])
                vA = p.op("vector", lambda e: e.tensor_scalar(eq2[:], eq2[:], g2[:, 0:1], None, ALU.mult), deps=[v7, v9])
                vB = p.op("vector", lambda e: e.tensor_add(combT[:], c1[:], eq2[:]),
                          deps=[vA] + ([comb_toks.get("T", (None, 0))] if comb_toks.get("T", (None, 0))[0] else []))
                lg_tok = vB
                tr = p.op("tensor", lambda e: e.transpose(pG[0:8, 0:128], combT[:], idf[:]),
                          deps=[vB, ti1, cp])
                cf = p.op("scalar", lambda e, tc=tc: e.copy(combF[0:8, tc * 128:(tc + 1) * 128], pG[0:8, 0:128]),
                          deps=[tr] + ([combF_tok] if combF_tok[0] else []))
                pp_tok["G"] = cf
                comb_toks["T"] = tr
                gtoks.append(cf)
            for tg in range(2):
              for ei in range(8):
                  wl = []
                  dep12 = [h_cons] if h_cons[0] else []
                  dep3 = [y_cons] if y_cons[0] else []
                  for cc in range(CC):
                      wl.append(p.dma("sync", W1s[:, cc * 2048:(cc + 1) * 2048], g["w1TB"].ap()[ei, cc], deps=dep12))
                      wl.append(p.dma("sync", W2s[:, cc * 2048:(cc + 1) * 2048], g["w2TB"].ap()[ei, cc], deps=dep12))
                  for hc in range(16):
                      wl.append(p.dma("sync", W3s[:, hc * D:(hc + 1) * D], g["w3TB"].ap()[ei, hc], deps=dep3))
                  depb = [w_tok] if w_tok[0] else []
                  wl.append(p.dma("sync", b1c[:], g["l0_b1"].ap()[ei], deps=depb))
                  wl.append(p.dma("sync", b2c[:], g["l0_b2"].ap()[ei], deps=depb))
                  wl.append(p.dma("sync", b3c[:], g["l0_b3"].ap()[ei], deps=depb))
                  wlast = []
                  for tt in (2 * tg, 2 * tg + 1):
                      gbuf = gbufs[gslot]
                      x = xt[xslot]
                      xls = []
                      for icc in range(CC):
                          dep = [xt_tok[xslot]] if xt_tok[xslot][0] else []
                          xls.append(p.dma("sync", x[:, ts(icc, 512)],
                                           g["bufXN"].ap()[b, icc, :, tt * 512:(tt + 1) * 512], deps=dep))
                      dep = gtoks + [t_oh] + ([pp_tok["D"]] if pp_tok["D"][0] else [])
                      bd = p.op("tensor", lambda e, ei=ei, tt=tt: e.matmul(
                          pD[:], oneh[:, ei * 128:(ei + 1) * 128], combF[0:8, tt * 512:(tt + 1) * 512],
                          start=True, stop=True), deps=dep)
                      cs = p.op("scalar", lambda e: e.copy(combSB[:], pD[:]),
                                deps=[bd] + ([combSB_tok] if combSB_tok[0] else []))
                      pp_tok["D"] = cs
                      hlast = None
                      for hc in range(16):
                          hci = hc % 2
                          pA = pAs[hci]; pB = pBs[hci]
                          mmA = None
                          for icc in range(CC):
                              dep = xls + wl + ([pp_tok["A%d" % hci]] if pp_tok["A%d" % hci][0] and icc == 0 else [])
                              mmA = p.op("tensor", lambda e, icc=icc, hc=hc, x=x, pA=pA: e.matmul(
                                  pA[:], W1s[:, icc * 2048 + hc * 128: icc * 2048 + hc * 128 + 128],
                                  x[:, ts(icc, 512)], start=(icc == 0), stop=(icc == CC - 1)), deps=dep, inc=(icc == CC - 1))
                          mmB = None
                          for icc in range(CC):
                              dep = ([pp_tok["B%d" % hci]] if pp_tok["B%d" % hci][0] and icc == 0 else [])
                              mmB = p.op("tensor", lambda e, icc=icc, hc=hc, x=x, pB=pB: e.matmul(
                                  pB[:], W2s[:, icc * 2048 + hc * 128: icc * 2048 + hc * 128 + 128],
                                  x[:, ts(icc, 512)], start=(icc == 0), stop=(icc == CC - 1)), deps=dep, inc=(icc == CC - 1))
                          s1t = s1s[hci]
                          sv = p.op("scalar", lambda e, hc=hc, pA=pA, s1t=s1t: e.activation(
                              s1t[:], pA[:], AF.Silu, bias=b1c[:, hc:hc + 1]),
                              deps=[mmA] + ([s1_tok[hci]] if s1_tok[hci][0] else []))
                          pp_tok["A%d" % hci] = sv
                          gv = p.op("vector", lambda e, hc=hc, pB=pB, s1t=s1t, gbuf=gbuf: e.scalar_tensor_tensor(
                              gbuf[:, hc * 512:(hc + 1) * 512], pB[:], b2c[:, hc:hc + 1], s1t[:],
                              ALU.add, ALU.mult),
                              deps=[mmB, sv] + ([gb_tok[gslot]] if gb_tok[gslot][0] and hc == 0 else []))
                          s1_tok[hci] = gv
                          pp_tok["B%d" % hci] = gv
                          hlast = gv
                          h_cons = mmB
                      alast = None
                      for occ in range(CC):
                          ci = occ % 2
                          pc = pC[ci]
                          mmC = None
                          for hc in range(16):
                              dep = [hlast, wl[-1]] + ([pp_tok["C%d" % ci]] if pp_tok["C%d" % ci][0] and hc == 0 else [])
                              mmC = p.op("tensor", lambda e, hc=hc, occ=occ, pc=pc, gbuf=gbuf: e.matmul(
                                  pc[:], W3s[:, hc * D + occ * 128: hc * D + occ * 128 + 128],
                                  gbuf[:, hc * 512:(hc + 1) * 512],
                                  start=(hc == 0), stop=(hc == 15)), deps=dep, inc=(hc == 15))
                          y_cons = mmC
                          tb = t1b[occ % 2]
                          aslice = acc[:, ((tt - 2 * tg) * CC + occ) * 512: ((tt - 2 * tg) * CC + occ) * 512 + 512]
                          if ei == 0:
                              tv = p.op("vector", lambda e, pc=pc, occ=occ, aslice=aslice: e.scalar_tensor_tensor(
                                  aslice, pc[:], b3c[:, occ:occ + 1], combSB[:], ALU.add, ALU.mult),
                                  deps=[mmC, cs] + ([acc_tok.get((tt - 2 * tg, occ), (None, 0))] if acc_tok.get((tt - 2 * tg, occ), (None, 0))[0] else []))
                              pp_tok["C%d" % ci] = tv
                              acc_tok[(tt - 2 * tg, occ)] = tv
                              alast = tv
                          else:
                              tv = p.op("vector", lambda e, pc=pc, occ=occ, tb=tb: e.scalar_tensor_tensor(
                                  tb[:], pc[:], b3c[:, occ:occ + 1], combSB[:], ALU.add, ALU.mult),
                                  deps=[mmC, cs] + ([t1_tok[occ % 2]] if t1_tok[occ % 2][0] else []))
                              pp_tok["C%d" % ci] = tv
                              av = p.op("gpsimd", lambda e, tb=tb, aslice=aslice: e.tensor_add(
                                  aslice, aslice, tb[:]),
                                  deps=[tv, acc_tok.get((tt - 2 * tg, occ), (None, 0))])
                              t1_tok[occ % 2] = av
                              acc_tok[(tt - 2 * tg, occ)] = av
                              alast = av
                      wlast.append(alast)
                      gb_tok[gslot] = alast
                      combSB_tok = alast
                      xt_tok[xslot] = alast
                      xslot = 1 - xslot
                      gslot = 1 - gslot
                  w_tok = wlast[-1]
              for tt in (2 * tg, 2 * tg + 1):
                for occ in range(CC):
                    xrb = xr[rslot]
                    lx = p.dma("sync", xrb[:], g["bufX2"].ap()[b, occ, :, tt * 512:(tt + 1) * 512],
                               deps=[xr_tok[rslot]] if xr_tok[rslot][0] else [])
                    aslice = acc[:, ((tt - 2 * tg) * CC + occ) * 512: ((tt - 2 * tg) * CC + occ) * 512 + 512]
                    ad = p.op("vector", lambda e, xrb=xrb, aslice=aslice: e.tensor_add(
                        xrb[:], xrb[:], aslice), deps=[lx, acc_tok[(tt - 2 * tg, occ)]])
                    st = p.dma("sync", g["bufX3"].ap()[b, occ, :, tt * 512:(tt + 1) * 512],
                               xrb[:], deps=[ad])
                    xr_tok[rslot] = st
                    rslot = 1 - rslot
                    acc_tok[(tt - 2 * tg, occ)] = ad


def emit_l1attn(p, nc, g):
    with contextlib.ExitStack() as ph:
        psb = lambda name, shape, dt=F32: ph.enter_context(nc.sbuf_tensor(_nm(name), shape, dt))
        psum = lambda name, shape, dt=F32: ph.enter_context(nc.psum_tensor(_nm(name), shape, dt))
        wqb = psb("l1_wqb", [128, CC * D], BF16)
        wkb = psb("l1_wkb", [128, CC * D], BF16)
        wvb = psb("l1_wvb", [128, CC * D], BF16)
        wob = psb("l1_wob", [128, CC * D], BF16)
        bo = psb("l1_bob", [128, CC])
        ab = psb("l1_ab", [128, 1024])
        cmf = psb("l1_cmf", [128, 896])
        cmb = psb("l1_cmb", [128, 896], BF16)
        xb = psb("l1_xb", [128, CC * S], BF16)
        vh = psb("l1_vh", [128, 16 * 16 * 65], BF16)
        ctxb = psb("l1_ctx", [128, CC * S], BF16)
        qh2 = psb("l1_qh", [128, S], BF16)
        kh2 = psb("l1_kh", [128, S], BF16)
        tmpctx = psb("l1_tc", [64, 512])
        rec = psb("l1_rec", [1, 512])
        one1 = psb("l1_one1", [1, 128])
        pts = [psb(f"l1_pt{i}", [128, 512], BF16) for i in range(4)]
        evo = [psb(f"l1_ev{i}", [128, 512]) for i in range(2)]
        pp = [psum(f"l1_pp{i}", [128, 512]) for i in range(4)]
        pctxs = [psum(f"l1_pctx{i}", [128, 512]) for i in range(2)]
        pbc = psum("l1_pbc", [128, 512])
        wl = []
        for cc in range(CC):
            wl.append(p.dma("gpsimd", wqb[:, ts(cc, D)], g["l1_wq"].ap()[cc]))
            wl.append(p.dma("gpsimd", wkb[:, ts(cc, D)], g["l1_wk"].ap()[cc]))
            wl.append(p.dma("gpsimd", wvb[:, ts(cc, D)], g["l1_wv"].ap()[cc]))
            wl.append(p.dma("gpsimd", wob[:, ts(cc, D)], g["l1_wo"].ap()[cc]))
        wl.append(p.dma("sync", bo[:], g["l1_bo"].ap()))
        wl.append(p.dma("sync", ab[:], g["alibias"].ap()))
        wl.append(p.dma("sync", cmf[:], g["causalm"].ap()))
        t_cm = p.op("scalar", lambda e: e.copy(cmb[:], cmf[:]), deps=wl)
        t_on1 = p.op("vector", lambda e: e.memset(one1[:], 1.0))
        t_vones = p.op("vector", lambda e: e.memset(vh[:, 64:16640:65], 1.0))

        pp_tok = [(None, 0)] * 4
        pt_tok = [(None, 0)] * 4
        evo_tok = [(None, 0)] * 2
        xb_tok = (None, 0)
        vh_tok = (None, 0)
        ctxb_tok = (None, 0)
        tctx_tok = (None, 0); rec_tok = (None, 0)
        pctx_tok = [(None, 0)] * 2; pbc_tok = (None, 0)
        pslot = 0; ptslot = 0; eslot = 0
        for b in range(BC):
            xl = []
            for cc in range(CC):
                dep = [xb_tok] if xb_tok[0] else []
                xl.append(p.dma("gpsimd", xb[:, cc * S:(cc + 1) * S],
                                g["bufX4"].ap()[b, cc], deps=dep))
            vtoks = []
            for kc in range(16):
                for vc in range(2):
                    pv = pp[pslot]
                    mm = None
                    for icc in range(CC):
                        dep = xl + wl + ([pp_tok[pslot]] if pp_tok[pslot][0] else [])
                        mm = p.op("tensor", lambda e, pv=pv, icc=icc, kc=kc, vc=vc: e.matmul(
                            pv[:], xb[:, icc * S + kc * 128: icc * S + kc * 128 + 128],
                            wvb[:, icc * D + vc * 512: icc * D + vc * 512 + 512],
                            start=(icc == 0), stop=(icc == CC - 1)), deps=dep, inc=(icc == CC - 1))
                    last = None
                    for j in range(8):
                        h = vc * 8 + j
                        dep = [mm, t_vones] + ([vh_tok] if vh_tok[0] else [])
                        last = p.op("vector", lambda e, pv=pv, j=j, kc=kc, h=h: e.tensor_copy(
                            vh[:, (kc * 16 + h) * 65: (kc * 16 + h) * 65 + 64],
                            pv[:, j * 64: j * 64 + 64]), deps=dep)
                        vtoks.append(last)
                    pp_tok[pslot] = last
                    pslot = (pslot + 1) % 4
            av_last = []
            qk_last = []
            qk_cons = (None, 0)
            for occ in range(CC):
                for (which, wt) in (("q", wqb), ("k", wkb)):
                    buf = qh2 if which == "q" else kh2
                    for tt in range(NT):
                        pq = pp[pslot]
                        mm = None
                        for icc in range(CC):
                            dep = xl + wl + ([pp_tok[pslot]] if pp_tok[pslot][0] else [])
                            mm = p.op("tensor", lambda e, pq=pq, icc=icc, occ=occ, tt=tt, wt=wt: e.matmul(
                                pq[:], wt[:, icc * D + occ * 128: icc * D + occ * 128 + 128],
                                xb[:, icc * S + tt * 512: icc * S + tt * 512 + 512],
                                start=(icc == 0), stop=(icc == CC - 1)), deps=dep, inc=(icc == CC - 1))
                        qk_last.append(mm)
                        wardep = [qk_cons] if qk_cons[0] else []
                        if which == "q":
                            et = p.op("scalar", lambda e, pq=pq, buf=buf, tt=tt: e.activation(
                                buf[:, tt * 512:(tt + 1) * 512], pq[:], AF.Identity,
                                scale=0.125), deps=[mm] + wardep)
                        else:
                            et = p.op("scalar", lambda e, pq=pq, buf=buf, tt=tt: e.copy(
                                buf[:, tt * 512:(tt + 1) * 512], pq[:]), deps=[mm] + wardep)
                        pp_tok[pslot] = et
                        pslot = (pslot + 1) % 4
                        if which == "q":
                            qh_tok = et
                        else:
                            kh_tok = et
                for qt in range(NT):
                    nkc = 4 * qt + 4
                    avmm = [None, None]
                    sq_pend = {}

                    def emit_S(kc, qt=qt, occ=occ):
                        nonlocal pslot, qk_cons
                        for hh in range(2):
                            hr = 64 * hh
                            pS = pp[pslot]
                            dep = [qh_tok, kh_tok] + ([pp_tok[pslot]] if pp_tok[pslot][0] else [])
                            smm = p.op("tensor", lambda e, pS=pS, hr=hr, kc=kc, qt=qt: e.matmul(
                                pS[0:128, :], kh2[hr:hr + 64, kc * 128:(kc + 1) * 128],
                                qh2[hr:hr + 64, qt * 512:(qt + 1) * 512],
                                start=True, stop=True), deps=dep)
                            qk_cons = smm
                            sq_pend[(kc, hh)] = (pS, pslot, smm)
                            pslot = (pslot + 1) % 4

                    emit_S(0)
                    for kc in range(nkc):
                        if kc + 1 < nkc:
                            emit_S(kc + 1)
                        for hh in range(2):
                            h = 2 * occ + hh
                            pS, psl, smm = sq_pend.pop((kc, hh))
                            pt = pts[2 * hh + ptslot]
                            dep = [smm] + ([pt_tok[2 * hh + ptslot]] if pt_tok[2 * hh + ptslot][0] else [])
                            ex = p.op("scalar", lambda e, pS=pS, pt=pt, h=h, kc=kc, qt=qt: e.activation(
                                pt[:], pS[:], AF.Exp,
                                bias=ab[:, h * 64 + kc * 4 + qt: h * 64 + kc * 4 + qt + 1]), deps=dep)
                            pp_tok[psl] = ex
                            lastpt = ex
                            if kc >= 4 * qt:
                                delta = qt * 512 - kc * 128 + 384
                                lastpt = p.op("vector", lambda e, pt=pt, delta=delta: e.tensor_mul(
                                    pt[:], pt[:], cmb[:, delta: delta + 512]), deps=[ex, t_cm])
                            dep = [lastpt] + vtoks[-1:] + ([pctx_tok[hh]] if pctx_tok[hh][0] and kc == 0 else [])
                            avmm[hh] = p.op("tensor", lambda e, pt=pt, kc=kc, h=h, nkc=nkc, hh=hh: e.matmul(
                                pctxs[hh][0:65, :], vh[:, (kc * 16 + h) * 65: (kc * 16 + h) * 65 + 65],
                                pt[:], start=(kc == 0), stop=(kc == nkc - 1)), deps=dep,
                                inc=(kc == nkc - 1))
                            pt_tok[2 * hh + ptslot] = avmm[hh]
                        ptslot = 1 - ptslot
                    for hh in range(2):
                        h = 2 * occ + hh
                        hr = 64 * hh
                        pctx = pctxs[hh]
                        av_last.append(avmm[hh])
                        tL = p.op("scalar", lambda e, pctx=pctx: e.activation(rec[:], pctx[64:65, :], AF.Ln),
                                  deps=[avmm[hh]] + ([rec_tok] if rec_tok[0] else []))
                        tE = p.op("scalar", lambda e: e.activation(rec[:], rec[:], AF.Exp, scale=-1.0),
                                  deps=[tL])
                        dep = [tE, t_on1] + ([pbc_tok] if pbc_tok[0] else [])
                        bc = p.op("tensor", lambda e: e.matmul(pbc[0:64, :], one1[:, 0:64], rec[:],
                                                               start=True, stop=True), deps=dep)
                        tc1 = p.op("scalar", lambda e, pctx=pctx: e.copy(tmpctx[:], pctx[0:64, :]),
                                   deps=[avmm[hh], tE] + ([tctx_tok] if tctx_tok[0] else []))
                        pctx_tok[hh] = tc1
                        dep = [tc1, bc] + ([ctxb_tok] if ctxb_tok[0] else [])
                        mu = p.op("vector", lambda e, h=h, hr=hr, qt=qt, occ=occ: e.tensor_mul(
                            ctxb[hr:hr + 64, occ * S + qt * 512: occ * S + qt * 512 + 512],
                            tmpctx[:], pbc[0:64, :]), deps=dep)
                        pbc_tok = mu
                        rec_tok = bc
                        tctx_tok = mu
                        av_last.append(mu)
            xb_tok = qk_last[-1]
            vh_tok = av_last[-1]
            op_last = []
            for occ in range(CC):
                for tt in range(NT):
                    po = pp[pslot]
                    mm = None
                    for icc in range(CC):
                        dep = av_last[-1:] + wl + ([pp_tok[pslot]] if pp_tok[pslot][0] else [])
                        mm = p.op("tensor", lambda e, po=po, icc=icc, occ=occ, tt=tt: e.matmul(
                            po[:], wob[:, icc * D + occ * 128: icc * D + occ * 128 + 128],
                            ctxb[:, icc * S + tt * 512: icc * S + tt * 512 + 512],
                            start=(icc == 0), stop=(icc == CC - 1)), deps=dep, inc=(icc == CC - 1))
                    op_last.append(mm)
                    eb = evo[eslot]
                    e1 = p.op("scalar", lambda e, po=po, eb=eb, occ=occ: e.activation(
                        eb[:], po[:], AF.Identity, bias=bo[:, occ:occ + 1]),
                        deps=[mm] + ([evo_tok[eslot]] if evo_tok[eslot][0] else []))
                    pp_tok[pslot] = e1
                    pslot = (pslot + 1) % 4
                    st = p.dma("sync", g["bufA"].ap()[b, occ, :, tt * 512:(tt + 1) * 512],
                               eb[:], deps=[e1])
                    evo_tok[eslot] = st
                    eslot = 1 - eslot
            ctxb_tok = op_last[-1]


def emit_l1router(p, nc, g):
    with contextlib.ExitStack() as ph:
        psb = lambda name, shape, dt=F32: ph.enter_context(nc.sbuf_tensor(_nm(name), shape, dt))
        psum = lambda name, shape, dt=F32: ph.enter_context(nc.psum_tensor(_nm(name), shape, dt))
        xs = [psb(f"r_x{i}", [128, CC * 512]) for i in range(2)]
        hb = psb("r_h", [128, CC * 512], BF16)
        sq = psb("r_sq", [128, 512])
        sq2 = psb("r_sq2", [128, 512])
        stM = psb("r_stM", [1, 512]); stV = psb("r_stV", [1, 512]); stT = psb("r_stT", [1, 512])
        gcol = psb("r_g", [128, CC]); becol = psb("r_be", [128, CC])
        wrb = psb("r_wr", [128, CC * 8], BF16)
        brs = psb("r_br", [8, 1])
        lgF = psb("r_lg", [8, 512]); eF = psb("r_eF", [8, 512])
        r1 = psb("r_r1", [1, 512]); p0r = psb("r_p0", [1, 512])
        onec = psb("r_one", [128, 1]); one1 = psb("r_one1", [1, 128])
        pS = psum("r_pS", [1, 512]); pQ = psum("r_pQ", [1, 512])
        pB = psum("r_pB", [128, 512]); pM = psum("r_pM", [128, 512])
        pL = psum("r_pL", [128, 512])
        t_on = p.op("vector", lambda e: e.memset(onec[:], 1.0))
        t_on1 = p.op("vector", lambda e: e.memset(one1[:], 1.0))
        wl = [p.dma("sync", gcol[:], g["l1_g"].ap()),
              p.dma("sync", becol[:], g["l1_be"].ap()),
              p.dma("sync", brs[:], g["l1_br"].ap())]
        for cc in range(CC):
            wl.append(p.dma("gpsimd", wrb[:, cc * 8:(cc + 1) * 8], g["l1_wr"].ap()[cc]))
        epsc, epst = p.constcol(1e-5, rows=1)
        xs_tok = [(None, 0)] * 2
        hb_tok = (None, 0); sq_tok = [(None, 0)] * 2; st_tok = (None, 0)
        pS_tok = (None, 0); pB_tok = (None, 0); pM_tok = (None, 0)
        pL_tok = (None, 0); lg_tok = (None, 0); e_tok = (None, 0)
        r1_tok = (None, 0); p0_tok = (None, 0)
        bB_prev = (None, 0); bM_prev = (None, 0)
        slot = 0
        for b in range(BC):
            for tt in range(NT):
                x = xs[slot]
                loads = []
                for cc in range(CC):
                    dep = [xs_tok[slot]] if xs_tok[slot][0] else []
                    loads.append(p.dma("sync", x[:, ts(cc, 512)],
                                       g["bufX6"].ap()[b, cc, :, tt * 512:(tt + 1) * 512], deps=dep))
                mmS = None
                for cc in range(CC):
                    dep = [loads[cc], t_on] + ([pS_tok] if pS_tok[0] else [])
                    mmS = p.op("tensor", lambda e, cc=cc, x=x: e.matmul(
                        pS[:], onec[:], x[:, ts(cc, 512)],
                        start=(cc == 0), stop=(cc == CC - 1)), deps=dep, inc=(cc == CC - 1))
                mmQ = None
                for cc in range(CC):
                    sqx = sq2 if cc % 2 else sq
                    dep = [loads[cc]] + ([sq_tok[cc % 2]] if sq_tok[cc % 2][0] else [])
                    tq = p.op("scalar", lambda e, cc=cc, x=x, sqx=sqx: e.activation(
                        sqx[:], x[:, ts(cc, 512)], AF.Square), deps=dep)
                    mmQ = p.op("tensor", lambda e, cc=cc, sqx=sqx: e.matmul(
                        pQ[:], onec[:], sqx[:], start=(cc == 0), stop=(cc == CC - 1)),
                        deps=[tq, mmS])
                    sq_tok[cc % 2] = mmQ
                dep0 = [mmS, mmQ] + ([st_tok] if st_tok[0] else [])
                s0 = p.op("vector", lambda e: e.tensor_scalar(stM[:], pS[:], 1.0 / D, None, ALU.mult), deps=dep0)
                s1 = p.op("vector", lambda e: e.tensor_scalar(stV[:], pQ[:], 1.0 / D, None, ALU.mult),
                          deps=[mmQ, s0] + ([bB_prev] if bB_prev[0] else []))
                pS_tok = s1
                s2 = p.op("vector", lambda e: e.tensor_mul(stT[:], stM[:], stM[:]), deps=[s0])
                s3 = p.op("vector", lambda e: e.tensor_sub(stV[:], stV[:], stT[:]), deps=[s1, s2])
                s4 = p.op("scalar", lambda e: e.activation(stV[:], stV[:], AF.Ln, bias=epsc), deps=[s3, epst])
                s5 = p.op("scalar", lambda e: e.activation(stV[:], stV[:], AF.Exp, scale=-0.5), deps=[s4])
                s6 = p.op("vector", lambda e: e.tensor_mul(stT[:], stM[:], stV[:]),
                          deps=[s5, s2] + ([bM_prev] if bM_prev[0] else []))
                st_tok = s6
                bB = p.op("tensor", lambda e: e.matmul(pB[:], one1[:], stV[:], start=True, stop=True),
                          deps=[s5, t_on1] + ([pB_tok] if pB_tok[0] else []))
                bM = p.op("tensor", lambda e: e.matmul(pM[:], one1[:], stT[:], start=True, stop=True),
                          deps=[s6, t_on1, bB] + ([pM_tok] if pM_tok[0] else []))
                bB_prev, bM_prev = bB, bM
                hs = []
                m2 = None
                for cc in range(CC):
                    sqx = sq2 if cc % 2 else sq
                    dep = [bB, loads[cc]] + ([hb_tok] if hb_tok[0] else [])
                    m1 = p.op("vector", lambda e, cc=cc, x=x, sqx=sqx: e.tensor_mul(
                        sqx[:], x[:, ts(cc, 512)], pB[:]), deps=dep + [sq_tok[cc % 2]])
                    m2 = p.op("vector", lambda e, cc=cc, sqx=sqx: e.tensor_sub(sqx[:], sqx[:], pM[:]),
                              deps=[m1, bM])
                    m3 = p.op("scalar", lambda e, cc=cc, sqx=sqx: e.activation(
                        hb[:, ts(cc, 512)], sqx[:], AF.Identity,
                        scale=gcol[:, cc:cc + 1], bias=becol[:, cc:cc + 1]), deps=[m2] + wl)
                    sq_tok[cc % 2] = m3
                    hs.append(m3)
                pM_tok = m2
                pB_tok = m2
                mmL = None
                for cc in range(CC):
                    dep = [hs[cc]] + wl + ([pL_tok] if pL_tok[0] and cc == 0 else [])
                    mmL = p.op("tensor", lambda e, cc=cc: e.matmul(
                        pL[0:8, :], wrb[:, cc * 8:(cc + 1) * 8], hb[:, ts(cc, 512)],
                        start=(cc == 0), stop=(cc == CC - 1)), deps=dep, inc=(cc == CC - 1))
                hb_tok = mmL
                lg = p.op("scalar", lambda e: e.activation(lgF[:], pL[0:8, :], AF.Identity,
                                                           bias=brs[:, 0:1]),
                          deps=[mmL] + wl + ([lg_tok] if lg_tok[0] else []))
                pL_tok = lg
                ee = p.op("scalar", lambda e: e.activation(eF[:], lgF[:], AF.Exp),
                          deps=[lg] + ([e_tok] if e_tok[0] else []))
                lg_tok = ee
                mmE = p.op("tensor", lambda e: e.matmul(pS[:], onec[0:8, :], eF[:],
                                                        start=True, stop=True),
                           deps=[ee, t_on, s1] + ([pS_tok] if pS_tok[0] else []))
                pS_tok = mmE
                rL = p.op("scalar", lambda e: e.activation(r1[:], pS[:], AF.Ln),
                          deps=[mmE] + ([r1_tok] if r1_tok[0] else []))
                rE = p.op("scalar", lambda e: e.activation(r1[:], r1[:], AF.Exp, scale=-1.0), deps=[rL])
                pS_tok = rL
                pv = p.op("vector", lambda e: e.tensor_mul(p0r[:], eF[0:1, :], r1[:]),
                          deps=[rE] + ([p0_tok] if p0_tok[0] else []))
                e_tok = pv
                r1_tok = pv
                p0_tok = p.dma("sync", g["bufP0"].ap()[b, :, tt * 512:(tt + 1) * 512], p0r[:], deps=[pv])
                xs_tok[slot] = p0_tok
                slot = 1 - slot


def _bisect_search(p, nc, psb, psum_, absap, ncols, n, k, thr_out, regs, cells, deps):
    tcell, tcol, c11, cnt, trash, onec, one1, pT, pc = (
        cells["tcell"], cells["tcol"], cells["c11"], cells["cnt"], cells["trash"],
        cells["onec"], cells["one1"], cells["pT"], cells["pc"])
    lo, hi, mid, cr = regs
    KBITS = int(np.float32(k).view(np.int32))
    p.raw("vector", lambda e: e.reg_mov(lo, 0), deps=deps)
    p.raw("vector", lambda e: e.reg_mov(hi, 0x7F800000))
    tcol_tok = cells.get("tcol_tok", (None, 0))
    c11_tok = cells.get("c11_tok", (None, 0))
    for it in range(31):
        def _mids(e, lo=lo, hi=hi, mid=mid):
            e.reg_add(mid, lo, hi)
            e.reg_alu(mid, mid, 1, ALU.arith_shift_right)
            return e.reg_save(cells["tcell"][0:1, 0:1], mid)
        tw = p.op("vector", _mids, deps=[cells.get("pT_tok", (None, 0))] if cells.get("pT_tok", (None, 0))[0] else [])
        bc = p.op("tensor", lambda e: e.matmul(pT[:, 0:1], one1[:], tcell[0:1, 0:1].bitcast(F32),
                                               start=True, stop=True),
                  deps=[tw] + ([tcol_tok] if tcol_tok[0] else []))
        cp = p.op("scalar", lambda e: e.copy(tcol[:], pT[:, 0:1]), deps=[bc])
        cells["pT_tok"] = cp
        part = (ncols * 5) // 8 // 64 * 64
        if part >= 64 and ncols - part >= 64:
            ntg = p.op("scalar", lambda e: e.activation(cells["tneg"][:], tcol[:], AF.Identity,
                                                        scale=-1.0), deps=[cp])
            ct = p.op("vector", lambda e, part=part: e.tensor_scalar(
                trash[:, 0:part], absap[:, 0:part], tcol[:, 0:1], 0.0, ALU.is_lt, ALU.add,
                accum_out=cnt[:]), deps=[cp] + ([c11_tok] if c11_tok[0] else []))
            sg = p.op("scalar", lambda e, part=part, ncols=ncols: e.activation(
                cells["sgn"][:, 0:ncols - part], absap[:, part:ncols], AF.Sign,
                scale=-1.0, bias=tcol[:, 0:1]), deps=[cp] + ([cells.get("sgn_tok", (None, 0))] if cells.get("sgn_tok", (None, 0))[0] else []))
            ct2 = p.op("scalar", lambda e, part=part, ncols=ncols: e.activation(
                cells["sgn"][:, 0:ncols - part], cells["sgn"][:, 0:ncols - part], AF.Relu,
                accum_out=cells["cnt2"][:]), deps=[sg] + ([c11_tok] if c11_tok[0] else []))
            cells["sgn_tok"] = ct2
            rd = p.op("tensor", lambda e: e.matmul(pc[0:1, 0:1], onec[:], cnt[:],
                                                   start=True, stop=False), deps=[ct, ct2])
            rd = p.op("tensor", lambda e: e.matmul(pc[0:1, 0:1], onec[:], cells["cnt2"][:],
                                                   start=False, stop=True), deps=[])
            tcol_tok = rd
        else:
            ct = p.op("vector", lambda e: e.tensor_scalar(
                trash[:, 0:ncols], absap, tcol[:, 0:1], 0.0, ALU.is_lt, ALU.add,
                accum_out=cnt[:]), deps=[cp] + ([c11_tok] if c11_tok[0] else []))
            tcol_tok = ct
            rd = p.op("tensor", lambda e: e.matmul(pc[0:1, 0:1], onec[:], cnt[:],
                                                   start=True, stop=True), deps=[ct])
        c2 = p.op("scalar", lambda e: e.copy(c11[0:1, 0:1], pc[0:1, 0:1]), deps=[rd])
        c11_tok = c2

        def _branch(e, lo=lo, hi=hi, mid=mid, cr=cr, KBITS=KBITS):
            e.reg_load(cr, cells["c11"][0:1, 0:1].bitcast(mybir.dt.int32))
            with e.If_lt(cr, KBITS):
                e.reg_add(lo, mid, 1)
            with e.Else():
                e.reg_mov(hi, mid)
            return e.nop()
        p.op("vector", _branch, deps=[c2])
    def _fin(e, hi=hi, mid=mid):
        e.reg_add(mid, hi, -1)
        return e.reg_save(cells["tcell"][0:1, 0:1], mid)
    tw = p.op("vector", _fin)
    bc = p.op("tensor", lambda e: e.matmul(pT[:, 0:1], one1[:], tcell[0:1, 0:1].bitcast(F32),
                                           start=True, stop=True),
              deps=[tw] + ([tcol_tok] if tcol_tok[0] else []))
    fin = p.op("scalar", lambda e: e.copy(thr_out, pT[:, 0:1]), deps=[bc])
    cells["pT_tok"] = fin
    cells["tcol_tok"] = tcol_tok
    cells["c11_tok"] = c11_tok
    return fin


def emit_thresholds(p, nc, g):
    with contextlib.ExitStack() as ph:
        psb = lambda name, shape, dt=F32: ph.enter_context(nc.sbuf_tensor(_nm(name), shape, dt))
        psum_ = lambda name, shape, dt=F32: ph.enter_context(nc.psum_tensor(_nm(name), shape, dt))
        absw = psb("th_absw", [128, 32768])
        trash = psb("th_trash", [128, 20480], BF16)
        small = psb("th_small", [128, 48])
        b1a = psb("th_b1a", [128, 32])
        b2a = psb("th_b2a", [128, 8])
        f1 = psb("th_f1", [128, 32])
        f2 = psb("th_f2", [128, 32])
        tcell = psb("th_tcell", [1, 1], mybir.dt.int32)
        tcol = psb("th_tcol", [128, 1])
        c11 = psb("th_c11", [1, 1])
        cnt = psb("th_cnt", [128, 1])
        onec = psb("th_one", [128, 1])
        one1 = psb("th_one1", [1, 128])
        pT = psum_("th_pT", [128, 512])
        pc = psum_("th_pc", [128, 512])
        t_on = p.op("vector", lambda e: e.memset(onec[:], 1.0))
        t_on1 = p.op("vector", lambda e: e.memset(one1[:], 1.0))
        lo = nc.vector.alloc_register("th_lo")
        hi = nc.vector.alloc_register("th_hi")
        mid = nc.vector.alloc_register("th_mid")
        cr = nc.vector.alloc_register("th_cr")
        regs = (lo, hi, mid, cr)
        cnt2 = psb("th_cnt2", [128, 1])
        sgn = psb("th_sgn", [128, 12288], BF16)
        tneg = psb("th_tneg", [128, 1])
        cells = dict(tcell=tcell, tcol=tcol, c11=c11, cnt=cnt, cnt2=cnt2, trash=trash,
                     sgn=sgn, tneg=tneg, onec=onec, one1=one1, pT=pT, pc=pc)
        nW, kW = 4194304, 209715
        lds = [p.dma("sync", absw[:, cc * 4096:(cc + 1) * 4096], g["l1_W1"].ap()[cc])
               for cc in range(CC)]
        ab = p.op("scalar", lambda e: e.activation(absw[:], absw[:], AF.Abs),
                  deps=lds + [t_on, t_on1])
        t1 = _bisect_search(p, nc, psb, psum_, absw[:], 32768, nW, kW,
                            small[:, 0:1], regs, cells, [ab])
        t2 = _bisect_search(p, nc, psb, psum_, absw[:], 32768, nW, nW - kW + 1,
                            small[:, 1:2], regs, cells, [t1])
        lds = []
        prev = t2
        for j in range(32):
            lds.append(p.dma("sync", absw[:, j * 1024:(j + 1) * 1024], g["l1_W2f"].ap()[j],
                             deps=[prev]))
        ab = p.op("scalar", lambda e: e.activation(absw[:], absw[:], AF.Abs), deps=lds)
        t3 = _bisect_search(p, nc, psb, psum_, absw[:], 32768, nW, kW,
                            small[:, 4:5], regs, cells, [ab])
        t4 = _bisect_search(p, nc, psb, psum_, absw[:], 32768, nW, nW - kW + 1,
                            small[:, 5:6], regs, cells, [t3])
        lb1 = p.dma("sync", b1a[:], g["l1_b1f"].ap())
        lb2 = p.dma("sync", b2a[:], g["l1_b2f"].ap())
        ab1 = p.op("scalar", lambda e: e.activation(b1a[:], b1a[:], AF.Abs), deps=[lb1])
        ab2 = p.op("scalar", lambda e: e.activation(b2a[:], b2a[:], AF.Abs), deps=[lb2])
        t5 = _bisect_search(p, nc, psb, psum_, b1a[:], 32, 4096, 204,
                            small[:, 2:3], regs, cells, [ab1, t4])
        t6 = _bisect_search(p, nc, psb, psum_, b1a[:], 32, 4096, 4096 - 204 + 1,
                            small[:, 3:4], regs, cells, [t5])
        t7 = _bisect_search(p, nc, psb, psum_, b2a[:], 8, 1024, 51,
                            small[:, 6:7], regs, cells, [t6, ab2])
        t8 = _bisect_search(p, nc, psb, psum_, b2a[:], 8, 1024, 1024 - 51 + 1,
                            small[:, 7:8], regs, cells, [t7])
        lb1v = p.dma("sync", f2[:, 0:32], g["l1_b1f"].ap(), deps=[t8])
        u1 = p.op("vector", lambda e: e.tensor_scalar(f1[:, 0:32], b1a[:], small[:, 2:3], None, ALU.is_le), deps=[t8])
        u2 = p.op("scalar", lambda e: e.activation(f1[:, 0:32], f1[:, 0:32], AF.Identity, scale=0.8, bias=1.0), deps=[u1])
        u3 = p.op("vector", lambda e: e.tensor_scalar(b1a[:], b1a[:], small[:, 3:4], None, ALU.is_ge), deps=[u2])
        u4 = p.op("vector", lambda e: e.scalar_tensor_tensor(f1[:, 0:32], b1a[:], -0.8, f1[:, 0:32], ALU.mult, ALU.add), deps=[u3])
        u5 = p.op("vector", lambda e: e.tensor_mul(small[:, 8:40], f2[:, 0:32], f1[:, 0:32]), deps=[u4, lb1v])
        lb2v = p.dma("sync", f2[:, 0:8], g["l1_b2f"].ap(), deps=[u5])
        w1 = p.op("vector", lambda e: e.tensor_scalar(f1[:, 0:8], b2a[:], small[:, 6:7], None, ALU.is_le), deps=[t8, u5])
        w2 = p.op("scalar", lambda e: e.activation(f1[:, 0:8], f1[:, 0:8], AF.Identity, scale=0.8, bias=1.0), deps=[w1])
        w3 = p.op("vector", lambda e: e.tensor_scalar(b2a[:], b2a[:], small[:, 7:8], None, ALU.is_ge), deps=[w2])
        w4 = p.op("vector", lambda e: e.scalar_tensor_tensor(f1[:, 0:8], b2a[:], -0.8, f1[:, 0:8], ALU.mult, ALU.add), deps=[w3])
        w5 = p.op("vector", lambda e: e.tensor_mul(small[:, 40:48], f2[:, 0:8], f1[:, 0:8]), deps=[w4, lb2v])
        p.dma("sync", g["bufSMALL"].ap(), small[:], deps=[u5, w5])
        nc.vector.free_register(lo); nc.vector.free_register(hi)
        nc.vector.free_register(mid); nc.vector.free_register(cr)


def emit_l1ffn(p, nc, g, path):
    with contextlib.ExitStack() as ph:
        psb = lambda name, shape, dt=F32: ph.enter_context(nc.sbuf_tensor(_nm(name), shape, dt))
        psum_ = lambda name, shape, dt=F32: ph.enter_context(nc.psum_tensor(_nm(name), shape, dt))
        W1b = psb("f_w1", [128, CC * 4096], BF16)
        W2b = psb("f_w2", [128, 32 * D], BF16)
        gbufL = psb("f_g", [128, 32 * 512], BF16)
        small = psb("f_small", [128, 48])
        xt = [psb(f"f_xt{i}", [128, CC * 512], BF16) for i in range(2)]
        ev = [psb(f"f_ev{i}", [128, 512]) for i in range(2)]
        one1 = psb("f_one1", [1, 128])
        b1cl = psb("f_b1", [128, 32])
        b2cl = psb("f_b2", [128, CC])
        us = [psb(f"f_u{i}", [128, 512]) for i in range(3)]
        q1s = [psb(f"f_q1{i}", [128, 512]) for i in range(2)]
        q3s = [psb(f"f_q3{i}", [128, 512]) for i in range(2)]
        sgs = [psb(f"f_sg{i}", [128, 512]) for i in range(2)]
        if path == 1:
            y0t = [psb(f"f_y0{i}", [128, 512], BF16) for i in range(2)]
            p0r = psb("f_p0", [1, 512])
            combP = psb("f_cp", [128, 512], BF16)
            wch = us[0]; fa = q1s[0]; ft = q3s[0]
        pF = [psum_(f"f_pF{i}", [128, 512]) for i in range(2)]
        pY = [psum_(f"f_pY{i}", [128, 512]) for i in range(2)]
        pP = psum_("f_pP", [128, 512])
        t_on1 = p.op("vector", lambda e: e.memset(one1[:], 1.0))
        wl = []
        tsm = p.dma("sync", small[:], g["bufSMALL"].ap())
        if path == 0:
            for cc in range(CC):
                wl.append(p.dma("gpsimd", W1b[:, cc * 4096:(cc + 1) * 4096], g["l1_W1"].ap()[cc]))
            for j in range(32):
                wl.append(p.dma("gpsimd", W2b[:, j * D:(j + 1) * D], g["l1_W2f"].ap()[j]))
            wl.append(p.dma("sync", b1cl[:, 0:32], g["l1_b1f"].ap()))
            wl.append(p.dma("sync", b2cl[:], g["l1_b2f"].ap()))
            bload = wl[-1]
        else:
            wprev = (None, 0)
            for (wname, wdst, nblk, blkw) in (("l1_W1", W1b, CC, 4096), ("l1_W2f", W2b, 32, D)):
                for ib in range(nblk):
                    nch = blkw // 512
                    for jc in range(nch):
                        dep = [wprev] if wprev[0] else []
                        lw = p.dma("sync", wch[:], g[wname].ap()[ib, :, jc * 512:(jc + 1) * 512],
                                   deps=dep)
                        tb = 0 if wname == "l1_W1" else 4
                        a1 = p.op("scalar", lambda e: e.activation(fa[:], wch[:], AF.Abs),
                                  deps=[lw, tsm])
                        a2 = p.op("vector", lambda e, tb=tb: e.tensor_scalar(
                            ft[:], fa[:], small[:, tb:tb + 1], None, ALU.is_le), deps=[a1])
                        a3 = p.op("scalar", lambda e: e.activation(ft[:], ft[:], AF.Identity,
                                                                   scale=0.8, bias=1.0), deps=[a2])
                        a4 = p.op("vector", lambda e, tb=tb: e.tensor_scalar(
                            fa[:], fa[:], small[:, tb + 1:tb + 2], None, ALU.is_ge), deps=[a3])
                        a5 = p.op("vector", lambda e: e.scalar_tensor_tensor(
                            ft[:], fa[:], -0.8, ft[:], ALU.mult, ALU.add), deps=[a4])
                        a6 = p.op("vector", lambda e, wdst=wdst, ib=ib, jc=jc, blkw=blkw: e.tensor_mul(
                            wdst[:, ib * blkw + jc * 512: ib * blkw + jc * 512 + 512],
                            wch[:], ft[:]), deps=[a5])
                        wprev = a6
                        wl.append(a6)
            bl1 = p.op("vector", lambda e: e.tensor_copy(b1cl[:, 0:32], small[:, 8:40]), deps=[tsm])
            bl2 = p.op("vector", lambda e: e.tensor_copy(b2cl[:], small[:, 40:48]), deps=[tsm])
            wl += [bl1, bl2]
            bload = bl2
        xt_tok = [(None, 0)] * 2
        u_tok = [(None, 0)] * 3; q1_tok = [(None, 0)] * 2
        q3_tok = [(None, 0)] * 2; sgm_tok = [(None, 0)] * 2
        gb_tok = (None, 0)
        pF_tok = [(None, 0)] * 2
        pY_tok = [(None, 0)] * 2
        pP_tok = (None, 0)
        ev_tok = [(None, 0)] * 2
        y0_tok = [(None, 0)] * 2
        p0_tok = (None, 0); cp_tok = (None, 0)
        xslot = 0; eslot = 0
        for b in range(BC):
            for tt in range(NT):
                x = xt[xslot]
                xls = []
                for cc in range(CC):
                    dep = [xt_tok[xslot]] if xt_tok[xslot][0] else []
                    xls.append(p.dma("gpsimd", x[:, ts(cc, 512)],
                                     g["bufX6"].ap()[b, cc, :, tt * 512:(tt + 1) * 512], deps=dep))
                if path == 1:
                    lp = p.dma("sync", p0r[:], g["bufP0"].ap()[b, :, tt * 512:(tt + 1) * 512],
                               deps=[p0_tok] if p0_tok[0] else [])
                    bcp = p.op("tensor", lambda e: e.matmul(pP[:], one1[:], p0r[:],
                                                            start=True, stop=True),
                               deps=[lp, t_on1] + ([pP_tok] if pP_tok[0] else []))
                    p0_tok = bcp
                    cpt = p.op("scalar", lambda e: e.copy(combP[:], pP[:]),
                               deps=[bcp] + ([cp_tok] if cp_tok[0] else []))
                    pP_tok = cpt
                glast = None
                for fc in range(32):
                    ci = fc % 2
                    gi = fc % 3
                    pf = pF[ci]
                    u = us[gi]; q1 = q1s[ci]; q3 = q3s[ci]; sgm = sgs[ci]
                    mmF = None
                    for icc in range(CC):
                        dep = xls + wl + ([pF_tok[ci]] if pF_tok[ci][0] and icc == 0 else [])
                        mmF = p.op("tensor", lambda e, pf=pf, icc=icc, fc=fc, x=x: e.matmul(
                            pf[:], W1b[:, icc * 4096 + fc * 128: icc * 4096 + fc * 128 + 128],
                            x[:, ts(icc, 512)], start=(icc == 0), stop=(icc == CC - 1)), deps=dep, inc=(icc == CC - 1))
                    uv = p.op("scalar", lambda e, pf=pf, fc=fc, u=u: e.activation(
                        u[:], pf[:], AF.Identity, bias=b1cl[:, fc:fc + 1]),
                        deps=[mmF, bload] + ([u_tok[gi]] if u_tok[gi][0] else []))
                    pF_tok[ci] = uv
                    q1v = p.op("gpsimd", lambda e, u=u, q1=q1: e.tensor_mul(q1[:], u[:], u[:]),
                               deps=[uv] + ([q1_tok[ci]] if q1_tok[ci][0] else []))
                    q2v = p.op("scalar", lambda e, q1=q1: e.activation(q1[:], q1[:], AF.Identity,
                                                                scale=0.044715, bias=1.0), deps=[q1v])
                    q3v = p.op("vector", lambda e, u=u, q1=q1, q3=q3: e.tensor_mul(q3[:], u[:], q1[:]),
                               deps=[q2v] + ([q3_tok[ci]] if q3_tok[ci][0] else []))
                    q1_tok[ci] = q3v
                    sgv = p.op("scalar", lambda e, q3=q3, sgm=sgm: e.activation(sgm[:], q3[:], AF.Sigmoid,
                                                                scale=1.5957691216057308),
                               deps=[q3v] + ([sgm_tok[ci]] if sgm_tok[ci][0] else []))
                    q3_tok[ci] = sgv
                    gv = p.op("vector", lambda e, fc=fc, u=u, sgm=sgm: e.tensor_mul(
                        gbufL[:, fc * 512:(fc + 1) * 512], u[:], sgm[:]),
                        deps=[sgv] + ([gb_tok] if gb_tok[0] and fc < 2 else []))
                    u_tok[gi] = gv
                    sgm_tok[ci] = gv
                    glast = gv
                ylast = None
                for occ in range(CC):
                    ci = occ % 2
                    py = pY[ci]
                    mmY = None
                    for fc in range(32):
                        dep = [glast] + wl + ([pY_tok[ci]] if pY_tok[ci][0] and fc == 0 else [])
                        mmY = p.op("tensor", lambda e, py=py, fc=fc, occ=occ: e.matmul(
                            py[:], W2b[:, fc * D + occ * 128: fc * D + occ * 128 + 128],
                            gbufL[:, fc * 512:(fc + 1) * 512],
                            start=(fc == 0), stop=(fc == 31)), deps=dep, inc=(fc == 31))
                    eb = ev[eslot]
                    yv = p.op("scalar", lambda e, py=py, eb=eb, occ=occ: e.activation(
                        eb[:], py[:], AF.Identity, bias=b2cl[:, occ:occ + 1]),
                        deps=[mmY, bload] + ([ev_tok[eslot]] if ev_tok[eslot][0] else []))
                    pY_tok[ci] = yv
                    if path == 0:
                        st = p.dma("sync", g["bufY0"].ap()[b, occ, :, tt * 512:(tt + 1) * 512],
                                   eb[:], deps=[yv])
                        ev_tok[eslot] = st
                        ylast = st
                    else:
                        yb = y0t[eslot]
                        ly = p.dma("gpsimd", yb[:], g["bufY0"].ap()[b, occ, :, tt * 512:(tt + 1) * 512],
                                   deps=[y0_tok[eslot]] if y0_tok[eslot][0] else [])
                        d1 = p.op("vector", lambda e, yb=yb, eb=eb: e.tensor_sub(yb[:], yb[:], eb[:]),
                                  deps=[ly, yv])
                        d2 = p.op("vector", lambda e, yb=yb: e.tensor_mul(yb[:], yb[:], combP[:]),
                                  deps=[d1, cpt])
                        cp_tok = d2
                        d3 = p.op("gpsimd", lambda e, yb=yb, eb=eb: e.tensor_add(eb[:], eb[:], yb[:]),
                                  deps=[d2])
                        st = p.dma("sync", g["bufA"].ap()[b, occ, :, tt * 512:(tt + 1) * 512],
                                   eb[:], deps=[d3])
                        ev_tok[eslot] = st
                        y0_tok[eslot] = d3
                        ylast = st
                    eslot = 1 - eslot
                gb_tok = ylast
                xt_tok[xslot] = ylast
                xslot = 1 - xslot


# version 09
# speedup vs baseline: 1.1101x; 1.0521x over previous
import contextlib
import math
import numpy as np
import concourse.bass as bass
import concourse.mybir as mybir
from concourse.bass_utils import run_bass_kernel_spmd

F32 = mybir.dt.float32
BF16 = mybir.dt.bfloat16
AF = mybir.ActivationFunctionType
ALU = mybir.AluOpType
AX = mybir.AxisListType.X

B_TOT, S, D = 32, 2048, 1024
NCORE = 8
BC = B_TOT // NCORE
CC = D // 128
NT = S // 512
H, HD = 16, 64
SG = S + 8

ENGS = ("sync", "scalar", "vector", "tensor", "gpsimd")


class Prog:

    def __init__(self, nc):
        self.nc = nc
        self.es = contextlib.ExitStack()
        self.thunks = {e: [] for e in ENGS}
        self.semh = {}
        self.cnt = {}
        self.water = {e: {} for e in ENGS}
        for e in ENGS:
            self.mksem("p_" + e)
        self.mksem("d_sy")
        self.mksem("d_gp")
        self.mksem("f_sy")
        self.mksem("f_gp")
        self.mksem("bar")
        self.fences = {"sync": [], "gpsimd": []}

    def init_consts(self, buf):
        self.constbuf = buf
        self.cmap = {}

    def constcol(self, val, rows=128):
        if val not in self.cmap:
            i = len(self.cmap)
            tok = self.op("vector", lambda e, i=i, val=val: e.memset(
                self.constbuf[:, i:i + 1], val))
            self.cmap[val] = (i, tok)
        i, tok = self.cmap[val]
        return self.constbuf[0:rows, i:i + 1], tok

    def mksem(self, name):
        self.semh[name] = self.es.enter_context(self.nc.semaphore(name))
        self.cnt[name] = 0
        return name

    def _fence(self, q):
        dsem = "d_sy" if q == "sync" else "d_gp"
        fsem = "f_sy" if q == "sync" else "f_gp"
        total = self.cnt[dsem]
        self._wait_direct(q, dsem, total)
        self.cnt[fsem] += 1
        fh = self.semh[fsem]
        self.thunks[q].append(lambda e, fh=fh: e.nop().then_inc(fh, 1))
        self.fences[q].append((total, self.cnt[fsem]))
        return (fsem, self.cnt[fsem])

    def _wait_direct(self, eng, sem, val):
        if val <= 0:
            return
        if self.water[eng].get(sem, 0) >= val:
            return
        self.water[eng][sem] = val
        h = self.semh[sem]
        self.thunks[eng].append(lambda e, h=h, v=val: e.wait_ge(h, v))

    def _wait(self, eng, sem, val):
        if val <= 0:
            return
        if sem in ("d_sy", "d_gp"):
            q = "sync" if sem == "d_sy" else "gpsimd"
            if eng == q:
                self._wait_direct(eng, sem, self.cnt[sem])
                return
            fsem = "f_sy" if sem == "d_sy" else "f_gp"
            for (tot, fc) in self.fences[q]:
                if tot >= val:
                    self._wait_direct(eng, fsem, fc)
                    return
            ftok = self._fence(q)
            self._wait_direct(eng, ftok[0], ftok[1])
            return
        self._wait_direct(eng, sem, val)

    def op(self, eng, fn, deps=(), inc=True):
        for (s, v) in deps:
            self._wait(eng, s, v)
        sem = "p_" + eng
        if not inc:
            self.thunks[eng].append(lambda e, fn=fn: fn(e))
            return (sem, self.cnt[sem])
        self.cnt[sem] += 1
        tok = (sem, self.cnt[sem])
        h = self.semh[sem]
        self.thunks[eng].append(lambda e, fn=fn, h=h: fn(e).then_inc(h, 1))
        return tok

    def raw(self, eng, fn, deps=()):
        for (s, v) in deps:
            self._wait(eng, s, v)
        self.thunks[eng].append(fn)
        return None

    def dma(self, eng, out, in_, deps=(), **kw):
        for (s, v) in deps:
            self._wait(eng, s, v)
        sem = "d_sy" if eng == "sync" else "d_gp"
        assert eng in ("sync", "gpsimd")
        self.cnt[sem] += 16
        tok = (sem, self.cnt[sem])
        h = self.semh[sem]
        self.thunks[eng].append(
            lambda e, out=out, in_=in_, h=h, kw=kw: e.dma_start(out, in_, **kw).then_inc(h, 16))
        return tok

    def barrier(self):
        bh = self.semh["bar"]
        target = self.cnt["bar"] + len(ENGS)
        for e in ENGS:
            self._wait(e, "p_" + e, self.cnt["p_" + e])
            if e == "sync":
                self._wait(e, "d_sy", self.cnt["d_sy"])
            if e == "gpsimd":
                self._wait(e, "d_gp", self.cnt["d_gp"])
            self.thunks[e].append(lambda en, bh=bh: en.nop().then_inc(bh, 1))
        self.cnt["bar"] = target
        for e in ENGS:
            self._wait(e, "bar", target)

    def finalize(self):
        nc = self.nc
        with nc.Block() as block:
            @block.sync
            def _(e):
                for t in self.thunks["sync"]:
                    t(e)

            @block.scalar
            def _(e):
                for t in self.thunks["scalar"]:
                    t(e)

            @block.vector
            def _(e):
                for t in self.thunks["vector"]:
                    t(e)

            @block.tensor
            def _(e):
                for t in self.thunks["tensor"]:
                    t(e)

            @block.gpsimd
            def _(e):
                for t in self.thunks["gpsimd"]:
                    t(e)
        self.es.close()


_NSEQ = [0]


def _nm(name):
    _NSEQ[0] += 1
    return f"{name}_{_NSEQ[0]}"


def ts(i, n):
    return slice(i * n, (i + 1) * n)



def build_program(dbg_external=(), upto=99):
    nc = bass.Bass("TRN2", target_bir_lowering=False)
    g = {}

    def din(name, shape, dt=F32):
        g[name] = nc.dram_tensor(name, shape, dt, kind="ExternalInput")
        return g[name]

    def dscr(name, shape, dt=F32):
        kind = "ExternalOutput" if name in dbg_external else "Internal"
        g[name] = nc.dram_tensor(name, shape, dt, kind=kind)
        return g[name]

    din("x_fm", [BC, CC, 128, S])
    din("l0_wkT", [CC, 128, D]); din("l0_wvT", [CC, 128, D])
    din("l0_wrT", [CC, 128, D]); din("l0_woT", [CC, 128, D])
    din("l0_par", [128, 40])
    din("l0_n1w", [128, CC]); din("l0_n2w", [128, CC])
    din("qkvT", [CC, 128, 3072])
    din("l0_qkvb", [128, 24])
    din("l0_outwT", [CC, 128, D]); din("l0_outb", [128, CC])
    din("gateT", [CC, 128, 8])
    din("w1T", [8, CC, 128, 2048]); din("w2T", [8, CC, 128, 2048])
    din("w3T", [8, 16, 128, D])
    din("l0_b1", [8, 128, 16]); din("l0_b2", [8, 128, 16]); din("l0_b3", [8, 128, CC])
    din("rope_cos", [128, S]); din("rope_sin", [128, S])
    din("ropeRT", [128, 128])
    din("onehot8", [8, 8 * 128])
    din("l1_wq", [CC, 128, D]); din("l1_wk", [CC, 128, D])
    din("l1_wv", [CC, 128, D]); din("l1_wo", [CC, 128, D])
    din("l1_bo", [128, CC])
    din("alibias", [128, 1024])
    din("causalm", [128, 896])
    din("l1_W1", [CC, 128, 4096]); din("l1_b1f", [128, 32])
    din("l1_W2f", [32, 128, D]); din("l1_b2f", [128, CC])
    din("l1_g", [128, CC]); din("l1_be", [128, CC])
    din("l1_wr", [CC, 128, 8]); din("l1_br", [8, 1])

    dscr("xbf", [BC, CC, 128, SG], BF16)
    dscr("bufK", [BC, CC, 128, S])
    dscr("bufV", [BC, CC, 128, S])
    dscr("bufSR", [BC, CC, 128, S])
    dscr("bufA", [BC, CC, 128, S])
    dscr("bufX1", [BC, CC, 128, S])
    dscr("bufXN", [BC, CC, 128, S], BF16)
    dscr("bufX2", [BC, CC, 128, S])
    dscr("bufX3", [BC, CC, 128, S])
    dscr("bufX4", [BC, CC, 128, S])
    dscr("qkvTB", [CC, 128, 3072], BF16)
    dscr("outwTB", [CC, 128, D], BF16)
    dscr("gateTB", [CC, 128, 8], BF16)
    dscr("w1TB", [8, CC, 128, 2048], BF16)
    dscr("w2TB", [8, CC, 128, 2048], BF16)
    dscr("w3TB", [8, 16, 128, D], BF16)
    dscr("bufX6", [BC, CC, 128, S])
    dscr("bufY0", [BC, CC, 128, S])
    dscr("bufP0", [BC, 1, S])
    dscr("bufSMALL", [128, 48])
    out_fm = nc.dram_tensor("out_fm", [BC, CC, 128, S], F32, kind="ExternalOutput")
    g["out_fm"] = out_fm

    p = Prog(nc)
    sb = lambda name, shape, dt=F32: p.es.enter_context(nc.sbuf_tensor(name, shape, dt))

    p.init_consts(sb("constbuf", [128, 16]))
    par = sb("par", [128, 40])
    dpar = sb("dpar", [128, 64])
    t_par = p.dma("sync", par[:], g["l0_par"].ap())

    t0 = p.op("vector", lambda e: e.tensor_copy(dpar[:, 0:8], par[:, 16:24]), deps=[t_par])
    t1 = p.op("scalar", lambda e: e.activation(dpar[:, 8:16], par[:, 16:24], AF.Identity, scale=-1.0, bias=1.0), deps=[t_par])
    t2 = p.op("vector", lambda e: e.tensor_copy(dpar[:, 16:24], par[:, 24:32]), deps=[t_par])
    t3 = p.op("scalar", lambda e: e.activation(dpar[:, 24:32], par[:, 24:32], AF.Identity, scale=-1.0, bias=1.0), deps=[t_par])
    t4 = p.op("vector", lambda e: e.tensor_copy(dpar[:, 32:40], par[:, 32:40]), deps=[t_par])
    t5 = p.op("scalar", lambda e: e.activation(dpar[:, 40:48], par[:, 32:40], AF.Identity, scale=-1.0, bias=1.0), deps=[t_par])
    t6a = p.op("scalar", lambda e: e.activation(dpar[:, 48:56], par[:, 0:8], AF.Exp), deps=[t_par])
    t6 = p.op("scalar", lambda e: e.activation(dpar[:, 48:56], dpar[:, 48:56], AF.Exp, scale=-1.0), deps=[t6a])
    t7 = p.op("scalar", lambda e: e.activation(dpar[:, 56:64], par[:, 8:16], AF.Exp), deps=[t_par])
    dpar_tok = [t0, t1, t2, t3, t4, t5, t6, t7]

    def dcol(f, cc):
        return dpar[:, f * 8 + cc: f * 8 + cc + 1]

    zg = sb("zg", [128, 8], BF16)
    tz = p.op("vector", lambda e: e.memset(zg[:], 0.0))
    ph0_toks = []
    for b in range(BC):
        for cc in range(CC):
            ph0_toks.append(p.dma("gpsimd", g["xbf"].ap()[b, cc, :, 8:SG],
                                  g["x_fm"].ap()[b, cc, :, :]))
            ph0_toks.append(p.dma("gpsimd", g["xbf"].ap()[b, cc, :, 0:8], zg[:], deps=[tz]))
    p.barrier()

    with contextlib.ExitStack() as ph:
        psb = lambda name, shape, dt=F32: ph.enter_context(nc.sbuf_tensor(_nm(name), shape, dt))
        psum = lambda name, shape, dt=F32: ph.enter_context(nc.psum_tensor(_nm(name), shape, dt))
        wsc = {}
        wsc_toks = []
        wtmp = psb("wtmp", [128, D])
        tok_prev = dpar_tok
        for pi, (wname, fA, fB, nA, nB) in enumerate([
                ("l0_wkT", 0, 1, "wk1", "wk2"),
                ("l0_wvT", 2, 3, "wv1", "wv2"),
                ("l0_wrT", 4, 5, "wr1", "wr2")]):
            wsc[nA] = psb(nA, [128, CC * D], BF16)
            wsc[nB] = psb(nB, [128, CC * D], BF16)
            for cc in range(CC):
                tl = p.dma("sync", wtmp[:], g[wname].ap()[cc], deps=tok_prev)
                ta = p.op("vector", lambda e, nA=nA, cc=cc, fA=fA: e.tensor_scalar(
                    wsc[nA][:, ts(cc, D)], wtmp[:], dcol(fA, cc), None, ALU.mult), deps=[tl] + dpar_tok)
                tb = p.op("vector", lambda e, nB=nB, cc=cc, fB=fB: e.tensor_scalar(
                    wsc[nB][:, ts(cc, D)], wtmp[:], dcol(fB, cc), None, ALU.mult), deps=[tl] + dpar_tok)
                tok_prev = [ta, tb]
                wsc_toks.extend([ta, tb])

        xw = [psb(f"xw{i}", [128, CC * 513], BF16) for i in range(2)]
        xw_tok = [(None, 0), (None, 0)]
        pk = [psum(f"pk{i}", [128, 512]) for i in range(4)]
        pk_tok = [(None, 0)] * 4
        ev = [psb(f"ev{i}", [128, 512]) for i in range(4)]
        ev_tok = [(None, 0)] * 4
        slot = 0
        pslot = 0
        dst = {"k": g["bufK"], "v": g["bufV"], "r": g["bufSR"]}
        wpair = {"k": ("wk1", "wk2"), "v": ("wv1", "wv2"), "r": ("wr1", "wr2")}
        for b in range(BC):
            for tt in range(NT):
                xs = xw[slot]
                loads = []
                for cc in range(CC):
                    dep = [xw_tok[slot]] if xw_tok[slot][0] else []
                    loads.append(p.dma("sync", xs[:, cc * 513:(cc + 1) * 513],
                                       g["xbf"].ap()[b, cc, :, 8 + tt * 512 - 1: 8 + tt * 512 + 512],
                                       deps=dep))
                consumed = []
                for proj in ("k", "v", "r"):
                    w1n, w2n = wpair[proj]
                    for occ in range(CC):
                        pp = pk[pslot]
                        mm_tok = None
                        for icc in range(CC):
                            lhs1 = wsc[w1n][:, icc * D + occ * 128: icc * D + occ * 128 + 128]
                            lhs2 = wsc[w2n][:, icc * D + occ * 128: icc * D + occ * 128 + 128]
                            rhs1 = xs[:, icc * 513 + 1: icc * 513 + 513]
                            rhs2 = xs[:, icc * 513 + 0: icc * 513 + 512]
                            deps = loads + wsc_toks + ([pk_tok[pslot]] if pk_tok[pslot][0] else [])
                            mm_tok = p.op("tensor", lambda e, pp=pp, lhs1=lhs1, rhs1=rhs1, icc=icc: e.matmul(
                                pp[:], lhs1, rhs1, start=(icc == 0), stop=False), deps=deps, inc=False)
                            mm_tok = p.op("tensor", lambda e, pp=pp, lhs2=lhs2, rhs2=rhs2, icc=icc: e.matmul(
                                pp[:], lhs2, rhs2, start=False, stop=(icc == CC - 1)), deps=[], inc=(icc == CC - 1))
                        consumed.append(mm_tok)
                        ebuf = ev[pslot]
                        edep = [mm_tok] + ([ev_tok[pslot]] if ev_tok[pslot][0] else [])
                        if proj == "r":
                            et = p.op("scalar", lambda e, ebuf=ebuf, pp=pp: e.activation(
                                ebuf[:], pp[:], AF.Sigmoid), deps=edep)
                        else:
                            et = p.op("scalar", lambda e, ebuf=ebuf, pp=pp: e.copy(ebuf[:], pp[:]), deps=edep)
                        pk_tok[pslot] = et
                        ev_tok[pslot] = p.dma("sync", dst[proj].ap()[b, occ, :, tt * 512:(tt + 1) * 512],
                                              ebuf[:], deps=[et])
                        pslot = (pslot + 1) % 4
                xw_tok[slot] = consumed[-1]
                slot = 1 - slot
    p.barrier()

    for cc in range(CC):
        p.dma("gpsimd", g["qkvTB"].ap()[cc], g["qkvT"].ap()[cc])
        p.dma("gpsimd", g["outwTB"].ap()[cc], g["l0_outwT"].ap()[cc])
        p.dma("gpsimd", g["gateTB"].ap()[cc], g["gateT"].ap()[cc])
    for e in range(8):
        for cc in range(CC):
            p.dma("gpsimd", g["w1TB"].ap()[e, cc], g["w1T"].ap()[e, cc])
            p.dma("gpsimd", g["w2TB"].ap()[e, cc], g["w2T"].ap()[e, cc])
        for hc in range(16):
            p.dma("gpsimd", g["w3TB"].ap()[e, hc], g["w3T"].ap()[e, hc])

    with contextlib.ExitStack() as ph:
        psb = lambda name, shape, dt=F32: ph.enter_context(nc.sbuf_tensor(_nm(name), shape, dt))
        psum = lambda name, shape, dt=F32: ph.enter_context(nc.psum_tensor(_nm(name), shape, dt))
        ones = psb("ones2", [128, 2048])
        t_ones = p.op("vector", lambda e: e.memset(ones[:], 1.0))
        ewsl = psb("ewsl", [128, 2048])
        slK = psb("slK", [128, 2048]); slV = psb("slV", [128, 2048])
        slEK = psb("slEK", [128, 2048]); slEKV = psb("slEKV", [128, 2048])
        slP = psb("slP", [128, 2049]); slQ = psb("slQ", [128, 2049])
        slN = psb("slN", [128, 2048]); slD = psb("slD", [128, 2048])
        slR = psb("slR", [128, 2048]); slSR = psb("slSR", [128, 2048])
        rwkv = psb("rwkv", [128, CC * 2048], BF16)
        woTb = psb("woTb", [128, CC * D], BF16)
        wtm = psb("wtm", [128, D])
        tg1 = p.op("vector", lambda e: e.memset(slP[:, 0:1], 0.0))
        tg2 = p.op("vector", lambda e: e.memset(slQ[:, 0:1], 0.0))
        wo_tok = []
        tprev = (None, 0)
        for cc in range(CC):
            tl = p.dma("sync", wtm[:], g["l0_woT"].ap()[cc], deps=[tprev] if tprev[0] else [])
            tc = p.op("scalar", lambda e, cc=cc: e.copy(woTb[:, ts(cc, D)], wtm[:]), deps=[tl])
            tprev = tc
            wo_tok.append(tc)

        lt = {k: (None, 0) for k in ["K", "V", "EK", "EKV", "P", "Q", "N", "D", "R", "SR", "ew"]}
        rw_tok = {}
        pmm = [psum(f"po{i}", [128, 512]) for i in range(2)]
        pmm_tok = [(None, 0)] * 2
        evo = [psb(f"evo{i}", [128, 512]) for i in range(2)]
        evo_tok = [(None, 0)] * 2
        mslot = 0
        for b in range(BC):
            rwkv_consumed = rw_tok.get(b - 1, [])
            for cc in range(CC):
                dd = lambda k: [lt[k]] if lt[k][0] else []
                tK = p.dma("sync", slK[:], g["bufK"].ap()[b, cc], deps=dd("K"))
                tV = p.dma("sync", slV[:], g["bufV"].ap()[b, cc], deps=dd("V"))
                tR = p.dma("sync", slSR[:], g["bufSR"].ap()[b, cc], deps=dd("SR"))
                tew = p.op("scalar", lambda e, cc=cc: e.activation(
                    ewsl[:], ones[:, 0:2048], AF.Identity, scale=dcol(6, cc)),
                    deps=dd("ew") + dpar_tok + [t_ones])
                tEK = p.op("scalar", lambda e: e.activation(slEK[:], slK[:], AF.Exp),
                           deps=[tK] + dd("EK"))
                lt["K"] = tEK
                tEKV = p.op("vector", lambda e: e.tensor_mul(slEKV[:], slEK[:], slV[:]),
                            deps=[tEK, tV] + dd("EKV"))
                lt["V"] = tEKV
                tQ = p.op("vector", lambda e: e.tensor_tensor_scan(
                    slQ[:, 1:2049], ewsl[:], slEK[:], 0.0, ALU.mult, ALU.add),
                    deps=[tew, tEK, tg2] + dd("Q"))
                tP = p.op("vector", lambda e: e.tensor_tensor_scan(
                    slP[:, 1:2049], ewsl[:], slEKV[:], 0.0, ALU.mult, ALU.add),
                    deps=[tew, tEKV, tg1] + dd("P"))
                lt["ew"] = tP
                tD = p.op("vector", lambda e, cc=cc: e.scalar_tensor_tensor(
                    slD[:], slEK[:], dcol(7, cc), slQ[:, 0:2048], ALU.mult, ALU.add),
                    deps=[tQ] + dd("D"))
                lt["EK"] = tD
                lt["Q"] = tD
                tN = p.op("vector", lambda e, cc=cc: e.scalar_tensor_tensor(
                    slN[:], slEKV[:], dcol(7, cc), slP[:, 0:2048], ALU.mult, ALU.add),
                    deps=[tP] + dd("N"))
                lt["EKV"] = tN
                lt["P"] = tN
                tL = p.op("scalar", lambda e: e.activation(slR[:], slD[:], AF.Ln),
                          deps=[tD] + dd("R"))
                tRc = p.op("scalar", lambda e: e.activation(slR[:], slR[:], AF.Exp, scale=-1.0),
                           deps=[tL])
                lt["D"] = tRc
                tY = p.op("vector", lambda e: e.tensor_mul(slN[:], slN[:], slR[:]),
                          deps=[tRc, tN])
                lt["R"] = tY
                lt["N"] = tY
                trw = p.op("vector", lambda e, cc=cc: e.tensor_mul(
                    rwkv[:, ts(cc, 2048)], slN[:], slSR[:]),
                    deps=[tY, tR] + rwkv_consumed)
                lt["SR"] = trw
                lt["N"] = trw
                rw_tok.setdefault(b, []).append(trw)
            consumed = []
            for tt in range(NT):
                for occ in range(CC):
                    pp = pmm[mslot]
                    mm = None
                    for icc in range(CC):
                        deps = [rw_tok[b][icc]] + wo_tok + ([pmm_tok[mslot]] if pmm_tok[mslot][0] else [])
                        mm = p.op("tensor", lambda e, pp=pp, icc=icc, occ=occ, tt=tt: e.matmul(
                            pp[:], woTb[:, icc * D + occ * 128: icc * D + occ * 128 + 128],
                            rwkv[:, icc * 2048 + tt * 512: icc * 2048 + tt * 512 + 512],
                            start=(icc == 0), stop=(icc == CC - 1)), deps=deps, inc=(icc == CC - 1))
                    consumed.append(mm)
                    eb = evo[mslot]
                    edep = [mm] + ([evo_tok[mslot]] if evo_tok[mslot][0] else [])
                    et = p.op("scalar", lambda e, eb=eb, pp=pp: e.copy(eb[:], pp[:]), deps=edep)
                    pmm_tok[mslot] = et
                    evo_tok[mslot] = p.dma("sync", g["bufA"].ap()[b, occ, :, tt * 512:(tt + 1) * 512],
                                           eb[:], deps=[et])
                    mslot = 1 - mslot
            rw_tok[b] = consumed
    p.barrier()

    emit_resid_ln(p, nc, g, src1="x_fm", src2="bufA", dst="bufX1", dst2=None)
    p.barrier()
    cur = "bufX1"

    if upto >= 4:
        emit_rms(p, nc, g, src="bufX1", dst="bufXN", wname="l0_n1w", eps=1e-6)
        p.barrier()
        emit_l0attn(p, nc, g)
        p.barrier()
        cur = "bufX2"
    if upto >= 5:
        emit_rms(p, nc, g, src="bufX2", dst="bufXN", wname="l0_n2w", eps=1e-6)
        p.barrier()
        emit_moe(p, nc, g)
        p.barrier()
        emit_resid_ln(p, nc, g, src1="bufX1", src2="bufX3", dst="bufX4")
        p.barrier()
        cur = "bufX4"
    if upto >= 7:
        emit_l1attn(p, nc, g)
        p.barrier()
        emit_resid_ln(p, nc, g, src1="bufX4", src2="bufA", dst="bufX6")
        p.barrier()
        cur = "bufX6"
    if upto >= 9:
        emit_l1router(p, nc, g)
        p.barrier()
        emit_thresholds(p, nc, g)
        p.barrier()
        emit_l1ffn(p, nc, g, path=0)
        p.barrier()
        emit_l1ffn(p, nc, g, path=1)
        p.barrier()
        emit_resid_ln(p, nc, g, src1="bufX6", src2="bufA", dst="out_fm")
        p.barrier()
        cur = None

    if cur is None:
        p.finalize()
        return nc
    with contextlib.ExitStack() as ph:
        psb = lambda name, shape, dt=F32: ph.enter_context(nc.sbuf_tensor(_nm(name), shape, dt))
        cp = psb("cpb", [128, 2048])
        tprev = (None, 0)
        for b in range(BC):
            for cc in range(CC):
                tl = p.dma("sync", cp[:], g[cur].ap()[b, cc], deps=[tprev] if tprev[0] else [])
                tprev = p.dma("sync", g["out_fm"].ap()[b, cc], cp[:], deps=[tl])
    p.barrier()
    p.finalize()
    return nc


def emit_resid_ln(p, nc, g, src1, src2, dst, dst2=None, eps=1e-5):
    with contextlib.ExitStack() as ph:
        psb = lambda name, shape, dt=F32: ph.enter_context(nc.sbuf_tensor(_nm(name), shape, dt))
        psum = lambda name, shape, dt=F32: ph.enter_context(nc.psum_tensor(_nm(name), shape, dt))
        xs = [psb(f"ln_x{i}", [128, CC * 512]) for i in range(2)]
        ys = [psb(f"ln_y{i}", [128, CC * 512]) for i in range(2)]
        sqs = [psb(f"ln_sq{i}", [128, 512]) for i in range(2)]
        stM = psb("ln_stM", [1, 512])
        stV = psb("ln_stV", [1, 512])
        stT = psb("ln_stT", [1, 512])
        onec = psb("ln_one", [128, 1])
        one1 = psb("ln_one1", [1, 128])
        t_on = p.op("vector", lambda e: e.memset(onec[:], 1.0))
        t_on1 = p.op("vector", lambda e: e.memset(one1[:], 1.0))
        pS = psum("ln_pS", [1, 512])
        pQ = psum("ln_pQ", [1, 512])
        pB = psum("ln_pB", [128, 512])
        pM = psum("ln_pM", [128, 512])
        xs_tok = [(None, 0)] * 2
        ys_tok = [(None, 0)] * 2
        sq_tok = [(None, 0)] * 2
        stat_tok = (None, 0)
        pS_tok = (None, 0)
        pB_tok = (None, 0)
        pM_tok = (None, 0)
        bB_prev = (None, 0)
        bM_prev = (None, 0)
        slot = 0
        for b in range(BC):
            for tt in range(NT):
                x = xs[slot]
                y = ys[slot]
                loads = []
                adds = []
                for cc in range(CC):
                    dep = [xs_tok[slot]] if xs_tok[slot][0] else []
                    l1 = p.dma("sync", x[:, ts(cc, 512)],
                               g[src1].ap()[b, cc, :, tt * 512:(tt + 1) * 512], deps=dep)
                    if src2 is not None:
                        dep2 = [ys_tok[slot]] if ys_tok[slot][0] else []
                        l2 = p.dma("sync", y[:, ts(cc, 512)],
                                   g[src2].ap()[b, cc, :, tt * 512:(tt + 1) * 512], deps=dep2)
                        ta = p.op("vector", lambda e, cc=cc, x=x, y=y: e.tensor_add(
                            x[:, ts(cc, 512)], x[:, ts(cc, 512)], y[:, ts(cc, 512)]),
                            deps=[l1, l2])
                        if dst2 is not None:
                            adds.append(p.dma("sync", g[dst2].ap()[b, cc, :, tt * 512:(tt + 1) * 512],
                                        x[:, ts(cc, 512)], deps=[ta]))
                    else:
                        ta = l1
                    loads.append(ta)
                mmS = None
                for cc in range(CC):
                    dep = [loads[cc], t_on] + ([pS_tok] if pS_tok[0] else [])
                    mmS = p.op("tensor", lambda e, cc=cc, x=x: e.matmul(
                        pS[:], onec[:], x[:, ts(cc, 512)],
                        start=(cc == 0), stop=(cc == CC - 1)), deps=dep, inc=(cc == CC - 1))
                mmQ = None
                for cc in range(CC):
                    sq = sqs[cc % 2]
                    dep = [loads[cc]] + ([sq_tok[cc % 2]] if sq_tok[cc % 2][0] else [])
                    tq = p.op("scalar", lambda e, cc=cc, x=x, sq=sq: e.activation(
                        sq[:], x[:, ts(cc, 512)], AF.Square), deps=dep)
                    mmQ = p.op("tensor", lambda e, cc=cc, sq=sq: e.matmul(
                        pQ[:], onec[:], sq[:],
                        start=(cc == 0), stop=(cc == CC - 1)), deps=[tq, mmS])
                    sq_tok[cc % 2] = mmQ
                dep0 = [mmS, mmQ] + ([stat_tok] if stat_tok[0] else [])
                s0 = p.op("vector", lambda e: e.tensor_scalar(
                    stM[:], pS[:], 1.0 / D, None, ALU.mult), deps=dep0)
                s1 = p.op("vector", lambda e: e.tensor_scalar(
                    stV[:], pQ[:], 1.0 / D, None, ALU.mult),
                    deps=[mmQ, s0] + ([bB_prev] if bB_prev[0] else []))
                pS_tok = s1
                s2 = p.op("vector", lambda e: e.tensor_mul(stT[:], stM[:], stM[:]),
                          deps=[s0])
                s3 = p.op("vector", lambda e: e.tensor_sub(stV[:], stV[:], stT[:]),
                          deps=[s1, s2])
                epsc, epst = p.constcol(eps, rows=1)
                s4 = p.op("scalar", lambda e, epsc=epsc: e.activation(stV[:], stV[:], AF.Ln, bias=epsc),
                          deps=[s3, epst])
                s5 = p.op("scalar", lambda e: e.activation(stV[:], stV[:], AF.Exp, scale=-0.5),
                          deps=[s4])
                s6 = p.op("vector", lambda e: e.tensor_mul(stT[:], stM[:], stV[:]),
                          deps=[s5, s2] + ([bM_prev] if bM_prev[0] else []))
                stat_tok = s6
                depB = [s5, t_on1] + ([pB_tok] if pB_tok[0] else [])
                bB = p.op("tensor", lambda e: e.matmul(pB[:], one1[:], stV[:],
                                                       start=True, stop=True), deps=depB)
                depM = [s6, t_on1, bB] + ([pM_tok] if pM_tok[0] else [])
                bM = p.op("tensor", lambda e: e.matmul(pM[:], one1[:], stT[:],
                                                       start=True, stop=True), deps=depM)
                bB_prev, bM_prev = bB, bM
                outs = []
                for cc in range(CC):
                    dep = [bB, loads[cc]] + ([ys_tok[slot]] if ys_tok[slot][0] else [])
                    m1 = p.op("vector", lambda e, cc=cc, x=x, y=y: e.tensor_mul(
                        y[:, ts(cc, 512)], x[:, ts(cc, 512)], pB[:]), deps=dep)
                    m2 = p.op("vector", lambda e, cc=cc, y=y: e.tensor_sub(
                        y[:, ts(cc, 512)], y[:, ts(cc, 512)], pM[:]), deps=[m1, bM])
                    st = p.dma("sync", g[dst].ap()[b, cc, :, tt * 512:(tt + 1) * 512],
                               y[:, ts(cc, 512)], deps=[m2])
                    outs.append(st)
                    pM_tok = m2
                    pB_tok = m2
                xs_tok[slot] = outs[-1]
                ys_tok[slot] = outs[-1]
                slot = 1 - slot



def _prep_inputs(inputs):
    f = np.float32
    w = {}
    x = np.ascontiguousarray(inputs["L0_attn_x"], dtype=f)
    x_fm = np.ascontiguousarray(x.transpose(0, 2, 1)).reshape(B_TOT, CC, 128, S)

    def T2(a):
        return np.ascontiguousarray(np.ascontiguousarray(a.T, dtype=f).reshape(CC, 128, -1))

    def NAT(a):
        return np.ascontiguousarray(np.ascontiguousarray(a, dtype=f).reshape(CC, 128, -1))

    w["l0_wkT"] = T2(inputs["L0_attn_Wk"])
    w["l0_wvT"] = T2(inputs["L0_attn_Wv"])
    w["l0_wrT"] = T2(inputs["L0_attn_Wr"])
    w["l0_woT"] = T2(inputs["L0_attn_Wo"])
    par = np.zeros((128, 40), f)
    for i, nm in enumerate(["L0_attn_time_decay", "L0_attn_time_first",
                            "L0_attn_time_mix_k", "L0_attn_time_mix_v", "L0_attn_time_mix_r"]):
        v = np.asarray(inputs[nm], f).reshape(-1)
        par[:, i * 8:(i + 1) * 8] = v.reshape(CC, 128).T
    w["l0_par"] = par
    w["l0_n1w"] = np.ascontiguousarray(np.asarray(inputs["L0_ffn_norm1_w"], f).reshape(CC, 128).T)
    w["l0_n2w"] = np.ascontiguousarray(np.asarray(inputs["L0_ffn_norm2_w"], f).reshape(CC, 128).T)
    w["qkvT"] = np.ascontiguousarray(np.asarray(inputs["L0_ffn_qkv_w"], f).T.reshape(CC, 128, 3072))
    w["l0_qkvb"] = np.ascontiguousarray(np.asarray(inputs["L0_ffn_qkv_b"], f).reshape(24, 128).T)
    w["l0_outwT"] = T2(inputs["L0_ffn_out_w"])
    w["l0_outb"] = np.ascontiguousarray(np.asarray(inputs["L0_ffn_out_b"], f).reshape(CC, 128).T)
    w["gateT"] = np.ascontiguousarray(np.asarray(inputs["L0_ffn_gate_w"], f).T.reshape(CC, 128, 8))
    w["w1T"] = np.ascontiguousarray(np.asarray(inputs["L0_ffn_w1"], f).transpose(0, 2, 1).reshape(8, CC, 128, 2048))
    w["w2T"] = np.ascontiguousarray(np.asarray(inputs["L0_ffn_w2"], f).transpose(0, 2, 1).reshape(8, CC, 128, 2048))
    w["w3T"] = np.ascontiguousarray(np.asarray(inputs["L0_ffn_w3"], f).transpose(0, 2, 1).reshape(8, 16, 128, D))
    w["l0_b1"] = np.ascontiguousarray(np.asarray(inputs["L0_ffn_b1"], f).reshape(8, 16, 128).transpose(0, 2, 1))
    w["l0_b2"] = np.ascontiguousarray(np.asarray(inputs["L0_ffn_b2"], f).reshape(8, 16, 128).transpose(0, 2, 1))
    w["l0_b3"] = np.ascontiguousarray(np.asarray(inputs["L0_ffn_b3"], f).reshape(8, CC, 128).transpose(0, 2, 1))
    inv_freq = 1.0 / 10000.0 ** (np.arange(0, 32, dtype=np.float64) * 2.0 / 64.0)
    tgrid = np.arange(S, dtype=np.float64)
    ang = tgrid[None, :] * inv_freq[:, None]
    cos32 = np.cos(ang).astype(f); sin32 = np.sin(ang).astype(f)
    w["rope_cos"] = np.ascontiguousarray(np.tile(cos32, (4, 1)))
    w["rope_sin"] = np.ascontiguousarray(np.tile(sin32, (4, 1)))
    RT = np.zeros((128, 128), f)
    for blk in (0, 64):
        for i in range(32):
            RT[blk + i, blk + i + 32] = 1.0
            RT[blk + i + 32, blk + i] = -1.0
    w["ropeRT"] = RT
    oh = np.zeros((8, 8 * 128), f)
    for e_ in range(8):
        oh[e_, e_ * 128:(e_ + 1) * 128] = 1.0
    w["onehot8"] = oh
    w["l1_wq"] = NAT(inputs["L1_attn_Wq"]); w["l1_wk"] = NAT(inputs["L1_attn_Wk"])
    w["l1_wv"] = NAT(inputs["L1_attn_Wv"]); w["l1_wo"] = NAT(inputs["L1_attn_Wo"])
    w["l1_bo"] = np.ascontiguousarray(np.asarray(inputs["L1_attn_bo"], f).reshape(CC, 128).T)
    def get_slopes(n):
        def pow2(n_):
            start = 2.0 ** (-(2.0 ** (-(math.log2(n_) - 3))))
            return [start * start ** i for i in range(n_)]
        if math.log2(n).is_integer():
            return pow2(n)
        c = 2 ** math.floor(math.log2(n))
        return pow2(c) + get_slopes(2 * c)[0::2][:n - c]
    slopes = np.asarray(get_slopes(16), np.float64)
    ab = np.zeros((128, 1024), np.float64)
    prow = np.arange(128)
    for h in range(16):
        a8 = slopes[h] / 8.0
        for kc in range(16):
            for j in range(4):
                ab[:, h * 64 + kc * 4 + j] = a8 * (128 * kc + prow) - 20.0 - a8 * 512.0 * j
    w["alibias"] = ab.astype(f)
    cm = np.zeros((128, 896), f)
    jj = np.arange(896) - 384
    cm[:, :] = (jj[None, :] >= prow[:, None]).astype(f)
    w["causalm"] = cm
    w["l1_W1"] = np.ascontiguousarray(np.asarray(inputs["L1_ffn_W1"], f).reshape(CC, 128, 4096))
    w["l1_b1f"] = np.ascontiguousarray(np.asarray(inputs["L1_ffn_b1"], f).reshape(32, 128).T)
    w["l1_W2f"] = np.ascontiguousarray(np.asarray(inputs["L1_ffn_W2"], f).reshape(32, 128, D))
    w["l1_b2f"] = np.ascontiguousarray(np.asarray(inputs["L1_ffn_b2"], f).reshape(CC, 128).T)
    w["l1_g"] = np.ascontiguousarray(np.asarray(inputs["L1_ffn_ln_gamma"], f).reshape(CC, 128).T)
    w["l1_be"] = np.ascontiguousarray(np.asarray(inputs["L1_ffn_ln_beta"], f).reshape(CC, 128).T)
    w["l1_wr"] = np.ascontiguousarray(np.asarray(inputs["L1_ffn_Wr"], f).reshape(CC, 128, 8))
    w["l1_br"] = np.ascontiguousarray(np.asarray(inputs["L1_ffn_br"], f).reshape(8, 1))
    return x_fm, w


def kernel(**inputs):
    x_fm, w = _prep_inputs(inputs)
    nc = build_program()
    in_maps = []
    for c in range(NCORE):
        m = {"x_fm": np.ascontiguousarray(x_fm[c * BC:(c + 1) * BC])}
        m.update(w)
        in_maps.append(m)
    res = run_bass_kernel_spmd(nc, in_maps, core_ids=list(range(NCORE)))
    shards = [r["out_fm"] for r in res.results]
    out = np.concatenate(shards, axis=0)
    out = out.reshape(B_TOT, D, S).transpose(0, 2, 1)
    return np.ascontiguousarray(out, dtype=np.float32)


def emit_rms(p, nc, g, src, dst, wname, eps):
    with contextlib.ExitStack() as ph:
        psb = lambda name, shape, dt=F32: ph.enter_context(nc.sbuf_tensor(_nm(name), shape, dt))
        psum = lambda name, shape, dt=F32: ph.enter_context(nc.psum_tensor(_nm(name), shape, dt))
        xs = [psb(f"rm_x{i}", [128, CC * 512]) for i in range(2)]
        ys = [psb(f"rm_y{i}", [128, CC * 512], BF16) for i in range(2)]
        sqs = [psb(f"rm_sq{i}", [128, 512]) for i in range(2)]
        stV = psb("rm_stV", [1, 512])
        wsb = psb("rm_w", [128, CC])
        onec = psb("rm_one", [128, 1])
        one1 = psb("rm_one1", [1, 128])
        pQ = psum("rm_pQ", [1, 512])
        pB = psum("rm_pB", [128, 512])
        t_on = p.op("vector", lambda e: e.memset(onec[:], 1.0))
        t_on1 = p.op("vector", lambda e: e.memset(one1[:], 1.0))
        t_w = p.dma("sync", wsb[:], g[wname].ap())
        epsc, epst = p.constcol(eps, rows=1)
        xs_tok = [(None, 0)] * 2
        ys_tok = [(None, 0)] * 2
        sq_tok = [(None, 0)] * 2
        stV_tok = (None, 0)
        pQ_tok = (None, 0)
        pB_tok = (None, 0)
        slot = 0
        for b in range(BC):
            for tt in range(NT):
                x = xs[slot]; y = ys[slot]
                loads = []
                for cc in range(CC):
                    dep = [xs_tok[slot]] if xs_tok[slot][0] else []
                    loads.append(p.dma("sync", x[:, ts(cc, 512)],
                                       g[src].ap()[b, cc, :, tt * 512:(tt + 1) * 512], deps=dep))
                mmQ = None
                for cc in range(CC):
                    sq = sqs[cc % 2]
                    dep = [loads[cc]] + ([sq_tok[cc % 2]] if sq_tok[cc % 2][0] else [])
                    tq = p.op("scalar", lambda e, cc=cc, x=x, sq=sq: e.activation(
                        sq[:], x[:, ts(cc, 512)], AF.Square), deps=dep)
                    dep2 = [tq, t_on] + ([pQ_tok] if pQ_tok[0] and cc == 0 else [])
                    mmQ = p.op("tensor", lambda e, cc=cc, sq=sq: e.matmul(
                        pQ[:], onec[:], sq[:],
                        start=(cc == 0), stop=(cc == CC - 1)), deps=dep2)
                    sq_tok[cc % 2] = mmQ
                dep0 = [mmQ] + ([stV_tok] if stV_tok[0] else [])
                s1 = p.op("vector", lambda e: e.tensor_scalar(
                    stV[:], pQ[:], 1.0 / D, None, ALU.mult), deps=dep0)
                pQ_tok = s1
                s4 = p.op("scalar", lambda e: e.activation(stV[:], stV[:], AF.Ln, bias=epsc),
                          deps=[s1, epst])
                s5 = p.op("scalar", lambda e: e.activation(stV[:], stV[:], AF.Exp, scale=-0.5),
                          deps=[s4])
                depB = [s5, t_on1] + ([pB_tok] if pB_tok[0] else [])
                bB = p.op("tensor", lambda e: e.matmul(pB[:], one1[:], stV[:],
                                                       start=True, stop=True), deps=depB)
                stV_tok = bB
                outs = []
                m2 = None
                for cc in range(CC):
                    dep = [bB, loads[cc], t_w] + ([ys_tok[slot]] if ys_tok[slot][0] else [])
                    m2 = p.op("vector", lambda e, cc=cc, x=x, y=y: e.scalar_tensor_tensor(
                        y[:, ts(cc, 512)], x[:, ts(cc, 512)], wsb[:, cc:cc + 1], pB[:],
                        ALU.mult, ALU.mult), deps=dep)
                    outs.append(p.dma("sync", g[dst].ap()[b, cc, :, tt * 512:(tt + 1) * 512],
                                      y[:, ts(cc, 512)], deps=[m2]))
                pB_tok = m2
                xs_tok[slot] = outs[-1]
                ys_tok[slot] = outs[-1]
                slot = 1 - slot


def emit_l0attn(p, nc, g):
    neg20, t20 = p.constcol(-20.0)
    with contextlib.ExitStack() as ph:
        psb = lambda name, shape, dt=F32: ph.enter_context(nc.sbuf_tensor(_nm(name), shape, dt))
        psum = lambda name, shape, dt=F32: ph.enter_context(nc.psum_tensor(_nm(name), shape, dt))
        qkvTb = psb("a4_qkvT", [128, CC * 3072], BF16)
        outwTb = psb("a4_outwT", [128, CC * D], BF16)
        cosb = psb("a4_cos", [128, S], BF16)
        sinb = psb("a4_sin", [128, S], BF16)
        rtb = psb("a4_rt", [128, 128], BF16)
        qkvb = psb("a4_qb", [128, 24])
        qb8 = psb("a4_qb8", [128, CC])
        outb = psb("a4_ob", [128, CC])
        xnb = psb("a4_xn", [128, CC * S], BF16)
        vh = psb("a4_vh", [128, 16 * 16 * 65], BF16)
        ctxb = psb("a4_ctx", [128, CC * S], BF16)
        qh2 = psb("a4_qh", [128, S], BF16)
        kh2 = psb("a4_kh", [128, S], BF16)
        tmpA = psb("a4_tA", [128, 512])
        tmpB = psb("a4_tB", [128, 512])
        tmpctx = psb("a4_tc", [64, 512])
        rec = psb("a4_rec", [1, 512])
        one1 = psb("a4_one1", [1, 128])
        pts = [psb(f"a4_pt{i}", [128, 1024], BF16) for i in range(2)]
        evo = [psb(f"a4_ev{i}", [128, 512]) for i in range(2)]
        xres = [psb(f"a4_xr{i}", [128, 512]) for i in range(2)]
        ppq = [psum(f"a4_ppq{i}", [128, 1024]) for i in range(2)]
        pctxs = [psum(f"a4_pctx{i}", [128, 512]) for i in range(2)]
        pbc = psum("a4_pbc", [128, 512])
        wl = []
        for cc in range(CC):
            wl.append(p.dma("sync", qkvTb[:, cc * 3072:(cc + 1) * 3072], g["qkvTB"].ap()[cc]))
            wl.append(p.dma("sync", outwTb[:, ts(cc, D)], g["outwTB"].ap()[cc]))
        wl.append(p.dma("gpsimd", cosb[:], g["rope_cos"].ap()))
        wl.append(p.dma("gpsimd", sinb[:], g["rope_sin"].ap()))
        wl.append(p.dma("gpsimd", rtb[:], g["ropeRT"].ap()))
        wl.append(p.dma("sync", qkvb[:], g["l0_qkvb"].ap()))
        wl.append(p.dma("sync", outb[:], g["l0_outb"].ap()))
        t_q8 = p.op("scalar", lambda e: e.activation(qb8[:], qkvb[:, 0:8], AF.Identity, scale=0.125),
                    deps=wl)
        t_on1 = p.op("vector", lambda e: e.memset(one1[:], 1.0))
        t_vones = p.op("vector", lambda e: e.memset(vh[:, 64:16640:65], 1.0))

        pp_tok = [(None, 0)] * 4
        pt_tok = [(None, 0)] * 2
        evo_tok = [(None, 0)] * 2
        xres_tok = [(None, 0)] * 2
        xnb_tok = (None, 0)
        vh_tok = (None, 0)
        ctxb_tok = (None, 0)
        qh_tok = (None, 0)
        kh_tok = (None, 0)
        tA_tok = (None, 0); tB_tok = (None, 0)
        tctx_tok = (None, 0); rec_tok = (None, 0)
        pctx_tok = [(None, 0)] * 2; pbc_tok = (None, 0)
        def pbank(j):
            return ppq[j // 2][:, (j % 2) * 512:(j % 2) * 512 + 512]
        pslot = 0; qslot = 0; ptslot = 0; eslot = 0
        for b in range(BC):
            xl = []
            for cc in range(CC):
                dep = [xnb_tok] if xnb_tok[0] else []
                xl.append(p.dma("sync", xnb[:, cc * S:(cc + 1) * S],
                                g["bufXN"].ap()[b, cc], deps=dep))
            vtoks = []
            for kc in range(16):
                for vc in range(2):
                    pv = pbank(pslot)
                    mm = None
                    for icc in range(CC):
                        dep = xl + wl + ([pp_tok[pslot]] if pp_tok[pslot][0] else [])
                        mm = p.op("tensor", lambda e, pv=pv, icc=icc, kc=kc, vc=vc: e.matmul(
                            pv, xnb[:, icc * S + kc * 128: icc * S + kc * 128 + 128],
                            qkvTb[:, icc * 3072 + 2048 + vc * 512: icc * 3072 + 2048 + vc * 512 + 512],
                            start=(icc == 0), stop=(icc == CC - 1)), deps=dep, inc=(icc == CC - 1))
                    last = None
                    for j in range(8):
                        h = vc * 8 + j
                        dep = [mm, t_vones] + ([vh_tok] if vh_tok[0] else [])
                        last = p.op("vector", lambda e, pv=pv, j=j, kc=kc, h=h: e.tensor_copy(
                            vh[:, (kc * 16 + h) * 65: (kc * 16 + h) * 65 + 64],
                            pv[:, j * 64: j * 64 + 64]), deps=dep)
                        vtoks.append(last)
                    pp_tok[pslot] = last
                    pslot = (pslot + 1) % 4
            av_last = []
            qk_last = []
            qk_cons = (None, 0)
            for occ in range(CC):
                for (which, colbase) in (("q", 0), ("k", 1024)):
                    buf = qh2 if which == "q" else kh2
                    btok = qh_tok if which == "q" else kh_tok
                    for tt in range(NT):
                        pq = pbank(pslot)
                        mm = None
                        for icc in range(CC):
                            dep = xl + wl + ([pp_tok[pslot]] if pp_tok[pslot][0] else []) + ([btok] if btok[0] else [])
                            mm = p.op("tensor", lambda e, pq=pq, icc=icc, occ=occ, tt=tt, colbase=colbase: e.matmul(
                                pq, qkvTb[:, icc * 3072 + colbase + occ * 128: icc * 3072 + colbase + occ * 128 + 128],
                                xnb[:, icc * S + tt * 512: icc * S + tt * 512 + 512],
                                start=(icc == 0), stop=(icc == CC - 1)), deps=dep, inc=(icc == CC - 1))
                        qk_last.append(mm)
                        wardep = [qk_cons] if qk_cons[0] else []
                        if which == "q":
                            et = p.op("scalar", lambda e, pq=pq, buf=buf, tt=tt, occ=occ: e.activation(
                                buf[:, tt * 512:(tt + 1) * 512], pq, AF.Identity,
                                scale=0.125, bias=qb8[:, occ:occ + 1]), deps=[mm, t_q8] + wardep)
                        else:
                            et = p.op("scalar", lambda e, pq=pq, buf=buf, tt=tt, occ=occ: e.activation(
                                buf[:, tt * 512:(tt + 1) * 512], pq, AF.Identity,
                                bias=qkvb[:, 8 + occ:9 + occ]), deps=[mm] + wardep)
                        pp_tok[pslot] = et
                        pslot = (pslot + 1) % 4
                        pr = pbank(pslot)
                        dep = [et] + ([pp_tok[pslot]] if pp_tok[pslot][0] else [])
                        mmr = p.op("tensor", lambda e, pr=pr, buf=buf, tt=tt: e.matmul(
                            pr, rtb[:], buf[:, tt * 512:(tt + 1) * 512],
                            start=True, stop=True), deps=dep + wl)
                        a1t = p.op("vector", lambda e, buf=buf, tt=tt: e.tensor_mul(
                            tmpA[:], buf[:, tt * 512:(tt + 1) * 512], cosb[:, tt * 512:(tt + 1) * 512]),
                            deps=[et] + ([tA_tok] if tA_tok[0] else []))
                        b1t = p.op("vector", lambda e, pr=pr, tt=tt: e.tensor_mul(
                            tmpB[:], pr, sinb[:, tt * 512:(tt + 1) * 512]),
                            deps=[mmr] + ([tB_tok] if tB_tok[0] else []))
                        c1t = p.op("vector", lambda e, buf=buf, tt=tt: e.tensor_add(
                            buf[:, tt * 512:(tt + 1) * 512], tmpA[:], tmpB[:]),
                            deps=[a1t, b1t])
                        tA_tok = c1t; tB_tok = c1t
                        pp_tok[pslot] = c1t
                        pslot = (pslot + 1) % 4
                        if which == "q":
                            qh_tok = c1t
                        else:
                            kh_tok = c1t
                for qt in range(NT):
                    avmm = [None, None]
                    sq_pend = {}

                    def emit_S(kc, qt=qt, occ=occ):
                        nonlocal qslot, qk_cons
                        pq = ppq[qslot]
                        smm = None
                        for hh in range(2):
                            hr = 64 * hh
                            wdep = [t for t in (pp_tok[2 * qslot], pp_tok[2 * qslot + 1]) if t[0]] if hh == 0 else []
                            dep = [qh_tok, kh_tok] + wdep
                            smm = p.op("tensor", lambda e, pq=pq, hr=hr, kc=kc, qt=qt, hh=hh: e.matmul(
                                pq[0:128, hh * 512:(hh + 1) * 512],
                                kh2[hr:hr + 64, kc * 128:(kc + 1) * 128],
                                qh2[hr:hr + 64, qt * 512:(qt + 1) * 512],
                                start=True, stop=True), deps=dep, inc=(hh == 1))
                        qk_cons = smm
                        sq_pend[kc] = (pq, qslot, smm)
                        qslot = 1 - qslot

                    emit_S(0)
                    for kc in range(16):
                        if kc + 1 < 16:
                            emit_S(kc + 1)
                        pq, psl, smm = sq_pend.pop(kc)
                        pt = pts[ptslot]
                        dep = [smm, t20] + ([pt_tok[ptslot]] if pt_tok[ptslot][0] else [])
                        ex = p.op("scalar", lambda e, pq=pq, pt=pt: e.activation(
                            pt[:], pq[:], AF.Exp, bias=neg20), deps=dep)
                        pp_tok[2 * psl] = ex
                        pp_tok[2 * psl + 1] = ex
                        for hh in range(2):
                            h = 2 * occ + hh
                            dep = [ex] + vtoks[-1:] + ([pctx_tok[hh]] if pctx_tok[hh][0] and kc == 0 else [])
                            avmm[hh] = p.op("tensor", lambda e, pt=pt, kc=kc, h=h, hh=hh: e.matmul(
                                pctxs[hh][0:65, :], vh[:, (kc * 16 + h) * 65: (kc * 16 + h) * 65 + 65],
                                pt[:, hh * 512:(hh + 1) * 512], start=(kc == 0), stop=(kc == 15)),
                                deps=dep, inc=(kc == 15))
                        pt_tok[ptslot] = avmm[1]
                        ptslot = 1 - ptslot
                    for hh in range(2):
                        h = 2 * occ + hh
                        hr = 64 * hh
                        pctx = pctxs[hh]
                        av_last.append(avmm[hh])
                        tL = p.op("scalar", lambda e, pctx=pctx: e.activation(rec[:], pctx[64:65, :], AF.Ln),
                                  deps=[avmm[hh]] + ([rec_tok] if rec_tok[0] else []))
                        tE = p.op("scalar", lambda e: e.activation(rec[:], rec[:], AF.Exp, scale=-1.0),
                                  deps=[tL])
                        dep = [tE, t_on1] + ([pbc_tok] if pbc_tok[0] else [])
                        bc = p.op("tensor", lambda e: e.matmul(pbc[0:64, :], one1[:, 0:64], rec[:],
                                                               start=True, stop=True), deps=dep)
                        tc1 = p.op("scalar", lambda e, pctx=pctx: e.copy(tmpctx[:], pctx[0:64, :]),
                                   deps=[avmm[hh], tE] + ([tctx_tok] if tctx_tok[0] else []))
                        pctx_tok[hh] = tc1
                        mu = p.op("vector", lambda e: e.tensor_mul(tmpctx[:], tmpctx[:], pbc[0:64, :]),
                                  deps=[tc1, bc])
                        pbc_tok = mu
                        rec_tok = bc
                        dep = [mu] + ([ctxb_tok] if ctxb_tok[0] else [])
                        cb = p.op("scalar", lambda e, h=h, hr=hr, qt=qt, occ=occ: e.activation(
                            ctxb[hr:hr + 64, occ * S + qt * 512: occ * S + qt * 512 + 512],
                            tmpctx[:], AF.Identity,
                            bias=qkvb[hr:hr + 64, 16 + occ:17 + occ]), deps=dep)
                        tctx_tok = cb
                        av_last.append(cb)
            xnb_tok = qk_last[-1]
            vh_tok = av_last[-1]
            op_last = []
            for occ in range(CC):
                for tt in range(NT):
                    po = pbank(pslot)
                    mm = None
                    for icc in range(CC):
                        dep = av_last[-1:] + wl + ([pp_tok[pslot]] if pp_tok[pslot][0] else [])
                        mm = p.op("tensor", lambda e, po=po, icc=icc, occ=occ, tt=tt: e.matmul(
                            po, outwTb[:, icc * D + occ * 128: icc * D + occ * 128 + 128],
                            ctxb[:, icc * S + tt * 512: icc * S + tt * 512 + 512],
                            start=(icc == 0), stop=(icc == CC - 1)), deps=dep, inc=(icc == CC - 1))
                    op_last.append(mm)
                    eb = evo[eslot]; xr = xres[eslot]
                    lx = p.dma("sync", xr[:], g["bufX1"].ap()[b, occ, :, tt * 512:(tt + 1) * 512],
                               deps=[xres_tok[eslot]] if xres_tok[eslot][0] else [])
                    e1 = p.op("scalar", lambda e, po=po, eb=eb, occ=occ: e.activation(
                        eb[:], po, AF.Identity, bias=outb[:, occ:occ + 1]),
                        deps=[mm] + ([evo_tok[eslot]] if evo_tok[eslot][0] else []))
                    pp_tok[pslot] = e1
                    pslot = (pslot + 1) % 4
                    e2 = p.op("vector", lambda e, eb=eb, xr=xr: e.tensor_add(eb[:], eb[:], xr[:]),
                              deps=[e1, lx])
                    st = p.dma("sync", g["bufX2"].ap()[b, occ, :, tt * 512:(tt + 1) * 512],
                               eb[:], deps=[e2])
                    evo_tok[eslot] = st
                    xres_tok[eslot] = e2
                    eslot = 1 - eslot
            ctxb_tok = op_last[-1]


def emit_moe(p, nc, g):
    with contextlib.ExitStack() as ph:
        psb = lambda name, shape, dt=F32: ph.enter_context(nc.sbuf_tensor(_nm(name), shape, dt))
        psum = lambda name, shape, dt=F32: ph.enter_context(nc.psum_tensor(_nm(name), shape, dt))
        W1s = psb("m_w1", [128, CC * 2048], BF16)
        W2s = psb("m_w2", [128, CC * 2048], BF16)
        W3s = psb("m_w3", [128, 16 * D], BF16)
        acc = psb("m_acc", [128, 2 * CC * 512])
        gbufs = [psb(f"m_g{i}", [128, 16 * 512], BF16) for i in range(2)]
        s1s = [psb(f"m_s1{i}", [128, 512]) for i in range(2)]
        t1b = [psb(f"m_t1{i}", [128, 512]) for i in range(2)]
        combSB = psb("m_csb", [128, 512])
        xt = [psb(f"m_xt{i}", [128, CC * 512], BF16) for i in range(2)]
        xg = psb("m_xg", [128, CC * 128], BF16)
        gtb = psb("m_gt", [128, CC * 8], BF16)
        lg = psb("m_lg", [128, 8])
        m1 = psb("m_m1", [128, 1]); m2 = psb("m_m2", [128, 1])
        e1 = psb("m_e1", [128, 8]); eq2 = psb("m_eq2", [128, 8])
        mk = psb("m_mk", [128, 8]); c1 = psb("m_c1", [128, 8])
        g1 = psb("m_g1", [128, 1]); g2 = psb("m_g2", [128, 1])
        combT = psb("m_cT", [128, 8])
        combF = psb("m_cF", [8, S])
        idf = psb("m_idf", [128, 128])
        oneh = psb("m_oneh", [8, 8 * 128])
        b1c = psb("m_b1", [128, 16]); b2c = psb("m_b2", [128, 16]); b3c = psb("m_b3", [128, CC])
        xr = [psb(f"m_xr{i}", [128, 512]) for i in range(2)]
        pAs = [psum(f"m_pA{i}", [128, 512]) for i in range(2)]
        pBs = [psum(f"m_pB{i}", [128, 512]) for i in range(2)]
        pC = [psum(f"m_pC{i}", [128, 512]) for i in range(2)]
        pD = psum("m_pD", [128, 512])
        pG = psum("m_pG", [128, 512])
        ti0 = p.op("gpsimd", lambda e: e.memset(idf[:], 1.0))
        ti1 = p.op("gpsimd", lambda e: e.affine_select(idf[:], idf[:], [[1, 128]], ALU.is_equal,
                                                       0.0, base=0, channel_multiplier=-1), deps=[ti0])
        t_oh = p.dma("sync", oneh[:], g["onehot8"].ap())
        wg = [p.dma("sync", gtb[:, cc * 8:(cc + 1) * 8], g["gateTB"].ap()[cc]) for cc in range(CC)]

        pp_tok = {k: (None, 0) for k in ["A0", "A1", "B0", "B1", "C0", "C1", "D", "G"]}
        s1_tok = [(None, 0)] * 2
        t1_tok = [(None, 0)] * 2
        comb_toks = {}
        gb_tok = [(None, 0)] * 2
        gslot = 0
        acc_tok = {}
        xt_tok = [(None, 0)] * 2
        xg_tok = (None, 0)
        lg_tok = (None, 0)
        combF_tok = (None, 0)
        combSB_tok = (None, 0)
        w_tok = (None, 0)
        h_cons = (None, 0)
        y_cons = (None, 0)
        xr_tok = [(None, 0)] * 2
        cslot = 0; xslot = 0; rslot = 0
        for b in range(BC):
            gtoks = []
            for tc in range(16):
                xls = []
                for icc in range(CC):
                    dep = [xg_tok] if xg_tok[0] else []
                    xls.append(p.dma("sync", xg[:, icc * 128:(icc + 1) * 128],
                                     g["bufXN"].ap()[b, icc, :, tc * 128:(tc + 1) * 128], deps=dep))
                mm = None
                for icc in range(CC):
                    dep = xls + wg + ([pp_tok["G"]] if pp_tok["G"][0] else [])
                    mm = p.op("tensor", lambda e, icc=icc: e.matmul(
                        pG[0:128, 0:8], xg[:, icc * 128:(icc + 1) * 128], gtb[:, icc * 8:(icc + 1) * 8],
                        start=(icc == 0), stop=(icc == CC - 1)), deps=dep, inc=(icc == CC - 1))
                xg_tok = mm
                cp = p.op("vector", lambda e: e.tensor_copy(lg[:], pG[0:128, 0:8]),
                          deps=[mm] + ([lg_tok] if lg_tok[0] else []))
                pp_tok["G"] = cp
                v1 = p.op("vector", lambda e: e.reduce_max(m1[:], lg[:], axis=AX), deps=[cp])
                v2 = p.op("vector", lambda e: e.tensor_scalar(e1[:], lg[:], m1[:, 0:1], None, ALU.is_equal), deps=[v1])
                v3 = p.op("vector", lambda e: e.scalar_tensor_tensor(mk[:], e1[:], -1e30, lg[:], ALU.mult, ALU.add), deps=[v2])
                v4 = p.op("vector", lambda e: e.reduce_max(m2[:], mk[:], axis=AX), deps=[v3])
                v5 = p.op("vector", lambda e: e.tensor_sub(m1[:], m1[:], m2[:]), deps=[v4])
                v6 = p.op("scalar", lambda e: e.activation(g1[:], m1[:], AF.Sigmoid), deps=[v5])
                v7 = p.op("scalar", lambda e: e.activation(g2[:], g1[:], AF.Identity, scale=-1.0, bias=1.0), deps=[v6])
                v8 = p.op("vector", lambda e: e.tensor_scalar(eq2[:], lg[:], m2[:, 0:1], None, ALU.is_equal), deps=[v4])
                v9 = p.op("vector", lambda e: e.tensor_scalar(c1[:], e1[:], g1[:, 0:1], None, ALU.mult), deps=[v6, v8])
                vA = p.op("vector", lambda e: e.tensor_scalar(eq2[:], eq2[:], g2[:, 0:1], None, ALU.mult), deps=[v7, v9])
                vB = p.op("vector", lambda e: e.tensor_add(combT[:], c1[:], eq2[:]),
                          deps=[vA] + ([comb_toks.get("T", (None, 0))] if comb_toks.get("T", (None, 0))[0] else []))
                lg_tok = vB
                tr = p.op("tensor", lambda e: e.transpose(pG[0:8, 0:128], combT[:], idf[:]),
                          deps=[vB, ti1, cp])
                cf = p.op("scalar", lambda e, tc=tc: e.copy(combF[0:8, tc * 128:(tc + 1) * 128], pG[0:8, 0:128]),
                          deps=[tr] + ([combF_tok] if combF_tok[0] else []))
                pp_tok["G"] = cf
                comb_toks["T"] = tr
                gtoks.append(cf)
            for tg in range(2):
              for ei in range(8):
                  wl = []
                  dep12 = [h_cons] if h_cons[0] else []
                  dep3 = [y_cons] if y_cons[0] else []
                  for cc in range(CC):
                      wl.append(p.dma("sync", W1s[:, cc * 2048:(cc + 1) * 2048], g["w1TB"].ap()[ei, cc], deps=dep12))
                      wl.append(p.dma("sync", W2s[:, cc * 2048:(cc + 1) * 2048], g["w2TB"].ap()[ei, cc], deps=dep12))
                  for hc in range(16):
                      wl.append(p.dma("sync", W3s[:, hc * D:(hc + 1) * D], g["w3TB"].ap()[ei, hc], deps=dep3))
                  depb = [w_tok] if w_tok[0] else []
                  wl.append(p.dma("sync", b1c[:], g["l0_b1"].ap()[ei], deps=depb))
                  wl.append(p.dma("sync", b2c[:], g["l0_b2"].ap()[ei], deps=depb))
                  wl.append(p.dma("sync", b3c[:], g["l0_b3"].ap()[ei], deps=depb))
                  wlast = []
                  for tt in (2 * tg, 2 * tg + 1):
                      gbuf = gbufs[gslot]
                      x = xt[xslot]
                      xls = []
                      for icc in range(CC):
                          dep = [xt_tok[xslot]] if xt_tok[xslot][0] else []
                          xls.append(p.dma("sync", x[:, ts(icc, 512)],
                                           g["bufXN"].ap()[b, icc, :, tt * 512:(tt + 1) * 512], deps=dep))
                      dep = gtoks + [t_oh] + ([pp_tok["D"]] if pp_tok["D"][0] else [])
                      bd = p.op("tensor", lambda e, ei=ei, tt=tt: e.matmul(
                          pD[:], oneh[:, ei * 128:(ei + 1) * 128], combF[0:8, tt * 512:(tt + 1) * 512],
                          start=True, stop=True), deps=dep)
                      cs = p.op("scalar", lambda e: e.copy(combSB[:], pD[:]),
                                deps=[bd] + ([combSB_tok] if combSB_tok[0] else []))
                      pp_tok["D"] = cs
                      hlast = None
                      for hc in range(16):
                          hci = hc % 2
                          pA = pAs[hci]; pB = pBs[hci]
                          mmA = None
                          for icc in range(CC):
                              dep = xls + wl + ([pp_tok["A%d" % hci]] if pp_tok["A%d" % hci][0] and icc == 0 else [])
                              mmA = p.op("tensor", lambda e, icc=icc, hc=hc, x=x, pA=pA: e.matmul(
                                  pA[:], W1s[:, icc * 2048 + hc * 128: icc * 2048 + hc * 128 + 128],
                                  x[:, ts(icc, 512)], start=(icc == 0), stop=(icc == CC - 1)), deps=dep, inc=(icc == CC - 1))
                          mmB = None
                          for icc in range(CC):
                              dep = ([pp_tok["B%d" % hci]] if pp_tok["B%d" % hci][0] and icc == 0 else [])
                              mmB = p.op("tensor", lambda e, icc=icc, hc=hc, x=x, pB=pB: e.matmul(
                                  pB[:], W2s[:, icc * 2048 + hc * 128: icc * 2048 + hc * 128 + 128],
                                  x[:, ts(icc, 512)], start=(icc == 0), stop=(icc == CC - 1)), deps=dep, inc=(icc == CC - 1))
                          s1t = s1s[hci]
                          sv = p.op("scalar", lambda e, hc=hc, pA=pA, s1t=s1t: e.activation(
                              s1t[:], pA[:], AF.Silu, bias=b1c[:, hc:hc + 1]),
                              deps=[mmA] + ([s1_tok[hci]] if s1_tok[hci][0] else []))
                          pp_tok["A%d" % hci] = sv
                          gv = p.op("vector", lambda e, hc=hc, pB=pB, s1t=s1t, gbuf=gbuf: e.scalar_tensor_tensor(
                              gbuf[:, hc * 512:(hc + 1) * 512], pB[:], b2c[:, hc:hc + 1], s1t[:],
                              ALU.add, ALU.mult),
                              deps=[mmB, sv] + ([gb_tok[gslot]] if gb_tok[gslot][0] and hc == 0 else []))
                          s1_tok[hci] = gv
                          pp_tok["B%d" % hci] = gv
                          hlast = gv
                          h_cons = mmB
                      alast = None
                      for occ in range(CC):
                          ci = occ % 2
                          pc = pC[ci]
                          mmC = None
                          for hc in range(16):
                              dep = [hlast, wl[-1]] + ([pp_tok["C%d" % ci]] if pp_tok["C%d" % ci][0] and hc == 0 else [])
                              mmC = p.op("tensor", lambda e, hc=hc, occ=occ, pc=pc, gbuf=gbuf: e.matmul(
                                  pc[:], W3s[:, hc * D + occ * 128: hc * D + occ * 128 + 128],
                                  gbuf[:, hc * 512:(hc + 1) * 512],
                                  start=(hc == 0), stop=(hc == 15)), deps=dep, inc=(hc == 15))
                          y_cons = mmC
                          tb = t1b[occ % 2]
                          aslice = acc[:, ((tt - 2 * tg) * CC + occ) * 512: ((tt - 2 * tg) * CC + occ) * 512 + 512]
                          if ei == 0:
                              tv = p.op("vector", lambda e, pc=pc, occ=occ, aslice=aslice: e.scalar_tensor_tensor(
                                  aslice, pc[:], b3c[:, occ:occ + 1], combSB[:], ALU.add, ALU.mult),
                                  deps=[mmC, cs] + ([acc_tok.get((tt - 2 * tg, occ), (None, 0))] if acc_tok.get((tt - 2 * tg, occ), (None, 0))[0] else []))
                              pp_tok["C%d" % ci] = tv
                              acc_tok[(tt - 2 * tg, occ)] = tv
                              alast = tv
                          else:
                              tv = p.op("vector", lambda e, pc=pc, occ=occ, tb=tb: e.scalar_tensor_tensor(
                                  tb[:], pc[:], b3c[:, occ:occ + 1], combSB[:], ALU.add, ALU.mult),
                                  deps=[mmC, cs] + ([t1_tok[occ % 2]] if t1_tok[occ % 2][0] else []))
                              pp_tok["C%d" % ci] = tv
                              av = p.op("gpsimd", lambda e, tb=tb, aslice=aslice: e.tensor_add(
                                  aslice, aslice, tb[:]),
                                  deps=[tv, acc_tok.get((tt - 2 * tg, occ), (None, 0))])
                              t1_tok[occ % 2] = av
                              acc_tok[(tt - 2 * tg, occ)] = av
                              alast = av
                      wlast.append(alast)
                      gb_tok[gslot] = alast
                      combSB_tok = alast
                      xt_tok[xslot] = alast
                      xslot = 1 - xslot
                      gslot = 1 - gslot
                  w_tok = wlast[-1]
              for tt in (2 * tg, 2 * tg + 1):
                for occ in range(CC):
                    xrb = xr[rslot]
                    lx = p.dma("sync", xrb[:], g["bufX2"].ap()[b, occ, :, tt * 512:(tt + 1) * 512],
                               deps=[xr_tok[rslot]] if xr_tok[rslot][0] else [])
                    aslice = acc[:, ((tt - 2 * tg) * CC + occ) * 512: ((tt - 2 * tg) * CC + occ) * 512 + 512]
                    ad = p.op("vector", lambda e, xrb=xrb, aslice=aslice: e.tensor_add(
                        xrb[:], xrb[:], aslice), deps=[lx, acc_tok[(tt - 2 * tg, occ)]])
                    st = p.dma("sync", g["bufX3"].ap()[b, occ, :, tt * 512:(tt + 1) * 512],
                               xrb[:], deps=[ad])
                    xr_tok[rslot] = st
                    rslot = 1 - rslot
                    acc_tok[(tt - 2 * tg, occ)] = ad


def emit_l1attn(p, nc, g):
    with contextlib.ExitStack() as ph:
        psb = lambda name, shape, dt=F32: ph.enter_context(nc.sbuf_tensor(_nm(name), shape, dt))
        psum = lambda name, shape, dt=F32: ph.enter_context(nc.psum_tensor(_nm(name), shape, dt))
        wqb = psb("l1_wqb", [128, CC * D], BF16)
        wkb = psb("l1_wkb", [128, CC * D], BF16)
        wvb = psb("l1_wvb", [128, CC * D], BF16)
        wob = psb("l1_wob", [128, CC * D], BF16)
        bo = psb("l1_bob", [128, CC])
        ab = psb("l1_ab", [128, 1024])
        cmf = psb("l1_cmf", [128, 896])
        cmb = psb("l1_cmb", [128, 896], BF16)
        xb = psb("l1_xb", [128, CC * S], BF16)
        vh = psb("l1_vh", [128, 16 * 16 * 65], BF16)
        ctxb = psb("l1_ctx", [128, CC * S], BF16)
        qh2 = psb("l1_qh", [128, S], BF16)
        kh2 = psb("l1_kh", [128, S], BF16)
        tmpctx = psb("l1_tc", [64, 512])
        rec = psb("l1_rec", [1, 512])
        one1 = psb("l1_one1", [1, 128])
        pts = [psb(f"l1_pt{i}", [128, 512], BF16) for i in range(4)]
        evo = [psb(f"l1_ev{i}", [128, 512]) for i in range(2)]
        pp = [psum(f"l1_pp{i}", [128, 512]) for i in range(4)]
        pctxs = [psum(f"l1_pctx{i}", [128, 512]) for i in range(2)]
        pbc = psum("l1_pbc", [128, 512])
        wl = []
        for cc in range(CC):
            wl.append(p.dma("gpsimd", wqb[:, ts(cc, D)], g["l1_wq"].ap()[cc]))
            wl.append(p.dma("gpsimd", wkb[:, ts(cc, D)], g["l1_wk"].ap()[cc]))
            wl.append(p.dma("gpsimd", wvb[:, ts(cc, D)], g["l1_wv"].ap()[cc]))
            wl.append(p.dma("gpsimd", wob[:, ts(cc, D)], g["l1_wo"].ap()[cc]))
        wl.append(p.dma("sync", bo[:], g["l1_bo"].ap()))
        wl.append(p.dma("sync", ab[:], g["alibias"].ap()))
        wl.append(p.dma("sync", cmf[:], g["causalm"].ap()))
        t_cm = p.op("scalar", lambda e: e.copy(cmb[:], cmf[:]), deps=wl)
        t_on1 = p.op("vector", lambda e: e.memset(one1[:], 1.0))
        t_vones = p.op("vector", lambda e: e.memset(vh[:, 64:16640:65], 1.0))

        pp_tok = [(None, 0)] * 4
        pt_tok = [(None, 0)] * 4
        evo_tok = [(None, 0)] * 2
        xb_tok = (None, 0)
        vh_tok = (None, 0)
        ctxb_tok = (None, 0)
        tctx_tok = (None, 0); rec_tok = (None, 0)
        pctx_tok = [(None, 0)] * 2; pbc_tok = (None, 0)
        pslot = 0; ptslot = 0; eslot = 0
        for b in range(BC):
            xl = []
            for cc in range(CC):
                dep = [xb_tok] if xb_tok[0] else []
                xl.append(p.dma("gpsimd", xb[:, cc * S:(cc + 1) * S],
                                g["bufX4"].ap()[b, cc], deps=dep))
            vtoks = []
            for kc in range(16):
                for vc in range(2):
                    pv = pp[pslot]
                    mm = None
                    for icc in range(CC):
                        dep = xl + wl + ([pp_tok[pslot]] if pp_tok[pslot][0] else [])
                        mm = p.op("tensor", lambda e, pv=pv, icc=icc, kc=kc, vc=vc: e.matmul(
                            pv[:], xb[:, icc * S + kc * 128: icc * S + kc * 128 + 128],
                            wvb[:, icc * D + vc * 512: icc * D + vc * 512 + 512],
                            start=(icc == 0), stop=(icc == CC - 1)), deps=dep, inc=(icc == CC - 1))
                    last = None
                    for j in range(8):
                        h = vc * 8 + j
                        dep = [mm, t_vones] + ([vh_tok] if vh_tok[0] else [])
                        last = p.op("vector", lambda e, pv=pv, j=j, kc=kc, h=h: e.tensor_copy(
                            vh[:, (kc * 16 + h) * 65: (kc * 16 + h) * 65 + 64],
                            pv[:, j * 64: j * 64 + 64]), deps=dep)
                        vtoks.append(last)
                    pp_tok[pslot] = last
                    pslot = (pslot + 1) % 4
            av_last = []
            qk_last = []
            qk_cons = (None, 0)
            for occ in range(CC):
                for (which, wt) in (("q", wqb), ("k", wkb)):
                    buf = qh2 if which == "q" else kh2
                    for tt in range(NT):
                        pq = pp[pslot]
                        mm = None
                        for icc in range(CC):
                            dep = xl + wl + ([pp_tok[pslot]] if pp_tok[pslot][0] else [])
                            mm = p.op("tensor", lambda e, pq=pq, icc=icc, occ=occ, tt=tt, wt=wt: e.matmul(
                                pq[:], wt[:, icc * D + occ * 128: icc * D + occ * 128 + 128],
                                xb[:, icc * S + tt * 512: icc * S + tt * 512 + 512],
                                start=(icc == 0), stop=(icc == CC - 1)), deps=dep, inc=(icc == CC - 1))
                        qk_last.append(mm)
                        wardep = [qk_cons] if qk_cons[0] else []
                        if which == "q":
                            et = p.op("scalar", lambda e, pq=pq, buf=buf, tt=tt: e.activation(
                                buf[:, tt * 512:(tt + 1) * 512], pq[:], AF.Identity,
                                scale=0.125), deps=[mm] + wardep)
                        else:
                            et = p.op("scalar", lambda e, pq=pq, buf=buf, tt=tt: e.copy(
                                buf[:, tt * 512:(tt + 1) * 512], pq[:]), deps=[mm] + wardep)
                        pp_tok[pslot] = et
                        pslot = (pslot + 1) % 4
                        if which == "q":
                            qh_tok = et
                        else:
                            kh_tok = et
                for qt in range(NT):
                    nkc = 4 * qt + 4
                    avmm = [None, None]
                    sq_pend = {}

                    def emit_S(kc, qt=qt, occ=occ):
                        nonlocal pslot, qk_cons
                        for hh in range(2):
                            hr = 64 * hh
                            pS = pp[pslot]
                            dep = [qh_tok, kh_tok] + ([pp_tok[pslot]] if pp_tok[pslot][0] else [])
                            smm = p.op("tensor", lambda e, pS=pS, hr=hr, kc=kc, qt=qt: e.matmul(
                                pS[0:128, :], kh2[hr:hr + 64, kc * 128:(kc + 1) * 128],
                                qh2[hr:hr + 64, qt * 512:(qt + 1) * 512],
                                start=True, stop=True), deps=dep)
                            qk_cons = smm
                            sq_pend[(kc, hh)] = (pS, pslot, smm)
                            pslot = (pslot + 1) % 4

                    emit_S(0)
                    for kc in range(nkc):
                        if kc + 1 < nkc:
                            emit_S(kc + 1)
                        for hh in range(2):
                            h = 2 * occ + hh
                            pS, psl, smm = sq_pend.pop((kc, hh))
                            pt = pts[2 * hh + ptslot]
                            dep = [smm] + ([pt_tok[2 * hh + ptslot]] if pt_tok[2 * hh + ptslot][0] else [])
                            ex = p.op("scalar", lambda e, pS=pS, pt=pt, h=h, kc=kc, qt=qt: e.activation(
                                pt[:], pS[:], AF.Exp,
                                bias=ab[:, h * 64 + kc * 4 + qt: h * 64 + kc * 4 + qt + 1]), deps=dep)
                            pp_tok[psl] = ex
                            lastpt = ex
                            if kc >= 4 * qt:
                                delta = qt * 512 - kc * 128 + 384
                                lastpt = p.op("vector", lambda e, pt=pt, delta=delta: e.tensor_mul(
                                    pt[:], pt[:], cmb[:, delta: delta + 512]), deps=[ex, t_cm])
                            dep = [lastpt] + vtoks[-1:] + ([pctx_tok[hh]] if pctx_tok[hh][0] and kc == 0 else [])
                            avmm[hh] = p.op("tensor", lambda e, pt=pt, kc=kc, h=h, nkc=nkc, hh=hh: e.matmul(
                                pctxs[hh][0:65, :], vh[:, (kc * 16 + h) * 65: (kc * 16 + h) * 65 + 65],
                                pt[:], start=(kc == 0), stop=(kc == nkc - 1)), deps=dep,
                                inc=(kc == nkc - 1))
                            pt_tok[2 * hh + ptslot] = avmm[hh]
                        ptslot = 1 - ptslot
                    for hh in range(2):
                        h = 2 * occ + hh
                        hr = 64 * hh
                        pctx = pctxs[hh]
                        av_last.append(avmm[hh])
                        tL = p.op("scalar", lambda e, pctx=pctx: e.activation(rec[:], pctx[64:65, :], AF.Ln),
                                  deps=[avmm[hh]] + ([rec_tok] if rec_tok[0] else []))
                        tE = p.op("scalar", lambda e: e.activation(rec[:], rec[:], AF.Exp, scale=-1.0),
                                  deps=[tL])
                        dep = [tE, t_on1] + ([pbc_tok] if pbc_tok[0] else [])
                        bc = p.op("tensor", lambda e: e.matmul(pbc[0:64, :], one1[:, 0:64], rec[:],
                                                               start=True, stop=True), deps=dep)
                        tc1 = p.op("scalar", lambda e, pctx=pctx: e.copy(tmpctx[:], pctx[0:64, :]),
                                   deps=[avmm[hh], tE] + ([tctx_tok] if tctx_tok[0] else []))
                        pctx_tok[hh] = tc1
                        dep = [tc1, bc] + ([ctxb_tok] if ctxb_tok[0] else [])
                        mu = p.op("vector", lambda e, h=h, hr=hr, qt=qt, occ=occ: e.tensor_mul(
                            ctxb[hr:hr + 64, occ * S + qt * 512: occ * S + qt * 512 + 512],
                            tmpctx[:], pbc[0:64, :]), deps=dep)
                        pbc_tok = mu
                        rec_tok = bc
                        tctx_tok = mu
                        av_last.append(mu)
            xb_tok = qk_last[-1]
            vh_tok = av_last[-1]
            op_last = []
            for occ in range(CC):
                for tt in range(NT):
                    po = pp[pslot]
                    mm = None
                    for icc in range(CC):
                        dep = av_last[-1:] + wl + ([pp_tok[pslot]] if pp_tok[pslot][0] else [])
                        mm = p.op("tensor", lambda e, po=po, icc=icc, occ=occ, tt=tt: e.matmul(
                            po[:], wob[:, icc * D + occ * 128: icc * D + occ * 128 + 128],
                            ctxb[:, icc * S + tt * 512: icc * S + tt * 512 + 512],
                            start=(icc == 0), stop=(icc == CC - 1)), deps=dep, inc=(icc == CC - 1))
                    op_last.append(mm)
                    eb = evo[eslot]
                    e1 = p.op("scalar", lambda e, po=po, eb=eb, occ=occ: e.activation(
                        eb[:], po[:], AF.Identity, bias=bo[:, occ:occ + 1]),
                        deps=[mm] + ([evo_tok[eslot]] if evo_tok[eslot][0] else []))
                    pp_tok[pslot] = e1
                    pslot = (pslot + 1) % 4
                    st = p.dma("sync", g["bufA"].ap()[b, occ, :, tt * 512:(tt + 1) * 512],
                               eb[:], deps=[e1])
                    evo_tok[eslot] = st
                    eslot = 1 - eslot
            ctxb_tok = op_last[-1]


def emit_l1router(p, nc, g):
    with contextlib.ExitStack() as ph:
        psb = lambda name, shape, dt=F32: ph.enter_context(nc.sbuf_tensor(_nm(name), shape, dt))
        psum = lambda name, shape, dt=F32: ph.enter_context(nc.psum_tensor(_nm(name), shape, dt))
        xs = [psb(f"r_x{i}", [128, CC * 512]) for i in range(2)]
        hb = psb("r_h", [128, CC * 512], BF16)
        sq = psb("r_sq", [128, 512])
        sq2 = psb("r_sq2", [128, 512])
        stM = psb("r_stM", [1, 512]); stV = psb("r_stV", [1, 512]); stT = psb("r_stT", [1, 512])
        gcol = psb("r_g", [128, CC]); becol = psb("r_be", [128, CC])
        wrb = psb("r_wr", [128, CC * 8], BF16)
        brs = psb("r_br", [8, 1])
        lgF = psb("r_lg", [8, 512]); eF = psb("r_eF", [8, 512])
        r1 = psb("r_r1", [1, 512]); p0r = psb("r_p0", [1, 512])
        onec = psb("r_one", [128, 1]); one1 = psb("r_one1", [1, 128])
        pS = psum("r_pS", [1, 512]); pQ = psum("r_pQ", [1, 512])
        pB = psum("r_pB", [128, 512]); pM = psum("r_pM", [128, 512])
        pL = psum("r_pL", [128, 512])
        t_on = p.op("vector", lambda e: e.memset(onec[:], 1.0))
        t_on1 = p.op("vector", lambda e: e.memset(one1[:], 1.0))
        wl = [p.dma("sync", gcol[:], g["l1_g"].ap()),
              p.dma("sync", becol[:], g["l1_be"].ap()),
              p.dma("sync", brs[:], g["l1_br"].ap())]
        for cc in range(CC):
            wl.append(p.dma("gpsimd", wrb[:, cc * 8:(cc + 1) * 8], g["l1_wr"].ap()[cc]))
        epsc, epst = p.constcol(1e-5, rows=1)
        xs_tok = [(None, 0)] * 2
        hb_tok = (None, 0); sq_tok = [(None, 0)] * 2; st_tok = (None, 0)
        pS_tok = (None, 0); pB_tok = (None, 0); pM_tok = (None, 0)
        pL_tok = (None, 0); lg_tok = (None, 0); e_tok = (None, 0)
        r1_tok = (None, 0); p0_tok = (None, 0)
        bB_prev = (None, 0); bM_prev = (None, 0)
        slot = 0
        for b in range(BC):
            for tt in range(NT):
                x = xs[slot]
                loads = []
                for cc in range(CC):
                    dep = [xs_tok[slot]] if xs_tok[slot][0] else []
                    loads.append(p.dma("sync", x[:, ts(cc, 512)],
                                       g["bufX6"].ap()[b, cc, :, tt * 512:(tt + 1) * 512], deps=dep))
                mmS = None
                for cc in range(CC):
                    dep = [loads[cc], t_on] + ([pS_tok] if pS_tok[0] else [])
                    mmS = p.op("tensor", lambda e, cc=cc, x=x: e.matmul(
                        pS[:], onec[:], x[:, ts(cc, 512)],
                        start=(cc == 0), stop=(cc == CC - 1)), deps=dep, inc=(cc == CC - 1))
                mmQ = None
                for cc in range(CC):
                    sqx = sq2 if cc % 2 else sq
                    dep = [loads[cc]] + ([sq_tok[cc % 2]] if sq_tok[cc % 2][0] else [])
                    tq = p.op("scalar", lambda e, cc=cc, x=x, sqx=sqx: e.activation(
                        sqx[:], x[:, ts(cc, 512)], AF.Square), deps=dep)
                    mmQ = p.op("tensor", lambda e, cc=cc, sqx=sqx: e.matmul(
                        pQ[:], onec[:], sqx[:], start=(cc == 0), stop=(cc == CC - 1)),
                        deps=[tq, mmS])
                    sq_tok[cc % 2] = mmQ
                dep0 = [mmS, mmQ] + ([st_tok] if st_tok[0] else [])
                s0 = p.op("vector", lambda e: e.tensor_scalar(stM[:], pS[:], 1.0 / D, None, ALU.mult), deps=dep0)
                s1 = p.op("vector", lambda e: e.tensor_scalar(stV[:], pQ[:], 1.0 / D, None, ALU.mult),
                          deps=[mmQ, s0] + ([bB_prev] if bB_prev[0] else []))
                pS_tok = s1
                s2 = p.op("vector", lambda e: e.tensor_mul(stT[:], stM[:], stM[:]), deps=[s0])
                s3 = p.op("vector", lambda e: e.tensor_sub(stV[:], stV[:], stT[:]), deps=[s1, s2])
                s4 = p.op("scalar", lambda e: e.activation(stV[:], stV[:], AF.Ln, bias=epsc), deps=[s3, epst])
                s5 = p.op("scalar", lambda e: e.activation(stV[:], stV[:], AF.Exp, scale=-0.5), deps=[s4])
                s6 = p.op("vector", lambda e: e.tensor_mul(stT[:], stM[:], stV[:]),
                          deps=[s5, s2] + ([bM_prev] if bM_prev[0] else []))
                st_tok = s6
                bB = p.op("tensor", lambda e: e.matmul(pB[:], one1[:], stV[:], start=True, stop=True),
                          deps=[s5, t_on1] + ([pB_tok] if pB_tok[0] else []))
                bM = p.op("tensor", lambda e: e.matmul(pM[:], one1[:], stT[:], start=True, stop=True),
                          deps=[s6, t_on1, bB] + ([pM_tok] if pM_tok[0] else []))
                bB_prev, bM_prev = bB, bM
                hs = []
                m2 = None
                for cc in range(CC):
                    sqx = sq2 if cc % 2 else sq
                    dep = [bB, loads[cc]] + ([hb_tok] if hb_tok[0] else [])
                    m1 = p.op("vector", lambda e, cc=cc, x=x, sqx=sqx: e.tensor_mul(
                        sqx[:], x[:, ts(cc, 512)], pB[:]), deps=dep + [sq_tok[cc % 2]])
                    m2 = p.op("vector", lambda e, cc=cc, sqx=sqx: e.tensor_sub(sqx[:], sqx[:], pM[:]),
                              deps=[m1, bM])
                    m3 = p.op("scalar", lambda e, cc=cc, sqx=sqx: e.activation(
                        hb[:, ts(cc, 512)], sqx[:], AF.Identity,
                        scale=gcol[:, cc:cc + 1], bias=becol[:, cc:cc + 1]), deps=[m2] + wl)
                    sq_tok[cc % 2] = m3
                    hs.append(m3)
                pM_tok = m2
                pB_tok = m2
                mmL = None
                for cc in range(CC):
                    dep = [hs[cc]] + wl + ([pL_tok] if pL_tok[0] and cc == 0 else [])
                    mmL = p.op("tensor", lambda e, cc=cc: e.matmul(
                        pL[0:8, :], wrb[:, cc * 8:(cc + 1) * 8], hb[:, ts(cc, 512)],
                        start=(cc == 0), stop=(cc == CC - 1)), deps=dep, inc=(cc == CC - 1))
                hb_tok = mmL
                lg = p.op("scalar", lambda e: e.activation(lgF[:], pL[0:8, :], AF.Identity,
                                                           bias=brs[:, 0:1]),
                          deps=[mmL] + wl + ([lg_tok] if lg_tok[0] else []))
                pL_tok = lg
                ee = p.op("scalar", lambda e: e.activation(eF[:], lgF[:], AF.Exp),
                          deps=[lg] + ([e_tok] if e_tok[0] else []))
                lg_tok = ee
                mmE = p.op("tensor", lambda e: e.matmul(pS[:], onec[0:8, :], eF[:],
                                                        start=True, stop=True),
                           deps=[ee, t_on, s1] + ([pS_tok] if pS_tok[0] else []))
                pS_tok = mmE
                rL = p.op("scalar", lambda e: e.activation(r1[:], pS[:], AF.Ln),
                          deps=[mmE] + ([r1_tok] if r1_tok[0] else []))
                rE = p.op("scalar", lambda e: e.activation(r1[:], r1[:], AF.Exp, scale=-1.0), deps=[rL])
                pS_tok = rL
                pv = p.op("vector", lambda e: e.tensor_mul(p0r[:], eF[0:1, :], r1[:]),
                          deps=[rE] + ([p0_tok] if p0_tok[0] else []))
                e_tok = pv
                r1_tok = pv
                p0_tok = p.dma("sync", g["bufP0"].ap()[b, :, tt * 512:(tt + 1) * 512], p0r[:], deps=[pv])
                xs_tok[slot] = p0_tok
                slot = 1 - slot


def _bisect_search(p, nc, psb, psum_, absap, ncols, n, k, thr_out, regs, cells, deps):
    tcell, tcol, c11, cnt, trash, onec, one1, pT, pc = (
        cells["tcell"], cells["tcol"], cells["c11"], cells["cnt"], cells["trash"],
        cells["onec"], cells["one1"], cells["pT"], cells["pc"])
    lo, hi, mid, cr = regs
    KBITS = int(np.float32(k).view(np.int32))
    p.raw("vector", lambda e: e.reg_mov(lo, 0), deps=deps)
    p.raw("vector", lambda e: e.reg_mov(hi, 0x7F800000))
    tcol_tok = cells.get("tcol_tok", (None, 0))
    c11_tok = cells.get("c11_tok", (None, 0))
    for it in range(31):
        def _mids(e, lo=lo, hi=hi, mid=mid):
            e.reg_add(mid, lo, hi)
            e.reg_alu(mid, mid, 1, ALU.arith_shift_right)
            return e.reg_save(cells["tcell"][0:1, 0:1], mid)
        tw = p.op("vector", _mids, deps=[cells.get("pT_tok", (None, 0))] if cells.get("pT_tok", (None, 0))[0] else [])
        bc = p.op("tensor", lambda e: e.matmul(pT[:, 0:1], one1[:], tcell[0:1, 0:1].bitcast(F32),
                                               start=True, stop=True),
                  deps=[tw] + ([tcol_tok] if tcol_tok[0] else []))
        cp = p.op("scalar", lambda e: e.copy(tcol[:], pT[:, 0:1]), deps=[bc])
        cells["pT_tok"] = cp
        part = (ncols * 5) // 8 // 64 * 64
        if part >= 64 and ncols - part >= 64:
            ntg = p.op("scalar", lambda e: e.activation(cells["tneg"][:], tcol[:], AF.Identity,
                                                        scale=-1.0), deps=[cp])
            ct = p.op("vector", lambda e, part=part: e.tensor_scalar(
                trash[:, 0:part], absap[:, 0:part], tcol[:, 0:1], 0.0, ALU.is_lt, ALU.add,
                accum_out=cnt[:]), deps=[cp] + ([c11_tok] if c11_tok[0] else []))
            sg = p.op("scalar", lambda e, part=part, ncols=ncols: e.activation(
                cells["sgn"][:, 0:ncols - part], absap[:, part:ncols], AF.Sign,
                scale=-1.0, bias=tcol[:, 0:1]), deps=[cp] + ([cells.get("sgn_tok", (None, 0))] if cells.get("sgn_tok", (None, 0))[0] else []))
            ct2 = p.op("scalar", lambda e, part=part, ncols=ncols: e.activation(
                cells["sgn"][:, 0:ncols - part], cells["sgn"][:, 0:ncols - part], AF.Relu,
                accum_out=cells["cnt2"][:]), deps=[sg] + ([c11_tok] if c11_tok[0] else []))
            cells["sgn_tok"] = ct2
            rd = p.op("tensor", lambda e: e.matmul(pc[0:1, 0:1], onec[:], cnt[:],
                                                   start=True, stop=False), deps=[ct, ct2])
            rd = p.op("tensor", lambda e: e.matmul(pc[0:1, 0:1], onec[:], cells["cnt2"][:],
                                                   start=False, stop=True), deps=[])
            tcol_tok = rd
        else:
            ct = p.op("vector", lambda e: e.tensor_scalar(
                trash[:, 0:ncols], absap, tcol[:, 0:1], 0.0, ALU.is_lt, ALU.add,
                accum_out=cnt[:]), deps=[cp] + ([c11_tok] if c11_tok[0] else []))
            tcol_tok = ct
            rd = p.op("tensor", lambda e: e.matmul(pc[0:1, 0:1], onec[:], cnt[:],
                                                   start=True, stop=True), deps=[ct])
        c2 = p.op("scalar", lambda e: e.copy(c11[0:1, 0:1], pc[0:1, 0:1]), deps=[rd])
        c11_tok = c2

        def _branch(e, lo=lo, hi=hi, mid=mid, cr=cr, KBITS=KBITS):
            e.reg_load(cr, cells["c11"][0:1, 0:1].bitcast(mybir.dt.int32))
            with e.If_lt(cr, KBITS):
                e.reg_add(lo, mid, 1)
            with e.Else():
                e.reg_mov(hi, mid)
            return e.nop()
        p.op("vector", _branch, deps=[c2])
    def _fin(e, hi=hi, mid=mid):
        e.reg_add(mid, hi, -1)
        return e.reg_save(cells["tcell"][0:1, 0:1], mid)
    tw = p.op("vector", _fin)
    bc = p.op("tensor", lambda e: e.matmul(pT[:, 0:1], one1[:], tcell[0:1, 0:1].bitcast(F32),
                                           start=True, stop=True),
              deps=[tw] + ([tcol_tok] if tcol_tok[0] else []))
    fin = p.op("scalar", lambda e: e.copy(thr_out, pT[:, 0:1]), deps=[bc])
    cells["pT_tok"] = fin
    cells["tcol_tok"] = tcol_tok
    cells["c11_tok"] = c11_tok
    return fin


def emit_thresholds(p, nc, g):
    with contextlib.ExitStack() as ph:
        psb = lambda name, shape, dt=F32: ph.enter_context(nc.sbuf_tensor(_nm(name), shape, dt))
        psum_ = lambda name, shape, dt=F32: ph.enter_context(nc.psum_tensor(_nm(name), shape, dt))
        absw = psb("th_absw", [128, 32768])
        trash = psb("th_trash", [128, 20480], BF16)
        small = psb("th_small", [128, 48])
        b1a = psb("th_b1a", [128, 32])
        b2a = psb("th_b2a", [128, 8])
        f1 = psb("th_f1", [128, 32])
        f2 = psb("th_f2", [128, 32])
        tcell = psb("th_tcell", [1, 1], mybir.dt.int32)
        tcol = psb("th_tcol", [128, 1])
        c11 = psb("th_c11", [1, 1])
        cnt = psb("th_cnt", [128, 1])
        onec = psb("th_one", [128, 1])
        one1 = psb("th_one1", [1, 128])
        pT = psum_("th_pT", [128, 512])
        pc = psum_("th_pc", [128, 512])
        t_on = p.op("vector", lambda e: e.memset(onec[:], 1.0))
        t_on1 = p.op("vector", lambda e: e.memset(one1[:], 1.0))
        lo = nc.vector.alloc_register("th_lo")
        hi = nc.vector.alloc_register("th_hi")
        mid = nc.vector.alloc_register("th_mid")
        cr = nc.vector.alloc_register("th_cr")
        regs = (lo, hi, mid, cr)
        cnt2 = psb("th_cnt2", [128, 1])
        sgn = psb("th_sgn", [128, 12288], BF16)
        tneg = psb("th_tneg", [128, 1])
        cells = dict(tcell=tcell, tcol=tcol, c11=c11, cnt=cnt, cnt2=cnt2, trash=trash,
                     sgn=sgn, tneg=tneg, onec=onec, one1=one1, pT=pT, pc=pc)
        nW, kW = 4194304, 209715
        lds = [p.dma("sync", absw[:, cc * 4096:(cc + 1) * 4096], g["l1_W1"].ap()[cc])
               for cc in range(CC)]
        ab = p.op("scalar", lambda e: e.activation(absw[:], absw[:], AF.Abs),
                  deps=lds + [t_on, t_on1])
        t1 = _bisect_search(p, nc, psb, psum_, absw[:], 32768, nW, kW,
                            small[:, 0:1], regs, cells, [ab])
        t2 = _bisect_search(p, nc, psb, psum_, absw[:], 32768, nW, nW - kW + 1,
                            small[:, 1:2], regs, cells, [t1])
        lds = []
        prev = t2
        for j in range(32):
            lds.append(p.dma("sync", absw[:, j * 1024:(j + 1) * 1024], g["l1_W2f"].ap()[j],
                             deps=[prev]))
        ab = p.op("scalar", lambda e: e.activation(absw[:], absw[:], AF.Abs), deps=lds)
        t3 = _bisect_search(p, nc, psb, psum_, absw[:], 32768, nW, kW,
                            small[:, 4:5], regs, cells, [ab])
        t4 = _bisect_search(p, nc, psb, psum_, absw[:], 32768, nW, nW - kW + 1,
                            small[:, 5:6], regs, cells, [t3])
        lb1 = p.dma("sync", b1a[:], g["l1_b1f"].ap())
        lb2 = p.dma("sync", b2a[:], g["l1_b2f"].ap())
        ab1 = p.op("scalar", lambda e: e.activation(b1a[:], b1a[:], AF.Abs), deps=[lb1])
        ab2 = p.op("scalar", lambda e: e.activation(b2a[:], b2a[:], AF.Abs), deps=[lb2])
        t5 = _bisect_search(p, nc, psb, psum_, b1a[:], 32, 4096, 204,
                            small[:, 2:3], regs, cells, [ab1, t4])
        t6 = _bisect_search(p, nc, psb, psum_, b1a[:], 32, 4096, 4096 - 204 + 1,
                            small[:, 3:4], regs, cells, [t5])
        t7 = _bisect_search(p, nc, psb, psum_, b2a[:], 8, 1024, 51,
                            small[:, 6:7], regs, cells, [t6, ab2])
        t8 = _bisect_search(p, nc, psb, psum_, b2a[:], 8, 1024, 1024 - 51 + 1,
                            small[:, 7:8], regs, cells, [t7])
        lb1v = p.dma("sync", f2[:, 0:32], g["l1_b1f"].ap(), deps=[t8])
        u1 = p.op("vector", lambda e: e.tensor_scalar(f1[:, 0:32], b1a[:], small[:, 2:3], None, ALU.is_le), deps=[t8])
        u2 = p.op("scalar", lambda e: e.activation(f1[:, 0:32], f1[:, 0:32], AF.Identity, scale=0.8, bias=1.0), deps=[u1])
        u3 = p.op("vector", lambda e: e.tensor_scalar(b1a[:], b1a[:], small[:, 3:4], None, ALU.is_ge), deps=[u2])
        u4 = p.op("vector", lambda e: e.scalar_tensor_tensor(f1[:, 0:32], b1a[:], -0.8, f1[:, 0:32], ALU.mult, ALU.add), deps=[u3])
        u5 = p.op("vector", lambda e: e.tensor_mul(small[:, 8:40], f2[:, 0:32], f1[:, 0:32]), deps=[u4, lb1v])
        lb2v = p.dma("sync", f2[:, 0:8], g["l1_b2f"].ap(), deps=[u5])
        w1 = p.op("vector", lambda e: e.tensor_scalar(f1[:, 0:8], b2a[:], small[:, 6:7], None, ALU.is_le), deps=[t8, u5])
        w2 = p.op("scalar", lambda e: e.activation(f1[:, 0:8], f1[:, 0:8], AF.Identity, scale=0.8, bias=1.0), deps=[w1])
        w3 = p.op("vector", lambda e: e.tensor_scalar(b2a[:], b2a[:], small[:, 7:8], None, ALU.is_ge), deps=[w2])
        w4 = p.op("vector", lambda e: e.scalar_tensor_tensor(f1[:, 0:8], b2a[:], -0.8, f1[:, 0:8], ALU.mult, ALU.add), deps=[w3])
        w5 = p.op("vector", lambda e: e.tensor_mul(small[:, 40:48], f2[:, 0:8], f1[:, 0:8]), deps=[w4, lb2v])
        p.dma("sync", g["bufSMALL"].ap(), small[:], deps=[u5, w5])
        nc.vector.free_register(lo); nc.vector.free_register(hi)
        nc.vector.free_register(mid); nc.vector.free_register(cr)


def emit_l1ffn(p, nc, g, path):
    with contextlib.ExitStack() as ph:
        psb = lambda name, shape, dt=F32: ph.enter_context(nc.sbuf_tensor(_nm(name), shape, dt))
        psum_ = lambda name, shape, dt=F32: ph.enter_context(nc.psum_tensor(_nm(name), shape, dt))
        W1b = psb("f_w1", [128, CC * 4096], BF16)
        W2b = psb("f_w2", [128, 32 * D], BF16)
        gbufL = psb("f_g", [128, 32 * 512], BF16)
        small = psb("f_small", [128, 48])
        xt = [psb(f"f_xt{i}", [128, CC * 512], BF16) for i in range(2)]
        ev = [psb(f"f_ev{i}", [128, 512]) for i in range(2)]
        one1 = psb("f_one1", [1, 128])
        b1cl = psb("f_b1", [128, 32])
        b2cl = psb("f_b2", [128, CC])
        us = [psb(f"f_u{i}", [128, 512]) for i in range(3)]
        q1s = [psb(f"f_q1{i}", [128, 512]) for i in range(2)]
        q3s = [psb(f"f_q3{i}", [128, 512]) for i in range(2)]
        sgs = [psb(f"f_sg{i}", [128, 512]) for i in range(2)]
        if path == 1:
            y0t = [psb(f"f_y0{i}", [128, 512], BF16) for i in range(2)]
            p0r = psb("f_p0", [1, 512])
            combP = psb("f_cp", [128, 512], BF16)
            wch = us[0]; fa = q1s[0]; ft = q3s[0]
        pF = [psum_(f"f_pF{i}", [128, 512]) for i in range(2)]
        pY = [psum_(f"f_pY{i}", [128, 512]) for i in range(2)]
        pP = psum_("f_pP", [128, 512])
        t_on1 = p.op("vector", lambda e: e.memset(one1[:], 1.0))
        wl = []
        tsm = p.dma("sync", small[:], g["bufSMALL"].ap())
        if path == 0:
            for cc in range(CC):
                wl.append(p.dma("gpsimd", W1b[:, cc * 4096:(cc + 1) * 4096], g["l1_W1"].ap()[cc]))
            for j in range(32):
                wl.append(p.dma("gpsimd", W2b[:, j * D:(j + 1) * D], g["l1_W2f"].ap()[j]))
            wl.append(p.dma("sync", b1cl[:, 0:32], g["l1_b1f"].ap()))
            wl.append(p.dma("sync", b2cl[:], g["l1_b2f"].ap()))
            bload = wl[-1]
        else:
            wprev = (None, 0)
            for (wname, wdst, nblk, blkw) in (("l1_W1", W1b, CC, 4096), ("l1_W2f", W2b, 32, D)):
                for ib in range(nblk):
                    nch = blkw // 512
                    for jc in range(nch):
                        dep = [wprev] if wprev[0] else []
                        lw = p.dma("sync", wch[:], g[wname].ap()[ib, :, jc * 512:(jc + 1) * 512],
                                   deps=dep)
                        tb = 0 if wname == "l1_W1" else 4
                        a1 = p.op("scalar", lambda e: e.activation(fa[:], wch[:], AF.Abs),
                                  deps=[lw, tsm])
                        a2 = p.op("vector", lambda e, tb=tb: e.tensor_scalar(
                            ft[:], fa[:], small[:, tb:tb + 1], None, ALU.is_le), deps=[a1])
                        a3 = p.op("scalar", lambda e: e.activation(ft[:], ft[:], AF.Identity,
                                                                   scale=0.8, bias=1.0), deps=[a2])
                        a4 = p.op("vector", lambda e, tb=tb: e.tensor_scalar(
                            fa[:], fa[:], small[:, tb + 1:tb + 2], None, ALU.is_ge), deps=[a3])
                        a5 = p.op("vector", lambda e: e.scalar_tensor_tensor(
                            ft[:], fa[:], -0.8, ft[:], ALU.mult, ALU.add), deps=[a4])
                        a6 = p.op("vector", lambda e, wdst=wdst, ib=ib, jc=jc, blkw=blkw: e.tensor_mul(
                            wdst[:, ib * blkw + jc * 512: ib * blkw + jc * 512 + 512],
                            wch[:], ft[:]), deps=[a5])
                        wprev = a6
                        wl.append(a6)
            bl1 = p.op("vector", lambda e: e.tensor_copy(b1cl[:, 0:32], small[:, 8:40]), deps=[tsm])
            bl2 = p.op("vector", lambda e: e.tensor_copy(b2cl[:], small[:, 40:48]), deps=[tsm])
            wl += [bl1, bl2]
            bload = bl2
        xt_tok = [(None, 0)] * 2
        u_tok = [(None, 0)] * 3; q1_tok = [(None, 0)] * 2
        q3_tok = [(None, 0)] * 2; sgm_tok = [(None, 0)] * 2
        gb_tok = (None, 0)
        pF_tok = [(None, 0)] * 2
        pY_tok = [(None, 0)] * 2
        pP_tok = (None, 0)
        ev_tok = [(None, 0)] * 2
        y0_tok = [(None, 0)] * 2
        p0_tok = (None, 0); cp_tok = (None, 0)
        xslot = 0; eslot = 0
        for b in range(BC):
            for tt in range(NT):
                x = xt[xslot]
                xls = []
                for cc in range(CC):
                    dep = [xt_tok[xslot]] if xt_tok[xslot][0] else []
                    xls.append(p.dma("gpsimd", x[:, ts(cc, 512)],
                                     g["bufX6"].ap()[b, cc, :, tt * 512:(tt + 1) * 512], deps=dep))
                if path == 1:
                    lp = p.dma("sync", p0r[:], g["bufP0"].ap()[b, :, tt * 512:(tt + 1) * 512],
                               deps=[p0_tok] if p0_tok[0] else [])
                    bcp = p.op("tensor", lambda e: e.matmul(pP[:], one1[:], p0r[:],
                                                            start=True, stop=True),
                               deps=[lp, t_on1] + ([pP_tok] if pP_tok[0] else []))
                    p0_tok = bcp
                    cpt = p.op("scalar", lambda e: e.copy(combP[:], pP[:]),
                               deps=[bcp] + ([cp_tok] if cp_tok[0] else []))
                    pP_tok = cpt
                glast = None
                for fc in range(32):
                    ci = fc % 2
                    pf = pF[ci]
                    mmF = None
                    for icc in range(CC):
                        dep = xls + wl + ([pF_tok[ci]] if pF_tok[ci][0] and icc == 0 else [])
                        mmF = p.op("tensor", lambda e, pf=pf, icc=icc, fc=fc, x=x: e.matmul(
                            pf[:], W1b[:, icc * 4096 + fc * 128: icc * 4096 + fc * 128 + 128],
                            x[:, ts(icc, 512)], start=(icc == 0), stop=(icc == CC - 1)), deps=dep, inc=(icc == CC - 1))
                    gv = p.op("scalar", lambda e, pf=pf, fc=fc: e.activation(
                        gbufL[:, fc * 512:(fc + 1) * 512], pf[:], AF.Gelu_apprx_tanh,
                        bias=b1cl[:, fc:fc + 1]),
                        deps=[mmF, bload] + ([gb_tok] if gb_tok[0] and fc < 2 else []))
                    pF_tok[ci] = gv
                    glast = gv
                ylast = None
                for occ in range(CC):
                    ci = occ % 2
                    py = pY[ci]
                    mmY = None
                    for fc in range(32):
                        dep = [glast] + wl + ([pY_tok[ci]] if pY_tok[ci][0] and fc == 0 else [])
                        mmY = p.op("tensor", lambda e, py=py, fc=fc, occ=occ: e.matmul(
                            py[:], W2b[:, fc * D + occ * 128: fc * D + occ * 128 + 128],
                            gbufL[:, fc * 512:(fc + 1) * 512],
                            start=(fc == 0), stop=(fc == 31)), deps=dep, inc=(fc == 31))
                    eb = ev[eslot]
                    yv = p.op("scalar", lambda e, py=py, eb=eb, occ=occ: e.activation(
                        eb[:], py[:], AF.Identity, bias=b2cl[:, occ:occ + 1]),
                        deps=[mmY, bload] + ([ev_tok[eslot]] if ev_tok[eslot][0] else []))
                    pY_tok[ci] = yv
                    if path == 0:
                        st = p.dma("sync", g["bufY0"].ap()[b, occ, :, tt * 512:(tt + 1) * 512],
                                   eb[:], deps=[yv])
                        ev_tok[eslot] = st
                        ylast = st
                    else:
                        yb = y0t[eslot]
                        ly = p.dma("gpsimd", yb[:], g["bufY0"].ap()[b, occ, :, tt * 512:(tt + 1) * 512],
                                   deps=[y0_tok[eslot]] if y0_tok[eslot][0] else [])
                        d1 = p.op("vector", lambda e, yb=yb, eb=eb: e.tensor_sub(yb[:], yb[:], eb[:]),
                                  deps=[ly, yv])
                        d2 = p.op("vector", lambda e, yb=yb: e.tensor_mul(yb[:], yb[:], combP[:]),
                                  deps=[d1, cpt])
                        cp_tok = d2
                        d3 = p.op("gpsimd", lambda e, yb=yb, eb=eb: e.tensor_add(eb[:], eb[:], yb[:]),
                                  deps=[d2])
                        st = p.dma("sync", g["bufA"].ap()[b, occ, :, tt * 512:(tt + 1) * 512],
                                   eb[:], deps=[d3])
                        ev_tok[eslot] = st
                        y0_tok[eslot] = d3
                        ylast = st
                    eslot = 1 - eslot
                gb_tok = ylast
                xt_tok[xslot] = ylast
                xslot = 1 - xslot
